# version 11
# baseline (speedup 1.0000x reference)
import math
from contextlib import ExitStack

import numpy as np
import ml_dtypes

import concourse.bass as bass
import concourse.mybir as mybir
from concourse.bass_utils import run_bass_kernel_spmd
from concourse.aot_env import allow_ici_all_to_all

F32 = mybir.dt.float32
BF16 = mybir.dt.bfloat16
AF = mybir.ActivationFunctionType
ALU = mybir.AluOpType
AXX = mybir.AxisListType.X

NCORES = 8
C_EXP = 8.0
LN_EPS = 1e-5
Q_EPS = 1e-8


def full_cfg():
    return dict(S=8192, D=2048, NH=32, FH=5632)


def derive(cfg):
    c = dict(cfg)
    S, D, NH, FH = c["S"], c["D"], c["NH"], c["FH"]
    assert D == NH * 64
    c["HPC"] = NH // NCORES
    assert c["HPC"] % 2 == 0 or c["HPC"] == 1
    c["PPC"] = max(1, c["HPC"] // 2)
    c["TPC"] = S // NCORES
    c["KC"] = D // 128
    c["NKT"] = S // 128
    c["QG"] = min(512, S)
    c["NQG"] = S // c["QG"]
    c["NDIAG"] = c["QG"] // 128
    c["MB"] = FH // 128
    c["DC"] = D // 128
    c["TH"] = min(512, c["TPC"])
    c["NTH"] = c["TPC"] // c["TH"]
    c["QKVW"] = 3 * c["HPC"] * 64
    return c



class Res:
    __slots__ = ("w", "r", "name")

    def __init__(self, name=""):
        self.w = {}
        self.r = {}
        self.name = name


class Prog:
    ENGS = ("pe", "act", "dve", "pool", "sp")
    DMA_RING = 6

    def __init__(self):
        self.ops = {e: [] for e in self.ENGS}
        self.cnt = {}
        self.guar = {e: {} for e in self.ENGS}
        self.pending = {e: None for e in self.ENGS}
        self.dma_i = {"sp": 0, "pool": 0}
        for e in self.ENGS:
            self.cnt[e] = 0
        for q in ("sp", "pool"):
            for i in range(self.DMA_RING):
                self.cnt[f"{q}d{i}"] = 0
        self.cnt["cc"] = 0

    def barrier(self):
        snap = dict(self.cnt)
        for e in self.ENGS:
            cur = self.pending[e]
            if cur is None:
                self.pending[e] = dict(snap)
            else:
                for s, v in snap.items():
                    cur[s] = max(cur.get(s, 0), v)

    def op(self, eng, emit, reads=(), writes=(), inc=1, dma=False, sem=None,
           lag=False):
        if dma:
            idx = self.dma_i[eng]
            self.dma_i[eng] += 1
            sem = f"{eng}d{idx % self.DMA_RING}"
            inc = 16
        elif sem is None:
            sem = eng
        waits = {}
        if dma:
            prev = 16 * (idx // self.DMA_RING)
            if prev > 0:
                waits[sem] = prev
        if self.pending[eng] is not None:
            for s, v in self.pending[eng].items():
                if v > 0:
                    waits[s] = max(waits.get(s, 0), v)
            self.pending[eng] = None
        for r in reads:
            for s, v in r.w.items():
                waits[s] = max(waits.get(s, 0), v)
        for wr in writes:
            for s, v in wr.w.items():
                waits[s] = max(waits.get(s, 0), v)
            for s, v in wr.r.items():
                waits[s] = max(waits.get(s, 0), v)
        g = self.guar[eng]

        def fwd_own(sname, v):
            return (sname == eng or sname.startswith(eng + "d")) and v > self.cnt[sname]

        ww = {s: v for s, v in waits.items()
              if v > g.get(s, 0) and not fwd_own(s, v)}
        for s, v in ww.items():
            g[s] = v
        self.ops[eng].append((ww, emit, inc, sem))
        self.cnt[sem] += inc
        me = (sem, self.cnt[sem] + (1 if lag else 0))
        for wr in writes:
            wr.w = {me[0]: me[1]}
            wr.r = {}
        for r in reads:
            r.r[me[0]] = me[1]

    def final_wait(self, eng):
        snap = dict(self.cnt)

        def emit(e, _snap=snap):
            for s, v in _snap.items():
                if v > 0:
                    e.wait_ge(self._sems[s], v)
        self.ops[eng].append(({}, ("finalwait", emit), 0, eng))

    def emit_all(self, nc, block):
        sems = {}
        ctx = self._ctx
        for s in self.cnt:
            sems[s] = ctx.enter_context(nc.semaphore(f"sem_{s}"))
        self._sems = sems

        def mk(section, eng_name):
            def body(engobj):
                for waits, emit, inc, sem in self.ops[eng_name]:
                    for s, v in sorted(waits.items()):
                        engobj.wait_ge(sems[s], v)
                    if isinstance(emit, tuple) and emit[0] == "finalwait":
                        emit[1](engobj)
                        continue
                    inst = emit(engobj)
                    if inc:
                        inst.then_inc(sems[sem], inc)
            section(body)

        mk(block.tensor, "pe")
        mk(block.scalar, "act")
        mk(block.vector, "dve")
        mk(block.gpsimd, "pool")
        mk(block.sync, "sp")



def build_nc(c, debug=()):
    S, D = c["S"], c["D"]
    HPC, PPC, TPC = c["HPC"], c["PPC"], c["TPC"]
    KC, NKT, QG, NQG = c["KC"], c["NKT"], c["QG"], c["NQG"]
    MB, DC, TH, NTH = c["MB"], c["DC"], c["TH"], c["NTH"]
    QKVW = c["QKVW"]
    HW = HPC * 64

    nc = bass.Bass("TRN2", target_bir_lowering=False)
    p = Prog()

    ap_in = {}
    for name, shape, dt in [
        ("x_fm", [D, S], F32),
        ("x_own", [D, TPC], F32),
        ("wqkv", [D, QKVW], F32),
        ("bqkv_b", [128, QKVW], F32),
        ("cossin", [S, 2 * HPC * 16], F32),
        ("masks", [c["NDIAG"], 128, QG], F32),
        ("ident", [128, 128], BF16),
        ("ones_row_bf", [1, 128], BF16),
        ("ones_col_bf", [128, 1], BF16),
        ("ones_row_f32", [1, 128], F32),
        ("consts", [128, 8], F32),
        ("wout", [D, D], F32),
        ("bout_g", [128, DC], F32),
        ("w1t", [MB, 128, KC, 128], F32),
        ("w2t", [MB, 128, KC, 128], F32),
        ("w3t", [DC, 128, MB, 128], F32),
    ]:
        ap_in[name] = nc.dram_tensor(name, shape, dt, kind="ExternalInput").ap()

    out_t = nc.dram_tensor("out_c", [D, TPC], F32, kind="ExternalOutput").ap()

    a2a_in = nc.dram_tensor("a2a_in", [D, TPC], BF16)
    a2a_out = nc.dram_tensor("a2a_out", [D, TPC], BF16)
    y_dram = nc.dram_tensor("y_dram", [D, TPC], F32)
    r2_dram = nc.dram_tensor("r2_dram", [D, TPC], F32)

    ctx = ExitStack()
    p._ctx = ctx
    e = ctx.enter_context

    def sb(name, shape, dt):
        return e(nc.sbuf_tensor(name, shape, dt))

    dbg = set(debug)

    def dump(name, src_ap, reads):
        if name not in dbg:
            return
        t = nc.dram_tensor("dbg_" + name, list(src_ap.shape), src_ap.dtype,
                           kind="ExternalOutput").ap()
        p.op("sp", lambda E, t=t, src_ap=src_ap: E.dma_start(out=t, in_=src_ap),
             reads=list(reads), writes=[], dma=True)

    consts_sb = sb("consts_sb", [128, 8], F32)
    ident_sb = sb("ident_sb", [128, 128], BF16)
    onesr_bf = sb("onesr_bf", [1, 128], BF16)
    onesc_bf = sb("onesc_bf", [128, 1], BF16)
    onesr_f = sb("onesr_f", [1, 128], F32)
    bout_sb = sb("bout_sb", [128, DC], F32)
    r_small = Res("small")
    for t, src in [(consts_sb, "consts"), (ident_sb, "ident"),
                   (onesr_bf, "ones_row_bf"), (onesc_bf, "ones_col_bf"),
                   (onesr_f, "ones_row_f32"), (bout_sb, "bout_g")]:
        p.op("sp", lambda E, t=t, src=src: E.dma_start(out=t[:, :], in_=ap_in[src]),
             writes=[r_small], dma=True)

    NG1 = (KC + 7) // 8 * MB
    NG3 = (MB + 7) // 8 * DC
    gacc1 = sb("gacc1", [128, NG1], F32)
    gacc2 = sb("gacc2", [128, NG1], F32)
    gacc3 = sb("gacc3", [128, NG3], F32)
    cvec = sb("cvec", [128, 4], F32)
    ncvec = sb("ncvec", [128, 4], F32)
    r_gacc = [Res("gacc1"), Res("gacc2"), Res("gacc3")]
    r_cvec = Res("cvec")

    r_A = Res("A")
    r_a2ain = Res("a2ain")

    GW = 1024
    GCH = GW // 128
    gamma_jobs = []
    gcols = [0, 0, 0]

    def add_gjobs(src3, acc):
        nt = src3.shape[1]
        for a in range(0, nt, GCH):
            b = min(a + GCH, nt)
            gamma_jobs.append((src3[:, a:b, :].rearrange("p m j -> p (m j)"),
                               (b - a) * 128, acc, gcols[acc]))
            gcols[acc] += 1
    for m in range(MB):
        add_gjobs(ap_in["w1t"][m], 0)
        add_gjobs(ap_in["w2t"][m], 1)
    for dc in range(DC):
        add_gjobs(ap_in["w3t"][dc], 2)

    def make_gamma_ops(gbuf, r_gbuf):
        state = {"i": 0}

        def step():
            i = state["i"]
            if i >= len(gamma_jobs):
                return
            state["i"] += 1
            src, width, acc, col = gamma_jobs[i]
            slot = i % 2
            gt = gbuf[slot]
            rs = r_gbuf[slot]
            accs = [gacc1, gacc2, gacc3][acc]
            p.op("sp", lambda E, gt=gt, src=src, width=width:
                 E.dma_start(out=gt[:, 0:width], in_=src),
                 reads=[], writes=[rs], dma=True)
            p.op("dve", lambda E, gt=gt, accs=accs, col=col, width=width:
                 E.tensor_reduce(accs[:, col:col + 1], gt[:, 0:width], AXX, ALU.add, True),
                 reads=[rs], writes=[r_gacc[acc]])
        return step

    with ExitStack() as s01:
        e1 = s01.enter_context
        A_st = [e1(nc.sbuf_tensor(f"A_st{i}", [64, QG], BF16)) for i in range(2)]
        q_sb = e1(nc.sbuf_tensor("q_sb", [128, HPC, S], BF16))
        k_sb = e1(nc.sbuf_tensor("k_sb", [128, PPC, S], BF16))
        v_sb = e1(nc.sbuf_tensor("v_sb", [128, NKT * HPC * 65 + 63], BF16))
        vv = v_sb[:, 0:NKT * HPC * 65].rearrange("p (t h c) -> p t h c", h=HPC, c=65)
        gbuf = [e1(nc.sbuf_tensor(f"gbuf{i}", [128, GW], F32)) for i in range(2)]
        r_gbuf = [Res("gbuf0"), Res("gbuf1")]
        gamma_step = make_gamma_ops(gbuf, r_gbuf)

        def drain_gamma():
            while True:
                before = len(p.ops["sp"]) + len(p.ops["pool"])
                gamma_step()
                if len(p.ops["sp"]) + len(p.ops["pool"]) == before:
                    break

        _gf_done = [False]
        gtot = e1(nc.sbuf_tensor("gtot", [128, 4], F32))
        grow = e1(nc.sbuf_tensor("grow", [1, 4], F32))
        onescf = e1(nc.sbuf_tensor("onescf", [128, 1], F32))

        def gamma_finalize():
            if _gf_done[0]:
                return
            _gf_done[0] = True
            ps_g = ps_s[0][0:1, 0:4]
            ps_gb = ps_s[0][0:128, 8:12]
            r_gtot, r_grow = Res("gtot"), Res("grow")
            nels = {0: c["FH"] * D, 1: c["FH"] * D, 2: D * c["FH"]}
            for i, (acc, w) in enumerate([(gacc1, gcols[0]), (gacc2, gcols[1]), (gacc3, gcols[2])]):
                p.op("dve", lambda E, i=i, acc=acc, w=w: E.tensor_reduce(
                    gtot[:, i:i + 1], acc[:, 0:w], AXX, ALU.add, False),
                    reads=[r_gacc[i]], writes=[r_gtot])
            p.op("dve", lambda E: E.memset(gtot[:, 3:4], 0.0), writes=[r_gtot])
            r_ocf = Res("ocf")
            p.op("dve", lambda E: E.memset(onescf[:, :], 1.0), writes=[r_ocf])
            p.op("pe", lambda E, ps_g=ps_g: E.matmul(ps_g, onescf[:, :], gtot[:, :],
                                                     start=True, stop=True),
                 reads=[r_gtot, r_ocf], writes=[r_ss[0]])
            p.op("dve", lambda E, ps_g=ps_g: E.tensor_copy(grow[:, :], ps_g),
                 reads=[r_ss[0]], writes=[r_grow])
            for i in range(3):
                scale = 0.5 / nels[i]
                p.op("dve", lambda E, i=i, scale=scale: E.tensor_scalar(
                    grow[:, i:i + 1], grow[:, i:i + 1], scale, 0.5 * Q_EPS,
                    ALU.mult, ALU.add),
                    reads=[r_grow], writes=[r_grow])
            p.op("pe", lambda E, ps_gb=ps_gb: E.matmul(ps_gb, onesr_f[:, :], grow[:, :],
                                                       start=True, stop=True),
                 reads=[r_grow, r_small], writes=[r_ss[0]])
            p.op("dve", lambda E, ps_gb=ps_gb: E.tensor_copy(cvec[:, :], ps_gb),
                 reads=[r_ss[0]], writes=[r_cvec])
            p.op("dve", lambda E: E.tensor_scalar(ncvec[:, :], cvec[:, :], -1.0, None, ALU.mult),
                 reads=[r_cvec], writes=[r_cvec])

        r_q, r_k, r_v = Res("q"), Res("k"), Res("v")
        r_Ast = [Res("Ast0"), Res("Ast1")]

        p.op("dve", lambda E: E.memset(v_sb[:, NKT * HPC * 65:], 0.0), writes=[r_v])
        p.op("dve", lambda E: E.memset(vv[:, :, :, 64:65], 1.0), writes=[r_v])
        for h in range(HPC):
            zlo, zhi = ((64, 128) if h % 2 == 0 else (0, 64)) if HPC > 1 else (64, 128)
            p.op("dve", lambda E, h=h, zlo=zlo, zhi=zhi:
                 E.memset(q_sb[zlo:zhi, h, :], 0.0), writes=[r_q])

        with ExitStack() as s0:
            e0 = s0.enter_context
            wqkv_sb = e0(nc.sbuf_tensor("wqkv_sb", [128, KC, QKVW], BF16))
            bqkv_sb = e0(nc.sbuf_tensor("bqkv_sb", [128, QKVW], F32))
            bksc_sb = e0(nc.sbuf_tensor("bksc_sb", [128, HW], F32))
            xg = [e0(nc.sbuf_tensor(f"xg{i}", [128, KC, 128], BF16)) for i in range(2)]
            cs = [e0(nc.sbuf_tensor(f"cs{i}", [128, 2 * HPC * 16], F32)) for i in range(2)]
            q_scr = [e0(nc.sbuf_tensor(f"q_scr{i}", [128, HPC, 64], F32)) for i in range(2)]
            k_scr = [e0(nc.sbuf_tensor(f"k_scr{i}", [128, HPC, 64], F32)) for i in range(2)]
            rt = [e0(nc.sbuf_tensor(f"rt{i}", [128, HPC, 16], F32)) for i in range(4)]
            qbf = [e0(nc.sbuf_tensor(f"qbf{i}", [128, HPC * 64], BF16)) for i in range(2)]
            kbf = [e0(nc.sbuf_tensor(f"kbf{i}", [128, HPC * 64], BF16)) for i in range(2)]

            ps_qk = [e0(nc.psum_tensor(f"ps_qk{i}", [128, 2 * HW], F32)) for i in range(2)]
            ps_v = [e0(nc.psum_tensor(f"ps_v{i}", [128, HW], F32)) for i in range(2)]
            ps_t = [e0(nc.psum_tensor(f"ps_t{i}", [128, 128], BF16)) for i in range(2)]

            r_wqkv, r_bq, r_bk = Res("wqkv"), Res("bqkv"), Res("bksc")
            r_xg = [Res("xg0"), Res("xg1")]
            r_cs = [Res("cs0"), Res("cs1")]
            r_psqk = [Res("psqk0"), Res("psqk1")]
            r_psv = [Res("psv0"), Res("psv1")]
            r_qscr = [Res("qscr0"), Res("qscr1")]
            r_kscr = [Res("kscr0"), Res("kscr1")]
            r_rt = [Res(f"rt{i}") for i in range(4)]
            r_qbf = [Res("qbf0"), Res("qbf1")]
            r_kbf = [Res("kbf0"), Res("kbf1")]
            r_pst = [Res("pst0"), Res("pst1")]

            p.op("pool", lambda E: E.dma_start(
                out=wqkv_sb[:, :, :],
                in_=ap_in["wqkv"].rearrange("(kc p) n -> p kc n", p=128)),
                writes=[r_wqkv], dma=True)
            p.op("sp", lambda E: E.dma_start(out=bqkv_sb[:, :], in_=ap_in["bqkv_b"]),
                 writes=[r_bq], dma=True)
            p.op("dve", lambda E: E.tensor_scalar(
                bksc_sb[:, :], bqkv_sb[:, HW:2 * HW], 0.125, None, ALU.mult),
                reads=[r_bq], writes=[r_bk])

            def emit_transposes(tt):
                pq = tt % 2
                for pr in range(PPC):
                    for pp, (bf, rbf) in enumerate(((qbf[pq], r_qbf[pq]), (kbf[pq], r_kbf[pq]))):
                        src_ap = bf[:, pr * 128:(pr + 1) * 128]
                        p.op("pe", lambda E, src_ap=src_ap, pp=pp:
                             E.transpose(ps_t[pp][:, :], src_ap, ident_sb[:, :]),
                             reads=[rbf, r_small], writes=[r_pst[pp]])
                        if pp == 1:
                            p.op("dve", lambda E, pp=pp, pr=pr, tt=tt:
                                 E.tensor_copy(k_sb[:, pr, tt * 128:(tt + 1) * 128],
                                               ps_t[pp][:, :]),
                                 reads=[r_pst[pp]], writes=[r_k])
                        else:
                            p.op("dve", lambda E, pp=pp, pr=pr, tt=tt:
                                 E.tensor_copy(q_sb[0:64, 2 * pr, tt * 128:(tt + 1) * 128],
                                               ps_t[pp][0:64, :]),
                                 reads=[r_pst[pp]], writes=[r_q])
                            if HPC > 1:
                                p.op("dve", lambda E, pp=pp, pr=pr, tt=tt:
                                     E.tensor_copy(q_sb[64:128, 2 * pr + 1, tt * 128:(tt + 1) * 128],
                                                   ps_t[pp][64:128, :]),
                                     reads=[r_pst[pp]], writes=[r_q])

            for tt in range(NKT):
                slot = tt % 2
                p.op("pool", lambda E, tt=tt, slot=slot: E.dma_start(
                    out=xg[slot][:, :, :],
                    in_=ap_in["x_fm"][:, tt * 128:(tt + 1) * 128]
                        .rearrange("(kc p) t -> p kc t", p=128)),
                    writes=[r_xg[slot]], dma=True)
                p.op("sp", lambda E, tt=tt, slot=slot: E.dma_start(
                    out=cs[slot][:, :],
                    in_=ap_in["cossin"][tt * 128:(tt + 1) * 128, :]),
                    writes=[r_cs[slot]], dma=True)
                pp0 = tt % 2
                for kc in range(KC):
                    st = kc == 0
                    sp_ = kc == KC - 1
                    for ps, rps, off, w_ in ((ps_qk[pp0], r_psqk[pp0], 0, 2 * HW),
                                             (ps_v[pp0], r_psv[pp0], 2 * HW, HW)):
                        p.op("pe", lambda E, slot=slot, kc=kc, ps=ps, off=off, w_=w_, st=st, sp_=sp_:
                             E.matmul(ps[:, :], xg[slot][:, kc, :],
                                      wqkv_sb[:, kc, off:off + w_], start=st, stop=sp_),
                             reads=[r_xg[slot], r_wqkv],
                             writes=[rps] if (st or sp_) else [],
                             inc=(1 if sp_ else 0), lag=not sp_)
                p.op("dve", lambda E, pp0=pp0: E.tensor_tensor(
                    q_scr[pp0][:, :, :], ps_qk[pp0][:, 0:HW].rearrange("p (h d) -> p h d", d=64),
                    bqkv_sb[:, 0:HW].rearrange("p (h d) -> p h d", d=64), ALU.add),
                    reads=[r_psqk[pp0], r_bq], writes=[r_qscr[pp0]])
                p.op("dve", lambda E, pp0=pp0: E.scalar_tensor_tensor(
                    k_scr[pp0][:, :, :], ps_qk[pp0][:, HW:2 * HW].rearrange("p (h d) -> p h d", d=64),
                    0.125, bksc_sb[:, :].rearrange("p (h d) -> p h d", d=64),
                    ALU.mult, ALU.add),
                    reads=[r_psqk[pp0], r_bk], writes=[r_kscr[pp0]])
                p.op("dve", lambda E, tt=tt, pp0=pp0: E.tensor_tensor(
                    vv[:, tt, :, 0:64], ps_v[pp0][:, :].rearrange("p (h d) -> p h d", d=64),
                    bqkv_sb[:, 2 * HW:].rearrange("p (h d) -> p h d", d=64), ALU.add),
                    reads=[r_psv[pp0], r_bq], writes=[r_v])
                for scr, rscr in ((q_scr[pp0], r_qscr[pp0]), (k_scr[pp0], r_kscr[pp0])):
                    u1 = scr[:, :, 0:16]
                    u2 = scr[:, :, 16:32]
                    co = cs[slot][:, 0:HPC * 16].rearrange("p (h d) -> p h d", d=16)
                    si = cs[slot][:, HPC * 16:].rearrange("p (h d) -> p h d", d=16)
                    p.op("dve", lambda E, u1=u1, co=co: E.tensor_tensor(rt[0][:, :, :], u1, co, ALU.mult),
                         reads=[rscr, r_cs[slot]], writes=[r_rt[0]])
                    p.op("dve", lambda E, u2=u2, si=si: E.tensor_tensor(rt[1][:, :, :], u2, si, ALU.mult),
                         reads=[rscr, r_cs[slot]], writes=[r_rt[1]])
                    p.op("dve", lambda E, u1=u1, si=si: E.tensor_tensor(rt[2][:, :, :], u1, si, ALU.mult),
                         reads=[rscr, r_cs[slot]], writes=[r_rt[2]])
                    p.op("dve", lambda E, u2=u2, co=co: E.tensor_tensor(rt[3][:, :, :], u2, co, ALU.mult),
                         reads=[rscr, r_cs[slot]], writes=[r_rt[3]])
                    p.op("dve", lambda E, u1=u1: E.tensor_tensor(u1, rt[0][:, :, :], rt[1][:, :, :], ALU.subtract),
                         reads=[r_rt[0], r_rt[1]], writes=[rscr])
                    p.op("dve", lambda E, u2=u2: E.tensor_tensor(u2, rt[2][:, :, :], rt[3][:, :, :], ALU.add),
                         reads=[r_rt[2], r_rt[3]], writes=[rscr])
                p.op("dve", lambda E, pp0=pp0: E.tensor_copy(
                    qbf[pp0][:, :], q_scr[pp0][:, :, :].rearrange("p h d -> p (h d)")),
                    reads=[r_qscr[pp0]], writes=[r_qbf[pp0]])
                p.op("dve", lambda E, pp0=pp0: E.tensor_copy(
                    kbf[pp0][:, :], k_scr[pp0][:, :, :].rearrange("p h d -> p (h d)")),
                    reads=[r_kscr[pp0]], writes=[r_kbf[pp0]])
                if tt >= 1:
                    emit_transposes(tt - 1)
                if tt == NKT - 1:
                    emit_transposes(tt)
                gamma_step()
                gamma_step()
            dump("q_sb", q_sb[:, :, :], [r_q])
            dump("k_sb", k_sb[:, :, :], [r_k])
            dump("v_sb", v_sb[:, :], [r_v])
            p.barrier()

        NS = 3
        with ExitStack() as s1:
            e2 = s1.enter_context
            masks_sb = e2(nc.sbuf_tensor("masks_sb", [128, c["NDIAG"], QG], F32))
            NPS = 6
            p_sb = [e2(nc.sbuf_tensor(f"p_sb{i}", [128, 2 * QG], BF16)) for i in range(NPS)]
            rcp_f = e2(nc.sbuf_tensor("rcp_f", [1, QG], F32))
            rcp_bf = e2(nc.sbuf_tensor("rcp_bf", [1, QG], BF16))
            rb_sb = e2(nc.sbuf_tensor("rb_sb", [64, QG], F32))
            ps_s = [e2(nc.psum_tensor(f"ps_s{i}", [128, 2 * QG], F32)) for i in range(NS)]
            ps_o = [e2(nc.psum_tensor(f"ps_o{i}", [128, QG], F32)) for i in range(2)]

            r_ps = [Res(f"p{i}") for i in range(NPS)]
            r_ss = [Res(f"ss{i}") for i in range(NS)]
            r_so = [Res("so0"), Res("so1")]
            r_rcp, r_rb = Res("rcp"), Res("rb")
            r_masks = Res("masks")
            p.op("sp", lambda E: E.dma_start(
                out=masks_sb[:, :, :],
                in_=ap_in["masks"].rearrange("n p q -> p n q")),
                writes=[r_masks], dma=True)

            a2a_v = a2a_in.ap().rearrange("(d hl hd) t -> hl hd d t", hl=HPC, hd=64)

            LAG = 3
            unit = 0
            for h in range(HPC):
                pr, hp = h // 2, h % 2
                for qg in range(NQG):
                    op_ = unit % 2
                    nkc = (qg + 1) * (QG // 128)
                    np_ = nkc // 2

                    def emit_spair(j, h=h, pr=pr, qg=qg):
                        sl = j % NS
                        for half in range(2):
                            kc = 2 * j + half
                            p.op("pe", lambda E, sl=sl, half=half, kc=kc, h=h, pr=pr, qg=qg:
                                 E.matmul(ps_s[sl][:, half * QG:(half + 1) * QG],
                                          k_sb[:, pr, kc * 128:(kc + 1) * 128],
                                          q_sb[:, h, qg * QG:(qg + 1) * QG],
                                          start=True, stop=True),
                                 reads=[r_k, r_q],
                                 writes=[r_ss[sl]],
                                 inc=(1 if half == 1 else 0), lag=(half == 0))
                        for half in range(2):
                            kc = 2 * j + half
                            diag = kc - qg * (QG // 128)
                            if diag >= 0:
                                p.op("dve", lambda E, sl=sl, half=half, diag=diag:
                                     E.tensor_tensor(
                                         ps_s[sl][:, half * QG:(half + 1) * QG],
                                         ps_s[sl][:, half * QG:(half + 1) * QG],
                                         masks_sb[:, diag, :], ALU.add),
                                     reads=[r_masks, r_ss[sl]], writes=[r_ss[sl]])
                        pl = j % NPS
                        p.op("act", lambda E, sl=sl, pl=pl: E.activation(
                            p_sb[pl][:, :], ps_s[sl][:, :], AF.Exp,
                            bias=consts_sb[:, 0:1], scale=1.0),
                            reads=[r_ss[sl], r_small], writes=[r_ps[pl]])

                    def emit_avpair(j, h=h, op_=op_, np_=np_):
                        pl = j % NPS
                        for half in range(2):
                            kc = 2 * j + half
                            st = kc == 0
                            sp_ = kc == 2 * np_ - 1
                            vb = (kc * HPC + h) * 65
                            p.op("pe", lambda E, pl=pl, half=half, vb=vb, op_=op_, st=st, sp_=sp_:
                                 E.matmul(ps_o[op_][:, :], v_sb[:, vb:vb + 128],
                                          p_sb[pl][:, half * QG:(half + 1) * QG],
                                          start=st, stop=sp_),
                                 reads=[r_v, r_ps[pl]],
                                 writes=[r_so[op_]] if (st or sp_) else [],
                                 inc=(1 if sp_ else 0), lag=not sp_)

                    for j in range(np_ + LAG):
                        if j < np_:
                            emit_spair(j)
                        if j >= LAG:
                            emit_avpair(j - LAG)

                    bsl = (np_ - 1) % NS
                    ast = unit % 2
                    p.op("dve", lambda E, op_=op_: E.reciprocal(rcp_f[:, :], ps_o[op_][64:65, :]),
                         reads=[r_so[op_]], writes=[r_rcp])
                    p.op("dve", lambda E: E.tensor_copy(rcp_bf[:, :], rcp_f[:, :]),
                         reads=[r_rcp], writes=[r_rcp])
                    p.op("pe", lambda E, bsl=bsl: E.matmul(
                        ps_s[bsl][0:64, 0:QG], onesr_bf[:, 0:64], rcp_bf[:, :],
                        start=True, stop=True),
                        reads=[r_rcp, r_small], writes=[r_ss[bsl]])
                    p.op("dve", lambda E, bsl=bsl: E.tensor_copy(rb_sb[:, :], ps_s[bsl][0:64, 0:QG]),
                         reads=[r_ss[bsl]], writes=[r_rb])
                    p.op("dve", lambda E, op_=op_, ast=ast: E.tensor_tensor(
                        A_st[ast][:, :], ps_o[op_][0:64, :], rb_sb[:, :], ALU.mult),
                        reads=[r_so[op_], r_rb], writes=[r_Ast[ast]])
                    hl = 2 * pr + hp if HPC > 1 else 0
                    tok0 = qg * QG
                    if QG <= TPC:
                        dst = a2a_v[hl][:, tok0 // TPC, tok0 % TPC:tok0 % TPC + QG]
                    else:
                        dst = a2a_v[hl][:, tok0 // TPC:(tok0 + QG) // TPC, :]
                    p.op("sp", lambda E, dst=dst, ast=ast: E.dma_start(
                        out=dst, in_=A_st[ast][:, :]),
                        reads=[r_Ast[ast]], writes=[r_a2ain], dma=True)
                    unit += 1
                    for _ in range(4):
                        gamma_step()
                    if unit == min(48, HPC * NQG - 2):
                        drain_gamma()
                        gamma_finalize()
            drain_gamma()
            gamma_finalize()
            p.barrier()

        dump("cvec", cvec[:, :], [r_cvec])
        dump("gacc1", gacc1[:, :], [r_gacc[0]])
        dump("gacc3", gacc3[:, :], [r_gacc[2]])
        p.barrier()

    r_a2a = Res("a2a")

    def emit_cc(E):
        with allow_ici_all_to_all():
            return E.collective_compute(
                "AllToAll", ALU.bypass, replica_groups=[list(range(NCORES))],
                ins=[a2a_in.ap()], outs=[a2a_out.ap()])
    p.op("pool", emit_cc, reads=[r_a2ain], writes=[r_a2a], inc=1, sem="cc")

    with ExitStack() as s23:
        e23 = s23.enter_context
        T = TPC
        stat = e23(nc.sbuf_tensor("stat", [1, 4 * TPC], F32))
        stat_bf = e23(nc.sbuf_tensor("stat_bf", [1, 2 * TPC], BF16))
        RB = e23(nc.sbuf_tensor("RB", [128, TPC], F32))
        MBc = e23(nc.sbuf_tensor("MBc", [128, TPC], F32))
        ysc = e23(nc.sbuf_tensor("ysc", [128, TPC], F32))
        r_stat, r_statbf = Res("stat"), Res("statbf")
        r_RB, r_MB, r_ysc = Res("RB"), Res("MB"), Res("ysc")

        def ln_finalize(ps1, ps2, rp1, rp2, nfeat, ps_bc, r_psbc):
            for th in range(NTH):
                p.op("dve", lambda E, th=th, ps1=ps1: E.tensor_scalar(
                    stat[:, th * TH:(th + 1) * TH], ps1[th][:, :],
                    1.0 / nfeat, None, ALU.mult),
                    reads=[rp1[th]], writes=[r_stat])
                p.op("dve", lambda E, th=th, ps2=ps2: E.tensor_scalar(
                    stat[:, T + th * TH:T + (th + 1) * TH], ps2[th][:, :],
                    1.0 / nfeat, None, ALU.mult),
                    reads=[rp2[th]], writes=[r_stat])
            p.op("dve", lambda E: E.tensor_tensor(
                stat[:, 3 * T:4 * T], stat[:, 0:T], stat[:, 0:T], ALU.mult),
                reads=[r_stat], writes=[r_stat])
            p.op("dve", lambda E: E.tensor_tensor(
                stat[:, T:2 * T], stat[:, T:2 * T], stat[:, 3 * T:4 * T], ALU.subtract),
                reads=[r_stat], writes=[r_stat])
            p.op("act", lambda E: E.activation(
                stat[:, 2 * T:3 * T], stat[:, T:2 * T], AF.Ln,
                bias=consts_sb[0:1, 1:2], scale=1.0),
                reads=[r_stat, r_small], writes=[r_stat])
            p.op("act", lambda E: E.activation(
                stat[:, 2 * T:3 * T], stat[:, 2 * T:3 * T], AF.Exp,
                bias=0.0, scale=-0.5),
                reads=[r_stat], writes=[r_stat])
            p.op("dve", lambda E: E.tensor_tensor(
                stat[:, 3 * T:4 * T], stat[:, 0:T], stat[:, 2 * T:3 * T], ALU.mult),
                reads=[r_stat], writes=[r_stat])
            p.op("dve", lambda E: E.tensor_copy(stat_bf[:, 0:T], stat[:, 2 * T:3 * T]),
                 reads=[r_stat], writes=[r_statbf])
            p.op("dve", lambda E: E.tensor_copy(stat_bf[:, T:2 * T], stat[:, 3 * T:4 * T]),
                 reads=[r_stat], writes=[r_statbf])
            for dst, rdst, off in ((RB, r_RB, 0), (MBc, r_MB, TPC)):
                for th in range(NTH):
                    p.op("pe", lambda E, off=off, th=th, ps_bc=ps_bc: E.matmul(
                        ps_bc[:, :], onesr_bf[:, :],
                        stat_bf[:, off + th * TH:off + (th + 1) * TH],
                        start=True, stop=True),
                        reads=[r_statbf, r_small], writes=[r_psbc])
                    p.op("dve", lambda E, dst=dst, th=th, ps_bc=ps_bc: E.tensor_copy(
                        dst[:, th * TH:(th + 1) * TH], ps_bc[:, :]),
                        reads=[r_psbc], writes=[rdst])

        sR = ExitStack()
        ybf_sb = sR.enter_context(nc.sbuf_tensor("ybf_sb", [128, KC, TPC], BF16, side="right"))
        r_ybf = Res("ybf")
        sR2 = ExitStack()
        xo_sb = [sR2.enter_context(nc.sbuf_tensor(f"xo{i}", [128, TPC], F32, side="right")) for i in range(2)]
        wo_sb = [sR2.enter_context(nc.sbuf_tensor(f"wo{i}", [128, KC, 128], BF16, side="right")) for i in range(2)]
        r_xo = [Res("xo0"), Res("xo1")]
        r_wo = [Res("wo0"), Res("wo1")]

        def wout_dma(dc):
            slw = dc % 2
            p.op("pool", lambda E, dc=dc, slw=slw: E.dma_start(
                out=wo_sb[slw][:, :, :],
                in_=ap_in["wout"][:, dc * 128:(dc + 1) * 128]
                    .rearrange("(kc p) j -> p kc j", p=128)),
                writes=[r_wo[slw]], dma=True)
            p.op("sp", lambda E, dc=dc, slw=slw: E.dma_start(
                out=xo_sb[slw][:, :], in_=ap_in["x_own"][dc * 128:(dc + 1) * 128, :]),
                writes=[r_xo[slw]], dma=True)

        wout_dma(0)

        with ExitStack() as s2:
            e3 = s2.enter_context
            af_sb = e3(nc.sbuf_tensor("af_sb", [128, KC, TPC], BF16))
            r1_sb = e3(nc.sbuf_tensor("r1_sb", [128, KC, TPC], F32))
            r1b = e3(nc.sbuf_tensor("r1b", [128, TPC], BF16))
            r1sq = e3(nc.sbuf_tensor("r1sq", [128, TPC], BF16))

            ps_w = [e3(nc.psum_tensor(f"ps_w{i}", [128, TH], F32)) for i in range(2)]
            ps_s1 = [e3(nc.psum_tensor(f"ps_s1{i}", [1, TH], F32)) for i in range(NTH)]
            ps_s2 = [e3(nc.psum_tensor(f"ps_s2{i}", [1, TH], F32)) for i in range(NTH)]
            ps_bc = e3(nc.psum_tensor("ps_bc", [128, TH], F32))

            r_af = Res("af")
            r_r1 = Res("r1")
            r_psw = [Res("psw0"), Res("psw1")]
            r_ps1 = [Res(f"ps1{i}") for i in range(NTH)]
            r_ps2 = [Res(f"ps2{i}") for i in range(NTH)]
            r_r1b, r_r1sq = Res("r1b"), Res("r1sq")
            r_psbc = Res("psbc")

            p.op("pool", lambda E: E.dma_start(
                out=af_sb[:, :, :],
                in_=a2a_out.ap().rearrange("(kc p) t -> p kc t", p=128)),
                reads=[r_a2a], writes=[r_af], dma=True)

            for dc in range(DC):
                slw = dc % 2
                if dc + 1 < DC:
                    wout_dma(dc + 1)
                for th in range(NTH):
                    pw = (dc * NTH + th) % 2
                    for kc in range(KC):
                        stw = kc == 0
                        spw = kc == KC - 1
                        p.op("pe", lambda E, slw=slw, kc=kc, th=th, pw=pw,
                             st=stw, sp_=spw:
                             E.matmul(ps_w[pw][:, :], wo_sb[slw][:, kc, :],
                                      af_sb[:, kc, th * TH:(th + 1) * TH], start=st, stop=sp_),
                             reads=[r_wo[slw], r_af],
                             writes=[r_psw[pw]] if (stw or spw) else [],
                             inc=(1 if spw else 0), lag=not spw)
                    p.op("dve", lambda E, dc=dc, th=th, pw=pw, slw=slw:
                         E.scalar_tensor_tensor(
                             r1_sb[:, dc, th * TH:(th + 1) * TH], ps_w[pw][:, :],
                             bout_sb[:, dc:dc + 1], xo_sb[slw][:, th * TH:(th + 1) * TH],
                             ALU.add, ALU.add),
                         reads=[r_psw[pw], r_xo[slw], r_small], writes=[r_r1])
                p.op("dve", lambda E, dc=dc: E.tensor_copy(r1b[:, :], r1_sb[:, dc, :]),
                     reads=[r_r1], writes=[r_r1b])
                p.op("act", lambda E: E.activation(r1sq[:, :], r1b[:, :], AF.Square),
                     reads=[r_r1b], writes=[r_r1sq])
                for th in range(NTH):
                    sp2 = dc == DC - 1
                    st2 = dc == 0
                    p.op("pe", lambda E, th=th, dc=dc, st=st2, sp_=sp2:
                         E.matmul(ps_s1[th][:, :], onesc_bf[:, :],
                                  r1b[:, th * TH:(th + 1) * TH], start=st, stop=sp_),
                         reads=[r_r1b, r_small], writes=[r_ps1[th]])
                    p.op("pe", lambda E, th=th, dc=dc, st=st2, sp_=sp2:
                         E.matmul(ps_s2[th][:, :], onesc_bf[:, :],
                                  r1sq[:, th * TH:(th + 1) * TH], start=st, stop=sp_),
                         reads=[r_r1sq, r_small], writes=[r_ps2[th]])

            ln_finalize(ps_s1, ps_s2, r_ps1, r_ps2, D, ps_bc, r_psbc)

            dump("r1", r1_sb[:, :, :], [r_r1])
            dump("af", af_sb[:, :, :], [r_af])
            for dc in range(DC):
                p.op("dve", lambda E, dc=dc: E.tensor_tensor(
                    ysc[:, :], r1_sb[:, dc, :], RB[:, :], ALU.mult),
                    reads=[r_r1, r_RB], writes=[r_ysc])
                p.op("dve", lambda E, dc=dc: E.tensor_tensor(
                    ysc[:, :], ysc[:, :], MBc[:, :], ALU.subtract),
                    reads=[r_MB, r_ysc], writes=[r_ysc])
                p.op("dve", lambda E, dc=dc: E.tensor_copy(ybf_sb[:, dc, :], ysc[:, :]),
                     reads=[r_ysc], writes=[r_ybf])
                p.op("sp", lambda E, dc=dc: E.dma_start(
                    out=y_dram.ap()[dc * 128:(dc + 1) * 128, :], in_=ysc[:, :]),
                    reads=[r_ysc], writes=[], dma=True)
            dump("ybf", ybf_sb[:, :, :], [r_ybf])
            p.barrier()

        sR2.close()

        with ExitStack() as s3:
            e4 = s3.enter_context
            H_sb = e4(nc.sbuf_tensor("H_sb", [128, MB, TPC], BF16))
            r_H = Res("H")
            with ExitStack() as s3a:
                e5 = s3a.enter_context
                wb = [e5(nc.sbuf_tensor(f"wb{i}", [128, KC * 128], F32)) for i in range(4)]
                qw = [e5(nc.sbuf_tensor(f"qw{i}", [128, KC, 128], BF16)) for i in range(4)]
                sig = e5(nc.sbuf_tensor("sig", [128, TH], BF16))
                ps_x1 = [e5(nc.psum_tensor(f"ps_x1{i}", [128, TH], F32)) for i in range(2)]
                ps_x2 = [e5(nc.psum_tensor(f"ps_x2{i}", [128, TH], F32)) for i in range(2)]
                r_wb = [Res(f"wb{i}") for i in range(4)]
                r_qw = [Res(f"qw{i}") for i in range(4)]
                r_sig = Res("sig")
                r_px1 = [Res("px10"), Res("px11")]
                r_px2 = [Res("px20"), Res("px21")]

                ts2 = e5(nc.sbuf_tensor("ts2", [128, KC * 128], BF16))
                r_ts2 = Res("ts2")

                def quantize(src_sb, r_src, width, dst, r_dst, ci):
                    p.op("act", lambda E: E.activation(
                        dst, src_sb[:, 0:width], AF.Sign,
                        bias=ncvec[:, ci:ci + 1], scale=1.0),
                        reads=[r_src, r_cvec], writes=[r_dst])
                    p.op("act", lambda E: E.activation(
                        ts2[:, 0:width], src_sb[:, 0:width], AF.Sign,
                        bias=cvec[:, ci:ci + 1], scale=1.0),
                        reads=[r_src, r_cvec], writes=[r_ts2])
                    p.op("dve", lambda E: E.tensor_tensor(
                        dst, dst, ts2[:, 0:width], ALU.add),
                        reads=[r_ts2, r_dst], writes=[r_dst])

                def w12_dma(m):
                    slw = m % 2
                    for wi, (wt, q_) in enumerate((("w1t", "sp"), ("w2t", "pool"))):
                        qi = slw * 2 + wi
                        p.op(q_, lambda E, wt=wt, m=m, qi=qi: E.dma_start(
                            out=wb[qi][:, :], in_=ap_in[wt][m].rearrange("p kc j -> p (kc j)")),
                            writes=[r_wb[qi]], dma=True)

                w12_dma(0)
                for m in range(MB):
                    slw = m % 2
                    if m + 1 < MB:
                        w12_dma(m + 1)
                    for wi, ci in ((0, 0), (1, 1)):
                        qi = slw * 2 + wi
                        quantize(wb[qi], r_wb[qi], KC * 128,
                                 qw[qi][:, :, :].rearrange("p kc j -> p (kc j)"), r_qw[qi], ci)
                        psx = (ps_x1, ps_x2)[wi]
                        rpx = (r_px1, r_px2)[wi]
                        for th in range(NTH):
                            px = th % 2
                            for kc in range(KC):
                                stf = kc == 0
                                spf = kc == KC - 1
                                p.op("pe", lambda E, qi=qi, kc=kc, th=th, px=px, psx=psx,
                                     st=stf, sp_=spf:
                                     E.matmul(psx[px][:, :], qw[qi][:, kc, :],
                                              ybf_sb[:, kc, th * TH:(th + 1) * TH],
                                              start=st, stop=sp_),
                                     reads=[r_qw[qi], r_ybf],
                                     writes=[rpx[px]] if (stf or spf) else [],
                                     inc=(1 if spf else 0), lag=not spf)
                    for th in range(NTH):
                        px = th % 2
                        p.op("act", lambda E, px=px: E.activation(
                            sig[:, :], ps_x1[px][:, :], AF.Sigmoid, scale=0.5),
                            reads=[r_px1[px]], writes=[r_sig])
                        p.op("dve", lambda E, m=m, th=th, px=px: E.tensor_tensor(
                            H_sb[:, m, th * TH:(th + 1) * TH], ps_x2[px][:, :],
                            sig[:, :], ALU.mult),
                            reads=[r_px2[px], r_sig], writes=[r_H])
                dump("H", H_sb[:, :, :], [r_H])
                p.barrier()
            sR.close()

            with ExitStack() as s3b:
                e6 = s3b.enter_context
                WH3 = (MB + 1) // 2
                w3b = [e6(nc.sbuf_tensor(f"w3b{i}", [128, WH3 * 128], F32)) for i in range(4)]
                q3 = [e6(nc.sbuf_tensor(f"q3{i}", [128, MB, 128], BF16)) for i in range(2)]
                yst = [e6(nc.sbuf_tensor("yst0", [128, TPC], F32))]
                ts23 = e6(nc.sbuf_tensor("ts23", [128, WH3 * 128], BF16))
                r_ts23 = Res("ts23")
                r2c = [e6(nc.sbuf_tensor(f"r2c{i}", [128, TPC], F32)) for i in range(2)]
                r2b = e6(nc.sbuf_tensor("r2b", [128, TPC], BF16))
                r2sq = r2b
                ps_r = [e6(nc.psum_tensor(f"ps_r{i}", [128, TH], F32)) for i in range(2)]
                ps_t1 = [e6(nc.psum_tensor(f"ps_t1{i}", [1, TH], F32)) for i in range(NTH)]
                ps_t2 = [e6(nc.psum_tensor(f"ps_t2{i}", [1, TH], F32)) for i in range(NTH)]
                r_w3b = [Res(f"w3b{i}") for i in range(4)]
                r_q3 = [Res("q30"), Res("q31")]
                r_yst = [Res("yst0")]
                r_r2c = [Res("r2c0"), Res("r2c1")]
                r_r2b, r_r2sq = Res("r2b"), Res("r2sq")
                r_pr = [Res("pr0"), Res("pr1")]
                r_pt1 = [Res(f"pt1{i}") for i in range(NTH)]
                r_pt2 = [Res(f"pt2{i}") for i in range(NTH)]
                r_r2dram = Res("r2dram")

                mh3 = MB // 2

                def w3_dma(dc):
                    slq = dc % 2
                    for hi, (m0, m1, q_) in enumerate(((0, mh3, "sp"), (mh3, MB, "pool"))):
                        slw = slq * 2 + hi
                        p.op(q_, lambda E, dc=dc, slw=slw, m0=m0, m1=m1: E.dma_start(
                            out=w3b[slw][:, 0:(m1 - m0) * 128],
                            in_=ap_in["w3t"][dc][:, m0:m1, :].rearrange("p m j -> p (m j)")),
                            writes=[r_w3b[slw]], dma=True)

                w3_dma(0)
                for dc in range(DC):
                    slq = dc % 2
                    if dc + 1 < DC:
                        w3_dma(dc + 1)
                    for hi, (m0, m1) in enumerate(((0, mh3), (mh3, MB))):
                        slw = slq * 2 + hi
                        wdt = (m1 - m0) * 128
                        qdst = q3[slq][:, m0:m1, :].rearrange("p m j -> p (m j)")
                        p.op("act", lambda E, slw=slw, wdt=wdt, qdst=qdst: E.activation(
                            qdst, w3b[slw][:, 0:wdt], AF.Sign,
                            bias=ncvec[:, 2:3], scale=1.0),
                            reads=[r_w3b[slw], r_cvec], writes=[r_q3[slq]])
                        p.op("act", lambda E, slw=slw, wdt=wdt: E.activation(
                            ts23[:, 0:wdt], w3b[slw][:, 0:wdt], AF.Sign,
                            bias=cvec[:, 2:3], scale=1.0),
                            reads=[r_w3b[slw], r_cvec], writes=[r_ts23])
                        p.op("dve", lambda E, wdt=wdt, qdst=qdst: E.tensor_tensor(
                            qdst, qdst, ts23[:, 0:wdt], ALU.add),
                            reads=[r_ts23, r_q3[slq]], writes=[r_q3[slq]])
                    p.op("sp", lambda E, dc=dc: E.dma_start(
                        out=yst[0][:, :], in_=y_dram.ap()[dc * 128:(dc + 1) * 128, :]),
                        reads=[], writes=[r_yst[0]], dma=True)
                    for th in range(NTH):
                        pr_ = (dc * NTH + th) % 2
                        for m in range(MB):
                            st3 = m == 0
                            sp3 = m == MB - 1
                            p.op("pe", lambda E, slq=slq, m=m, th=th, pr_=pr_,
                                 st=st3, sp_=sp3:
                                 E.matmul(ps_r[pr_][:, :], q3[slq][:, m, :],
                                          H_sb[:, m, th * TH:(th + 1) * TH],
                                          start=st, stop=sp_),
                                 reads=[r_q3[slq], r_H],
                                 writes=[r_pr[pr_]] if (st3 or sp3) else [],
                                 inc=(1 if sp3 else 0), lag=not sp3)
                        p.op("dve", lambda E, slq=slq, th=th, pr_=pr_: E.scalar_tensor_tensor(
                            r2c[slq][:, th * TH:(th + 1) * TH], ps_r[pr_][:, :],
                            0.25, yst[0][:, th * TH:(th + 1) * TH], ALU.mult, ALU.add),
                            reads=[r_pr[pr_], r_yst[0]], writes=[r_r2c[slq]])
                    p.op("dve", lambda E, slq=slq: E.tensor_copy(r2b[:, :], r2c[slq][:, :]),
                         reads=[r_r2c[slq]], writes=[r_r2b])
                    for th in range(NTH):
                        sp2 = dc == DC - 1
                        st2 = dc == 0
                        p.op("pe", lambda E, th=th, dc=dc, st=st2, sp_=sp2:
                             E.matmul(ps_t1[th][:, :], onesc_bf[:, :],
                                      r2b[:, th * TH:(th + 1) * TH], start=st, stop=sp_),
                             reads=[r_r2b, r_small], writes=[r_pt1[th]])
                    p.op("act", lambda E: E.activation(r2sq[:, :], r2b[:, :], AF.Square),
                         reads=[r_r2b], writes=[r_r2b])
                    for th in range(NTH):
                        sp2 = dc == DC - 1
                        st2 = dc == 0
                        p.op("pe", lambda E, th=th, dc=dc, st=st2, sp_=sp2:
                             E.matmul(ps_t2[th][:, :], onesc_bf[:, :],
                                      r2sq[:, th * TH:(th + 1) * TH], start=st, stop=sp_),
                             reads=[r_r2b, r_small], writes=[r_pt2[th]])
                    p.op("sp", lambda E, dc=dc, slq=slq: E.dma_start(
                        out=r2_dram.ap()[dc * 128:(dc + 1) * 128, :], in_=r2c[slq][:, :]),
                        reads=[r_r2c[slq]], writes=[r_r2dram], dma=True)

                ps_bc2 = e6(nc.psum_tensor("ps_bc2", [128, TH], F32))
                r_psbc2 = Res("psbc2")
                ln_finalize(ps_t1, ps_t2, r_pt1, r_pt2, D, ps_bc2, r_psbc2)

                for dc in range(DC):
                    slq = dc % 2
                    p.op("sp", lambda E, dc=dc, slq=slq: E.dma_start(
                        out=r2c[slq][:, :], in_=r2_dram.ap()[dc * 128:(dc + 1) * 128, :]),
                        reads=[r_r2dram], writes=[r_r2c[slq]], dma=True)
                    p.op("dve", lambda E, slq=slq: E.tensor_tensor(
                        r2c[slq][:, :], r2c[slq][:, :], RB[:, :], ALU.mult),
                        reads=[r_RB, r_r2c[slq]], writes=[r_r2c[slq]])
                    p.op("dve", lambda E, slq=slq: E.tensor_tensor(
                        r2c[slq][:, :], r2c[slq][:, :], MBc[:, :], ALU.subtract),
                        reads=[r_MB, r_r2c[slq]], writes=[r_r2c[slq]])
                    p.op("sp", lambda E, dc=dc, slq=slq: E.dma_start(
                        out=out_t[dc * 128:(dc + 1) * 128, :], in_=r2c[slq][:, :]),
                        reads=[r_r2c[slq]], writes=[], dma=True)

    p.final_wait("sp")

    with nc.Block() as block:
        p.emit_all(nc, block)
    ctx.close()
    n_ops = sum(len(v) for v in p.ops.values())
    return nc, n_ops



def prepare_inputs(c, L0_attn_x, L0_attn_Wqkv, L0_attn_b_qkv, L0_attn_Wout,
                   L0_attn_b_out, L0_ffn_w1, L0_ffn_w2, L0_ffn_w3):
    S, D = c["S"], c["D"]
    HPC, TPC, KC, MB, DC, QG = c["HPC"], c["TPC"], c["KC"], c["MB"], c["DC"], c["QG"]
    f32 = np.float32

    x2d = np.asarray(L0_attn_x, f32).reshape(S, D)
    x_fm = np.ascontiguousarray(x2d.T)
    Wqkv = np.asarray(L0_attn_Wqkv, f32)
    bqkv = np.asarray(L0_attn_b_qkv, f32)
    Wout = np.ascontiguousarray(np.asarray(L0_attn_Wout, f32))
    bout = np.asarray(L0_attn_b_out, f32)
    w1 = np.asarray(L0_ffn_w1, f32)
    w2 = np.asarray(L0_ffn_w2, f32)
    w3 = np.asarray(L0_ffn_w3, f32)

    inv = (1.0 / (10000.0 ** (np.arange(0, 32, 2) / 32.0))).astype(np.float64)
    t = np.arange(S, dtype=np.float64)
    fr = np.outer(t, inv)
    cos = np.cos(fr).astype(f32)
    sin = np.sin(fr).astype(f32)
    cossin = np.ascontiguousarray(
        np.concatenate([np.tile(cos, (1, HPC)), np.tile(sin, (1, HPC))], 1))

    nd = c["NDIAG"]
    masks = np.zeros((nd, 128, QG), f32)
    for i in range(nd):
        s_idx = np.arange(128)[:, None] + i * 128
        j_idx = np.arange(QG)[None, :]
        masks[i] = np.where(s_idx > j_idx, -10000.0, 0.0)

    ident = np.eye(128, dtype=ml_dtypes.bfloat16)
    ones_row_bf = np.ones((1, 128), ml_dtypes.bfloat16)
    ones_col_bf = np.ones((128, 1), ml_dtypes.bfloat16)
    ones_row_f32 = np.ones((1, 128), f32)
    consts = np.zeros((128, 8), f32)
    consts[:, 0] = -C_EXP
    consts[:, 1] = LN_EPS

    w1t = np.ascontiguousarray(w1.T.reshape(KC, 128, MB, 128).transpose(2, 1, 0, 3))
    w2t = np.ascontiguousarray(w2.T.reshape(KC, 128, MB, 128).transpose(2, 1, 0, 3))
    w3t = np.ascontiguousarray(w3.T.reshape(MB, 128, DC, 128).transpose(2, 1, 0, 3))
    bout_g = np.ascontiguousarray(bout.reshape(DC, 128).T)

    shared = dict(x_fm=x_fm, cossin=cossin, masks=masks, ident=ident,
                  ones_row_bf=ones_row_bf, ones_col_bf=ones_col_bf,
                  ones_row_f32=ones_row_f32, consts=consts, wout=Wout,
                  bout_g=bout_g, w1t=w1t, w2t=w2t, w3t=w3t)

    in_maps = []
    for cid in range(NCORES):
        HW = HPC * 64
        cols = slice(cid * HW, (cid + 1) * HW)
        wq = np.ascontiguousarray(np.concatenate(
            [Wqkv[:, cols], Wqkv[:, D:][:, cols], Wqkv[:, 2 * D:][:, cols]], 1))
        bq = np.concatenate([bqkv[cols], bqkv[D:][cols], bqkv[2 * D:][cols]])
        bq_b = np.ascontiguousarray(
            np.broadcast_to(bq[None, :], (128, 3 * HW)).astype(f32))
        x_own = np.ascontiguousarray(x_fm[:, cid * TPC:(cid + 1) * TPC])
        m = dict(shared)
        m.update(wqkv=wq, bqkv_b=bq_b, x_own=x_own)
        in_maps.append(m)
    return in_maps


def kernel(**inputs):
    c = derive(full_cfg())
    in_maps = prepare_inputs(c, **inputs)
    nc, n_ops = build_nc(c)
    res = run_bass_kernel_spmd(nc, in_maps, core_ids=list(range(NCORES)))
    out_fm = np.concatenate([res.results[i]["out_c"] for i in range(NCORES)], axis=1)
    return np.ascontiguousarray(out_fm.T).reshape(1, c["S"], c["D"]).astype(np.float32)


# version 12
# speedup vs baseline: 1.0119x; 1.0119x over previous
import math
from contextlib import ExitStack

import numpy as np
import ml_dtypes

import concourse.bass as bass
import concourse.mybir as mybir
from concourse.bass_utils import run_bass_kernel_spmd
from concourse.aot_env import allow_ici_all_to_all

F32 = mybir.dt.float32
BF16 = mybir.dt.bfloat16
AF = mybir.ActivationFunctionType
ALU = mybir.AluOpType
AXX = mybir.AxisListType.X

NCORES = 8
C_EXP = 8.0
LN_EPS = 1e-5
Q_EPS = 1e-8


def full_cfg():
    return dict(S=8192, D=2048, NH=32, FH=5632)


def derive(cfg):
    c = dict(cfg)
    S, D, NH, FH = c["S"], c["D"], c["NH"], c["FH"]
    assert D == NH * 64
    c["HPC"] = NH // NCORES
    assert c["HPC"] % 2 == 0 or c["HPC"] == 1
    c["PPC"] = max(1, c["HPC"] // 2)
    c["TPC"] = S // NCORES
    c["KC"] = D // 128
    c["NKT"] = S // 128
    c["QG"] = min(512, S)
    c["NQG"] = S // c["QG"]
    c["NDIAG"] = c["QG"] // 128
    c["MB"] = FH // 128
    c["DC"] = D // 128
    c["TH"] = min(512, c["TPC"])
    c["NTH"] = c["TPC"] // c["TH"]
    c["QKVW"] = 3 * c["HPC"] * 64
    return c



class Res:
    __slots__ = ("w", "r", "name")

    def __init__(self, name=""):
        self.w = {}
        self.r = {}
        self.name = name


class Prog:
    ENGS = ("pe", "act", "dve", "pool", "sp")
    DMA_RING = 6

    def __init__(self):
        self.ops = {e: [] for e in self.ENGS}
        self.cnt = {}
        self.guar = {e: {} for e in self.ENGS}
        self.pending = {e: None for e in self.ENGS}
        self.dma_i = {"sp": 0, "pool": 0}
        for e in self.ENGS:
            self.cnt[e] = 0
        for q in ("sp", "pool"):
            for i in range(self.DMA_RING):
                self.cnt[f"{q}d{i}"] = 0
        self.cnt["cc"] = 0

    def barrier(self):
        snap = dict(self.cnt)
        for e in self.ENGS:
            cur = self.pending[e]
            if cur is None:
                self.pending[e] = dict(snap)
            else:
                for s, v in snap.items():
                    cur[s] = max(cur.get(s, 0), v)

    def op(self, eng, emit, reads=(), writes=(), inc=1, dma=False, sem=None,
           lag=False):
        if dma:
            idx = self.dma_i[eng]
            self.dma_i[eng] += 1
            sem = f"{eng}d{idx % self.DMA_RING}"
            inc = 16
        elif sem is None:
            sem = eng
        waits = {}
        if dma:
            prev = 16 * (idx // self.DMA_RING)
            if prev > 0:
                waits[sem] = prev
        if self.pending[eng] is not None:
            for s, v in self.pending[eng].items():
                if v > 0:
                    waits[s] = max(waits.get(s, 0), v)
            self.pending[eng] = None
        for r in reads:
            for s, v in r.w.items():
                waits[s] = max(waits.get(s, 0), v)
        for wr in writes:
            for s, v in wr.w.items():
                waits[s] = max(waits.get(s, 0), v)
            for s, v in wr.r.items():
                waits[s] = max(waits.get(s, 0), v)
        g = self.guar[eng]

        def fwd_own(sname, v):
            return (sname == eng or sname.startswith(eng + "d")) and v > self.cnt[sname]

        ww = {s: v for s, v in waits.items()
              if v > g.get(s, 0) and not fwd_own(s, v)}
        for s, v in ww.items():
            g[s] = v
        self.ops[eng].append((ww, emit, inc, sem))
        self.cnt[sem] += inc
        me = (sem, self.cnt[sem] + (1 if lag else 0))
        for wr in writes:
            wr.w = {me[0]: me[1]}
            wr.r = {}
        for r in reads:
            r.r[me[0]] = me[1]

    def final_wait(self, eng):
        snap = dict(self.cnt)

        def emit(e, _snap=snap):
            for s, v in _snap.items():
                if v > 0:
                    e.wait_ge(self._sems[s], v)
        self.ops[eng].append(({}, ("finalwait", emit), 0, eng))

    def emit_all(self, nc, block):
        sems = {}
        ctx = self._ctx
        for s in self.cnt:
            sems[s] = ctx.enter_context(nc.semaphore(f"sem_{s}"))
        self._sems = sems

        def mk(section, eng_name):
            def body(engobj):
                for waits, emit, inc, sem in self.ops[eng_name]:
                    for s, v in sorted(waits.items()):
                        engobj.wait_ge(sems[s], v)
                    if isinstance(emit, tuple) and emit[0] == "finalwait":
                        emit[1](engobj)
                        continue
                    inst = emit(engobj)
                    if inc:
                        inst.then_inc(sems[sem], inc)
            section(body)

        mk(block.tensor, "pe")
        mk(block.scalar, "act")
        mk(block.vector, "dve")
        mk(block.gpsimd, "pool")
        mk(block.sync, "sp")



def build_nc(c, debug=()):
    S, D = c["S"], c["D"]
    HPC, PPC, TPC = c["HPC"], c["PPC"], c["TPC"]
    KC, NKT, QG, NQG = c["KC"], c["NKT"], c["QG"], c["NQG"]
    MB, DC, TH, NTH = c["MB"], c["DC"], c["TH"], c["NTH"]
    QKVW = c["QKVW"]
    HW = HPC * 64

    nc = bass.Bass("TRN2", target_bir_lowering=False)
    p = Prog()

    ap_in = {}
    for name, shape, dt in [
        ("x_fm", [D, S], F32),
        ("x_own", [D, TPC], F32),
        ("wqkv", [D, QKVW], F32),
        ("bqkv_b", [128, QKVW], F32),
        ("cossin", [S, 2 * HPC * 16], F32),
        ("masks", [c["NDIAG"], 128, QG], F32),
        ("ident", [128, 128], BF16),
        ("ones_row_bf", [1, 128], BF16),
        ("ones_col_bf", [128, 1], BF16),
        ("ones_row_f32", [1, 128], F32),
        ("consts", [128, 8], F32),
        ("wout", [D, D], F32),
        ("bout_g", [128, DC], F32),
        ("w1t", [MB, 128, KC, 128], F32),
        ("w2t", [MB, 128, KC, 128], F32),
        ("w3t", [DC, 128, MB, 128], F32),
    ]:
        ap_in[name] = nc.dram_tensor(name, shape, dt, kind="ExternalInput").ap()

    out_t = nc.dram_tensor("out_c", [D, TPC], F32, kind="ExternalOutput").ap()

    a2a_in = nc.dram_tensor("a2a_in", [D, TPC], BF16)
    a2a_out = nc.dram_tensor("a2a_out", [D, TPC], BF16)
    y_dram = nc.dram_tensor("y_dram", [D, TPC], F32)
    r2_dram = nc.dram_tensor("r2_dram", [D, TPC], F32)

    ctx = ExitStack()
    p._ctx = ctx
    e = ctx.enter_context

    def sb(name, shape, dt):
        return e(nc.sbuf_tensor(name, shape, dt))

    dbg = set(debug)

    def dump(name, src_ap, reads):
        if name not in dbg:
            return
        t = nc.dram_tensor("dbg_" + name, list(src_ap.shape), src_ap.dtype,
                           kind="ExternalOutput").ap()
        p.op("sp", lambda E, t=t, src_ap=src_ap: E.dma_start(out=t, in_=src_ap),
             reads=list(reads), writes=[], dma=True)

    consts_sb = sb("consts_sb", [128, 8], F32)
    ident_sb = sb("ident_sb", [128, 128], BF16)
    onesr_bf = sb("onesr_bf", [1, 128], BF16)
    onesc_bf = sb("onesc_bf", [128, 1], BF16)
    onesr_f = sb("onesr_f", [1, 128], F32)
    bout_sb = sb("bout_sb", [128, DC], F32)
    r_small = Res("small")
    for t, src in [(consts_sb, "consts"), (ident_sb, "ident"),
                   (onesr_bf, "ones_row_bf"), (onesc_bf, "ones_col_bf"),
                   (onesr_f, "ones_row_f32"), (bout_sb, "bout_g")]:
        p.op("sp", lambda E, t=t, src=src: E.dma_start(out=t[:, :], in_=ap_in[src]),
             writes=[r_small], dma=True)

    NG1 = (KC + 7) // 8 * MB
    NG3 = (MB + 7) // 8 * DC
    gacc1 = sb("gacc1", [128, NG1], F32)
    gacc2 = sb("gacc2", [128, NG1], F32)
    gacc3 = sb("gacc3", [128, NG3], F32)
    cvec = sb("cvec", [128, 4], F32)
    ncvec = sb("ncvec", [128, 4], F32)
    r_gacc = [Res("gacc1"), Res("gacc2"), Res("gacc3")]
    r_cvec = Res("cvec")

    r_A = Res("A")
    r_a2ain = Res("a2ain")

    GW = 1024
    GCH = GW // 128
    gamma_jobs = []
    gcols = [0, 0, 0]

    def add_gjobs(src3, acc):
        nt = src3.shape[1]
        for a in range(0, nt, GCH):
            b = min(a + GCH, nt)
            gamma_jobs.append((src3[:, a:b, :].rearrange("p m j -> p (m j)"),
                               (b - a) * 128, acc, gcols[acc]))
            gcols[acc] += 1
    for m in range(MB):
        add_gjobs(ap_in["w1t"][m], 0)
        add_gjobs(ap_in["w2t"][m], 1)
    for dc in range(DC):
        add_gjobs(ap_in["w3t"][dc], 2)

    def make_gamma_ops(gbuf, r_gbuf):
        state = {"i": 0}

        def step():
            i = state["i"]
            if i >= len(gamma_jobs):
                return
            state["i"] += 1
            src, width, acc, col = gamma_jobs[i]
            slot = i % 2
            gt = gbuf[slot]
            rs = r_gbuf[slot]
            accs = [gacc1, gacc2, gacc3][acc]
            p.op("sp", lambda E, gt=gt, src=src, width=width:
                 E.dma_start(out=gt[:, 0:width], in_=src),
                 reads=[], writes=[rs], dma=True)
            p.op("dve", lambda E, gt=gt, accs=accs, col=col, width=width:
                 E.tensor_reduce(accs[:, col:col + 1], gt[:, 0:width], AXX, ALU.add, True),
                 reads=[rs], writes=[r_gacc[acc]])
        return step

    with ExitStack() as s01:
        e1 = s01.enter_context
        A_st = [e1(nc.sbuf_tensor(f"A_st{i}", [64, QG], BF16)) for i in range(2)]
        q_sb = e1(nc.sbuf_tensor("q_sb", [128, HPC, S], BF16))
        k_sb = e1(nc.sbuf_tensor("k_sb", [128, PPC, S], BF16))
        v_sb = e1(nc.sbuf_tensor("v_sb", [128, NKT * HPC * 65 + 63], BF16))
        vv = v_sb[:, 0:NKT * HPC * 65].rearrange("p (t h c) -> p t h c", h=HPC, c=65)
        gbuf = [e1(nc.sbuf_tensor(f"gbuf{i}", [128, GW], F32)) for i in range(2)]
        r_gbuf = [Res("gbuf0"), Res("gbuf1")]
        gamma_step = make_gamma_ops(gbuf, r_gbuf)

        def drain_gamma():
            while True:
                before = len(p.ops["sp"]) + len(p.ops["pool"])
                gamma_step()
                if len(p.ops["sp"]) + len(p.ops["pool"]) == before:
                    break

        _gf_done = [False]
        gtot = e1(nc.sbuf_tensor("gtot", [128, 4], F32))
        grow = e1(nc.sbuf_tensor("grow", [1, 4], F32))
        onescf = e1(nc.sbuf_tensor("onescf", [128, 1], F32))

        def gamma_finalize():
            if _gf_done[0]:
                return
            _gf_done[0] = True
            ps_g = ps_s[0][0:1, 0:4]
            ps_gb = ps_s[0][0:128, 8:12]
            r_gtot, r_grow = Res("gtot"), Res("grow")
            nels = {0: c["FH"] * D, 1: c["FH"] * D, 2: D * c["FH"]}
            for i, (acc, w) in enumerate([(gacc1, gcols[0]), (gacc2, gcols[1]), (gacc3, gcols[2])]):
                p.op("dve", lambda E, i=i, acc=acc, w=w: E.tensor_reduce(
                    gtot[:, i:i + 1], acc[:, 0:w], AXX, ALU.add, False),
                    reads=[r_gacc[i]], writes=[r_gtot])
            p.op("dve", lambda E: E.memset(gtot[:, 3:4], 0.0), writes=[r_gtot])
            r_ocf = Res("ocf")
            p.op("dve", lambda E: E.memset(onescf[:, :], 1.0), writes=[r_ocf])
            p.op("pe", lambda E, ps_g=ps_g: E.matmul(ps_g, onescf[:, :], gtot[:, :],
                                                     start=True, stop=True),
                 reads=[r_gtot, r_ocf], writes=[r_ss[0]])
            p.op("dve", lambda E, ps_g=ps_g: E.tensor_copy(grow[:, :], ps_g),
                 reads=[r_ss[0]], writes=[r_grow])
            for i in range(3):
                scale = 0.5 / nels[i]
                p.op("dve", lambda E, i=i, scale=scale: E.tensor_scalar(
                    grow[:, i:i + 1], grow[:, i:i + 1], scale, 0.5 * Q_EPS,
                    ALU.mult, ALU.add),
                    reads=[r_grow], writes=[r_grow])
            p.op("pe", lambda E, ps_gb=ps_gb: E.matmul(ps_gb, onesr_f[:, :], grow[:, :],
                                                       start=True, stop=True),
                 reads=[r_grow, r_small], writes=[r_ss[0]])
            p.op("dve", lambda E, ps_gb=ps_gb: E.tensor_copy(cvec[:, :], ps_gb),
                 reads=[r_ss[0]], writes=[r_cvec])
            p.op("dve", lambda E: E.tensor_scalar(ncvec[:, :], cvec[:, :], -1.0, None, ALU.mult),
                 reads=[r_cvec], writes=[r_cvec])

        r_q, r_k, r_v = Res("q"), Res("k"), Res("v")
        r_Ast = [Res("Ast0"), Res("Ast1")]

        p.op("dve", lambda E: E.memset(v_sb[:, NKT * HPC * 65:], 0.0), writes=[r_v])
        p.op("dve", lambda E: E.memset(vv[:, :, :, 64:65], 1.0), writes=[r_v])
        for h in range(HPC):
            zlo, zhi = ((64, 128) if h % 2 == 0 else (0, 64)) if HPC > 1 else (64, 128)
            p.op("dve", lambda E, h=h, zlo=zlo, zhi=zhi:
                 E.memset(q_sb[zlo:zhi, h, :], 0.0), writes=[r_q])

        with ExitStack() as s0:
            e0 = s0.enter_context
            wqkv_sb = e0(nc.sbuf_tensor("wqkv_sb", [128, KC, QKVW], BF16))
            bqkv_sb = e0(nc.sbuf_tensor("bqkv_sb", [128, QKVW], F32))
            bksc_sb = e0(nc.sbuf_tensor("bksc_sb", [128, HW], F32))
            xg = [e0(nc.sbuf_tensor(f"xg{i}", [128, KC, 128], BF16)) for i in range(2)]
            cs = [e0(nc.sbuf_tensor(f"cs{i}", [128, 2 * HPC * 16], F32)) for i in range(2)]
            q_scr = [e0(nc.sbuf_tensor(f"q_scr{i}", [128, HPC, 64], F32)) for i in range(2)]
            k_scr = [e0(nc.sbuf_tensor(f"k_scr{i}", [128, HPC, 64], F32)) for i in range(2)]
            rt = [e0(nc.sbuf_tensor(f"rt{i}", [128, HPC, 16], F32)) for i in range(4)]
            qbf = [e0(nc.sbuf_tensor(f"qbf{i}", [128, HPC * 64], BF16)) for i in range(2)]
            kbf = [e0(nc.sbuf_tensor(f"kbf{i}", [128, HPC * 64], BF16)) for i in range(2)]

            ps_qk = [e0(nc.psum_tensor(f"ps_qk{i}", [128, 2 * HW], F32)) for i in range(2)]
            ps_v = [e0(nc.psum_tensor(f"ps_v{i}", [128, HW], F32)) for i in range(2)]
            ps_t = [e0(nc.psum_tensor(f"ps_t{i}", [128, 128], BF16)) for i in range(2)]

            r_wqkv, r_bq, r_bk = Res("wqkv"), Res("bqkv"), Res("bksc")
            r_xg = [Res("xg0"), Res("xg1")]
            r_cs = [Res("cs0"), Res("cs1")]
            r_psqk = [Res("psqk0"), Res("psqk1")]
            r_psv = [Res("psv0"), Res("psv1")]
            r_qscr = [Res("qscr0"), Res("qscr1")]
            r_kscr = [Res("kscr0"), Res("kscr1")]
            r_rt = [Res(f"rt{i}") for i in range(4)]
            r_qbf = [Res("qbf0"), Res("qbf1")]
            r_kbf = [Res("kbf0"), Res("kbf1")]
            r_pst = [Res("pst0"), Res("pst1")]

            p.op("pool", lambda E: E.dma_start(
                out=wqkv_sb[:, :, :],
                in_=ap_in["wqkv"].rearrange("(kc p) n -> p kc n", p=128)),
                writes=[r_wqkv], dma=True)
            p.op("sp", lambda E: E.dma_start(out=bqkv_sb[:, :], in_=ap_in["bqkv_b"]),
                 writes=[r_bq], dma=True)
            p.op("dve", lambda E: E.tensor_scalar(
                bksc_sb[:, :], bqkv_sb[:, HW:2 * HW], 0.125, None, ALU.mult),
                reads=[r_bq], writes=[r_bk])

            def emit_transposes(tt):
                pq = tt % 2
                for pr in range(PPC):
                    for pp, (bf, rbf) in enumerate(((qbf[pq], r_qbf[pq]), (kbf[pq], r_kbf[pq]))):
                        src_ap = bf[:, pr * 128:(pr + 1) * 128]
                        p.op("pe", lambda E, src_ap=src_ap, pp=pp:
                             E.transpose(ps_t[pp][:, :], src_ap, ident_sb[:, :]),
                             reads=[rbf, r_small], writes=[r_pst[pp]])
                        if pp == 1:
                            p.op("dve", lambda E, pp=pp, pr=pr, tt=tt:
                                 E.tensor_copy(k_sb[:, pr, tt * 128:(tt + 1) * 128],
                                               ps_t[pp][:, :]),
                                 reads=[r_pst[pp]], writes=[r_k])
                        else:
                            p.op("dve", lambda E, pp=pp, pr=pr, tt=tt:
                                 E.tensor_copy(q_sb[0:64, 2 * pr, tt * 128:(tt + 1) * 128],
                                               ps_t[pp][0:64, :]),
                                 reads=[r_pst[pp]], writes=[r_q])
                            if HPC > 1:
                                p.op("dve", lambda E, pp=pp, pr=pr, tt=tt:
                                     E.tensor_copy(q_sb[64:128, 2 * pr + 1, tt * 128:(tt + 1) * 128],
                                                   ps_t[pp][64:128, :]),
                                     reads=[r_pst[pp]], writes=[r_q])

            for tt in range(NKT):
                slot = tt % 2
                p.op("pool", lambda E, tt=tt, slot=slot: E.dma_start(
                    out=xg[slot][:, :, :],
                    in_=ap_in["x_fm"][:, tt * 128:(tt + 1) * 128]
                        .rearrange("(kc p) t -> p kc t", p=128)),
                    writes=[r_xg[slot]], dma=True)
                p.op("sp", lambda E, tt=tt, slot=slot: E.dma_start(
                    out=cs[slot][:, :],
                    in_=ap_in["cossin"][tt * 128:(tt + 1) * 128, :]),
                    writes=[r_cs[slot]], dma=True)
                pp0 = tt % 2
                for kc in range(KC):
                    st = kc == 0
                    sp_ = kc == KC - 1
                    for ps, rps, off, w_ in ((ps_qk[pp0], r_psqk[pp0], 0, 2 * HW),
                                             (ps_v[pp0], r_psv[pp0], 2 * HW, HW)):
                        p.op("pe", lambda E, slot=slot, kc=kc, ps=ps, off=off, w_=w_, st=st, sp_=sp_:
                             E.matmul(ps[:, :], xg[slot][:, kc, :],
                                      wqkv_sb[:, kc, off:off + w_], start=st, stop=sp_),
                             reads=[r_xg[slot], r_wqkv],
                             writes=[rps] if (st or sp_) else [],
                             inc=(1 if sp_ else 0), lag=not sp_)
                p.op("dve", lambda E, pp0=pp0: E.tensor_tensor(
                    q_scr[pp0][:, :, :], ps_qk[pp0][:, 0:HW].rearrange("p (h d) -> p h d", d=64),
                    bqkv_sb[:, 0:HW].rearrange("p (h d) -> p h d", d=64), ALU.add),
                    reads=[r_psqk[pp0], r_bq], writes=[r_qscr[pp0]])
                p.op("dve", lambda E, pp0=pp0: E.scalar_tensor_tensor(
                    k_scr[pp0][:, :, :], ps_qk[pp0][:, HW:2 * HW].rearrange("p (h d) -> p h d", d=64),
                    0.125, bksc_sb[:, :].rearrange("p (h d) -> p h d", d=64),
                    ALU.mult, ALU.add),
                    reads=[r_psqk[pp0], r_bk], writes=[r_kscr[pp0]])
                p.op("dve", lambda E, tt=tt, pp0=pp0: E.tensor_tensor(
                    vv[:, tt, :, 0:64], ps_v[pp0][:, :].rearrange("p (h d) -> p h d", d=64),
                    bqkv_sb[:, 2 * HW:].rearrange("p (h d) -> p h d", d=64), ALU.add),
                    reads=[r_psv[pp0], r_bq], writes=[r_v])
                for scr, rscr in ((q_scr[pp0], r_qscr[pp0]), (k_scr[pp0], r_kscr[pp0])):
                    u1 = scr[:, :, 0:16]
                    u2 = scr[:, :, 16:32]
                    co = cs[slot][:, 0:HPC * 16].rearrange("p (h d) -> p h d", d=16)
                    si = cs[slot][:, HPC * 16:].rearrange("p (h d) -> p h d", d=16)
                    p.op("dve", lambda E, u1=u1, co=co: E.tensor_tensor(rt[0][:, :, :], u1, co, ALU.mult),
                         reads=[rscr, r_cs[slot]], writes=[r_rt[0]])
                    p.op("dve", lambda E, u2=u2, si=si: E.tensor_tensor(rt[1][:, :, :], u2, si, ALU.mult),
                         reads=[rscr, r_cs[slot]], writes=[r_rt[1]])
                    p.op("dve", lambda E, u1=u1, si=si: E.tensor_tensor(rt[2][:, :, :], u1, si, ALU.mult),
                         reads=[rscr, r_cs[slot]], writes=[r_rt[2]])
                    p.op("dve", lambda E, u2=u2, co=co: E.tensor_tensor(rt[3][:, :, :], u2, co, ALU.mult),
                         reads=[rscr, r_cs[slot]], writes=[r_rt[3]])
                    p.op("dve", lambda E, u1=u1: E.tensor_tensor(u1, rt[0][:, :, :], rt[1][:, :, :], ALU.subtract),
                         reads=[r_rt[0], r_rt[1]], writes=[rscr])
                    p.op("dve", lambda E, u2=u2: E.tensor_tensor(u2, rt[2][:, :, :], rt[3][:, :, :], ALU.add),
                         reads=[r_rt[2], r_rt[3]], writes=[rscr])
                p.op("dve", lambda E, pp0=pp0: E.tensor_copy(
                    qbf[pp0][:, :], q_scr[pp0][:, :, :].rearrange("p h d -> p (h d)")),
                    reads=[r_qscr[pp0]], writes=[r_qbf[pp0]])
                p.op("dve", lambda E, pp0=pp0: E.tensor_copy(
                    kbf[pp0][:, :], k_scr[pp0][:, :, :].rearrange("p h d -> p (h d)")),
                    reads=[r_kscr[pp0]], writes=[r_kbf[pp0]])
                if tt >= 1:
                    emit_transposes(tt - 1)
                if tt == NKT - 1:
                    emit_transposes(tt)
                gamma_step()
                gamma_step()
            dump("q_sb", q_sb[:, :, :], [r_q])
            dump("k_sb", k_sb[:, :, :], [r_k])
            dump("v_sb", v_sb[:, :], [r_v])
            p.barrier()

        NS = 3
        with ExitStack() as s1:
            e2 = s1.enter_context
            masks_sb = e2(nc.sbuf_tensor("masks_sb", [128, c["NDIAG"], QG], F32))
            NPS = 6
            p_sb = [e2(nc.sbuf_tensor(f"p_sb{i}", [128, 2 * QG], BF16)) for i in range(NPS)]
            rcp_f = e2(nc.sbuf_tensor("rcp_f", [1, QG], F32))
            rcp_bf = e2(nc.sbuf_tensor("rcp_bf", [1, QG], BF16))
            rb_sb = e2(nc.sbuf_tensor("rb_sb", [64, QG], F32))
            ps_s = [e2(nc.psum_tensor(f"ps_s{i}", [128, 2 * QG], F32)) for i in range(NS)]
            ps_o = [e2(nc.psum_tensor(f"ps_o{i}", [128, QG], F32)) for i in range(2)]

            r_ps = [Res(f"p{i}") for i in range(NPS)]
            r_ss = [Res(f"ss{i}") for i in range(NS)]
            r_so = [Res("so0"), Res("so1")]
            r_rcp, r_rb = Res("rcp"), Res("rb")
            r_masks = Res("masks")
            p.op("sp", lambda E: E.dma_start(
                out=masks_sb[:, :, :],
                in_=ap_in["masks"].rearrange("n p q -> p n q")),
                writes=[r_masks], dma=True)

            a2a_v = a2a_in.ap().rearrange("(d hl hd) t -> hl hd d t", hl=HPC, hd=64)

            LAG = 2
            unit = 0
            for h in range(HPC):
                pr, hp = h // 2, h % 2
                for qg in range(NQG):
                    op_ = unit % 2
                    nkc = (qg + 1) * (QG // 128)
                    np_ = nkc // 2

                    def emit_spair(j, h=h, pr=pr, qg=qg):
                        sl = j % NS
                        for half in range(2):
                            kc = 2 * j + half
                            p.op("pe", lambda E, sl=sl, half=half, kc=kc, h=h, pr=pr, qg=qg:
                                 E.matmul(ps_s[sl][:, half * QG:(half + 1) * QG],
                                          k_sb[:, pr, kc * 128:(kc + 1) * 128],
                                          q_sb[:, h, qg * QG:(qg + 1) * QG],
                                          start=True, stop=True),
                                 reads=[r_k, r_q],
                                 writes=[r_ss[sl]],
                                 inc=(1 if half == 1 else 0), lag=(half == 0))
                        for half in range(2):
                            kc = 2 * j + half
                            diag = kc - qg * (QG // 128)
                            if diag >= 0:
                                p.op("dve", lambda E, sl=sl, half=half, diag=diag:
                                     E.tensor_tensor(
                                         ps_s[sl][:, half * QG:(half + 1) * QG],
                                         ps_s[sl][:, half * QG:(half + 1) * QG],
                                         masks_sb[:, diag, :], ALU.add),
                                     reads=[r_masks, r_ss[sl]], writes=[r_ss[sl]])
                        pl = j % NPS
                        p.op("act", lambda E, sl=sl, pl=pl: E.activation(
                            p_sb[pl][:, :], ps_s[sl][:, :], AF.Exp,
                            bias=consts_sb[:, 0:1], scale=1.0),
                            reads=[r_ss[sl], r_small], writes=[r_ps[pl]])

                    def emit_avpair(j, h=h, op_=op_, np_=np_):
                        pl = j % NPS
                        for half in range(2):
                            kc = 2 * j + half
                            st = kc == 0
                            sp_ = kc == 2 * np_ - 1
                            vb = (kc * HPC + h) * 65
                            p.op("pe", lambda E, pl=pl, half=half, vb=vb, op_=op_, st=st, sp_=sp_:
                                 E.matmul(ps_o[op_][:, :], v_sb[:, vb:vb + 128],
                                          p_sb[pl][:, half * QG:(half + 1) * QG],
                                          start=st, stop=sp_),
                                 reads=[r_v, r_ps[pl]],
                                 writes=[r_so[op_]] if (st or sp_) else [],
                                 inc=(1 if sp_ else 0), lag=not sp_)

                    for j in range(np_ + LAG):
                        if j < np_:
                            emit_spair(j)
                        if j >= LAG:
                            emit_avpair(j - LAG)

                    bsl = (np_ - 1) % NS
                    ast = unit % 2
                    p.op("dve", lambda E, op_=op_: E.reciprocal(rcp_f[:, :], ps_o[op_][64:65, :]),
                         reads=[r_so[op_]], writes=[r_rcp])
                    p.op("dve", lambda E: E.tensor_copy(rcp_bf[:, :], rcp_f[:, :]),
                         reads=[r_rcp], writes=[r_rcp])
                    p.op("pe", lambda E, bsl=bsl: E.matmul(
                        ps_s[bsl][0:64, 0:QG], onesr_bf[:, 0:64], rcp_bf[:, :],
                        start=True, stop=True),
                        reads=[r_rcp, r_small], writes=[r_ss[bsl]])
                    p.op("dve", lambda E, bsl=bsl: E.tensor_copy(rb_sb[:, :], ps_s[bsl][0:64, 0:QG]),
                         reads=[r_ss[bsl]], writes=[r_rb])
                    p.op("dve", lambda E, op_=op_, ast=ast: E.tensor_tensor(
                        A_st[ast][:, :], ps_o[op_][0:64, :], rb_sb[:, :], ALU.mult),
                        reads=[r_so[op_], r_rb], writes=[r_Ast[ast]])
                    hl = 2 * pr + hp if HPC > 1 else 0
                    tok0 = qg * QG
                    if QG <= TPC:
                        dst = a2a_v[hl][:, tok0 // TPC, tok0 % TPC:tok0 % TPC + QG]
                    else:
                        dst = a2a_v[hl][:, tok0 // TPC:(tok0 + QG) // TPC, :]
                    p.op("sp", lambda E, dst=dst, ast=ast: E.dma_start(
                        out=dst, in_=A_st[ast][:, :]),
                        reads=[r_Ast[ast]], writes=[r_a2ain], dma=True)
                    unit += 1
                    for _ in range(4):
                        gamma_step()
                    if unit == min(48, HPC * NQG - 2):
                        drain_gamma()
                        gamma_finalize()
            drain_gamma()
            gamma_finalize()
            p.barrier()

        dump("cvec", cvec[:, :], [r_cvec])
        dump("gacc1", gacc1[:, :], [r_gacc[0]])
        dump("gacc3", gacc3[:, :], [r_gacc[2]])
        p.barrier()

    r_a2a = Res("a2a")

    def emit_cc(E):
        with allow_ici_all_to_all():
            return E.collective_compute(
                "AllToAll", ALU.bypass, replica_groups=[list(range(NCORES))],
                ins=[a2a_in.ap()], outs=[a2a_out.ap()])
    p.op("pool", emit_cc, reads=[r_a2ain], writes=[r_a2a], inc=1, sem="cc")

    with ExitStack() as s23:
        e23 = s23.enter_context
        T = TPC
        stat = e23(nc.sbuf_tensor("stat", [1, 4 * TPC], F32))
        stat_bf = e23(nc.sbuf_tensor("stat_bf", [1, 2 * TPC], BF16))
        RB = e23(nc.sbuf_tensor("RB", [128, TPC], F32))
        MBc = e23(nc.sbuf_tensor("MBc", [128, TPC], F32))
        ysc = e23(nc.sbuf_tensor("ysc", [128, TPC], F32))
        r_stat, r_statbf = Res("stat"), Res("statbf")
        r_RB, r_MB, r_ysc = Res("RB"), Res("MB"), Res("ysc")

        def ln_finalize(ps1, ps2, rp1, rp2, nfeat, ps_bc, r_psbc):
            for th in range(NTH):
                p.op("dve", lambda E, th=th, ps1=ps1: E.tensor_scalar(
                    stat[:, th * TH:(th + 1) * TH], ps1[th][:, :],
                    1.0 / nfeat, None, ALU.mult),
                    reads=[rp1[th]], writes=[r_stat])
                p.op("dve", lambda E, th=th, ps2=ps2: E.tensor_scalar(
                    stat[:, T + th * TH:T + (th + 1) * TH], ps2[th][:, :],
                    1.0 / nfeat, None, ALU.mult),
                    reads=[rp2[th]], writes=[r_stat])
            p.op("dve", lambda E: E.tensor_tensor(
                stat[:, 3 * T:4 * T], stat[:, 0:T], stat[:, 0:T], ALU.mult),
                reads=[r_stat], writes=[r_stat])
            p.op("dve", lambda E: E.tensor_tensor(
                stat[:, T:2 * T], stat[:, T:2 * T], stat[:, 3 * T:4 * T], ALU.subtract),
                reads=[r_stat], writes=[r_stat])
            p.op("act", lambda E: E.activation(
                stat[:, 2 * T:3 * T], stat[:, T:2 * T], AF.Ln,
                bias=consts_sb[0:1, 1:2], scale=1.0),
                reads=[r_stat, r_small], writes=[r_stat])
            p.op("act", lambda E: E.activation(
                stat[:, 2 * T:3 * T], stat[:, 2 * T:3 * T], AF.Exp,
                bias=0.0, scale=-0.5),
                reads=[r_stat], writes=[r_stat])
            p.op("dve", lambda E: E.tensor_tensor(
                stat[:, 3 * T:4 * T], stat[:, 0:T], stat[:, 2 * T:3 * T], ALU.mult),
                reads=[r_stat], writes=[r_stat])
            p.op("dve", lambda E: E.tensor_copy(stat_bf[:, 0:T], stat[:, 2 * T:3 * T]),
                 reads=[r_stat], writes=[r_statbf])
            p.op("dve", lambda E: E.tensor_copy(stat_bf[:, T:2 * T], stat[:, 3 * T:4 * T]),
                 reads=[r_stat], writes=[r_statbf])
            for dst, rdst, off in ((RB, r_RB, 0), (MBc, r_MB, TPC)):
                for th in range(NTH):
                    p.op("pe", lambda E, off=off, th=th, ps_bc=ps_bc: E.matmul(
                        ps_bc[:, :], onesr_bf[:, :],
                        stat_bf[:, off + th * TH:off + (th + 1) * TH],
                        start=True, stop=True),
                        reads=[r_statbf, r_small], writes=[r_psbc])
                    p.op("dve", lambda E, dst=dst, th=th, ps_bc=ps_bc: E.tensor_copy(
                        dst[:, th * TH:(th + 1) * TH], ps_bc[:, :]),
                        reads=[r_psbc], writes=[rdst])

        sR = ExitStack()
        ybf_sb = sR.enter_context(nc.sbuf_tensor("ybf_sb", [128, KC, TPC], BF16, side="right"))
        r_ybf = Res("ybf")
        sR2 = ExitStack()
        xo_sb = [sR2.enter_context(nc.sbuf_tensor(f"xo{i}", [128, TPC], F32, side="right")) for i in range(2)]
        wo_sb = [sR2.enter_context(nc.sbuf_tensor(f"wo{i}", [128, KC, 128], BF16, side="right")) for i in range(2)]
        r_xo = [Res("xo0"), Res("xo1")]
        r_wo = [Res("wo0"), Res("wo1")]

        def wout_dma(dc):
            slw = dc % 2
            p.op("pool", lambda E, dc=dc, slw=slw: E.dma_start(
                out=wo_sb[slw][:, :, :],
                in_=ap_in["wout"][:, dc * 128:(dc + 1) * 128]
                    .rearrange("(kc p) j -> p kc j", p=128)),
                writes=[r_wo[slw]], dma=True)
            p.op("sp", lambda E, dc=dc, slw=slw: E.dma_start(
                out=xo_sb[slw][:, :], in_=ap_in["x_own"][dc * 128:(dc + 1) * 128, :]),
                writes=[r_xo[slw]], dma=True)

        wout_dma(0)

        with ExitStack() as s2:
            e3 = s2.enter_context
            af_sb = e3(nc.sbuf_tensor("af_sb", [128, KC, TPC], BF16))
            r1_sb = e3(nc.sbuf_tensor("r1_sb", [128, KC, TPC], F32))
            r1b = e3(nc.sbuf_tensor("r1b", [128, TPC], BF16))
            r1sq = e3(nc.sbuf_tensor("r1sq", [128, TPC], BF16))

            ps_w = [e3(nc.psum_tensor(f"ps_w{i}", [128, TH], F32)) for i in range(2)]
            ps_s1 = [e3(nc.psum_tensor(f"ps_s1{i}", [1, TH], F32)) for i in range(NTH)]
            ps_s2 = [e3(nc.psum_tensor(f"ps_s2{i}", [1, TH], F32)) for i in range(NTH)]
            ps_bc = e3(nc.psum_tensor("ps_bc", [128, TH], F32))

            r_af = Res("af")
            r_r1 = Res("r1")
            r_psw = [Res("psw0"), Res("psw1")]
            r_ps1 = [Res(f"ps1{i}") for i in range(NTH)]
            r_ps2 = [Res(f"ps2{i}") for i in range(NTH)]
            r_r1b, r_r1sq = Res("r1b"), Res("r1sq")
            r_psbc = Res("psbc")

            p.op("pool", lambda E: E.dma_start(
                out=af_sb[:, :, :],
                in_=a2a_out.ap().rearrange("(kc p) t -> p kc t", p=128)),
                reads=[r_a2a], writes=[r_af], dma=True)

            for dc in range(DC):
                slw = dc % 2
                if dc + 1 < DC:
                    wout_dma(dc + 1)
                for th in range(NTH):
                    pw = (dc * NTH + th) % 2
                    for kc in range(KC):
                        stw = kc == 0
                        spw = kc == KC - 1
                        p.op("pe", lambda E, slw=slw, kc=kc, th=th, pw=pw,
                             st=stw, sp_=spw:
                             E.matmul(ps_w[pw][:, :], wo_sb[slw][:, kc, :],
                                      af_sb[:, kc, th * TH:(th + 1) * TH], start=st, stop=sp_),
                             reads=[r_wo[slw], r_af],
                             writes=[r_psw[pw]] if (stw or spw) else [],
                             inc=(1 if spw else 0), lag=not spw)
                    p.op("dve", lambda E, dc=dc, th=th, pw=pw, slw=slw:
                         E.scalar_tensor_tensor(
                             r1_sb[:, dc, th * TH:(th + 1) * TH], ps_w[pw][:, :],
                             bout_sb[:, dc:dc + 1], xo_sb[slw][:, th * TH:(th + 1) * TH],
                             ALU.add, ALU.add),
                         reads=[r_psw[pw], r_xo[slw], r_small], writes=[r_r1])
                p.op("dve", lambda E, dc=dc: E.tensor_copy(r1b[:, :], r1_sb[:, dc, :]),
                     reads=[r_r1], writes=[r_r1b])
                p.op("act", lambda E: E.activation(r1sq[:, :], r1b[:, :], AF.Square),
                     reads=[r_r1b], writes=[r_r1sq])
                for th in range(NTH):
                    sp2 = dc == DC - 1
                    st2 = dc == 0
                    p.op("pe", lambda E, th=th, dc=dc, st=st2, sp_=sp2:
                         E.matmul(ps_s1[th][:, :], onesc_bf[:, :],
                                  r1b[:, th * TH:(th + 1) * TH], start=st, stop=sp_),
                         reads=[r_r1b, r_small], writes=[r_ps1[th]])
                    p.op("pe", lambda E, th=th, dc=dc, st=st2, sp_=sp2:
                         E.matmul(ps_s2[th][:, :], onesc_bf[:, :],
                                  r1sq[:, th * TH:(th + 1) * TH], start=st, stop=sp_),
                         reads=[r_r1sq, r_small], writes=[r_ps2[th]])

            ln_finalize(ps_s1, ps_s2, r_ps1, r_ps2, D, ps_bc, r_psbc)

            dump("r1", r1_sb[:, :, :], [r_r1])
            dump("af", af_sb[:, :, :], [r_af])
            for dc in range(DC):
                p.op("dve", lambda E, dc=dc: E.tensor_tensor(
                    ysc[:, :], r1_sb[:, dc, :], RB[:, :], ALU.mult),
                    reads=[r_r1, r_RB], writes=[r_ysc])
                p.op("dve", lambda E, dc=dc: E.tensor_tensor(
                    ysc[:, :], ysc[:, :], MBc[:, :], ALU.subtract),
                    reads=[r_MB, r_ysc], writes=[r_ysc])
                p.op("dve", lambda E, dc=dc: E.tensor_copy(ybf_sb[:, dc, :], ysc[:, :]),
                     reads=[r_ysc], writes=[r_ybf])
                p.op("sp", lambda E, dc=dc: E.dma_start(
                    out=y_dram.ap()[dc * 128:(dc + 1) * 128, :], in_=ysc[:, :]),
                    reads=[r_ysc], writes=[], dma=True)
            dump("ybf", ybf_sb[:, :, :], [r_ybf])
            p.barrier()

        sR2.close()

        with ExitStack() as s3:
            e4 = s3.enter_context
            H_sb = e4(nc.sbuf_tensor("H_sb", [128, MB, TPC], BF16))
            r_H = Res("H")
            with ExitStack() as s3a:
                e5 = s3a.enter_context
                wb = [e5(nc.sbuf_tensor(f"wb{i}", [128, KC * 128], F32)) for i in range(4)]
                qw = [e5(nc.sbuf_tensor(f"qw{i}", [128, KC, 128], BF16)) for i in range(4)]
                sig = e5(nc.sbuf_tensor("sig", [128, TH], BF16))
                ps_x1 = [e5(nc.psum_tensor(f"ps_x1{i}", [128, TH], F32)) for i in range(2)]
                ps_x2 = [e5(nc.psum_tensor(f"ps_x2{i}", [128, TH], F32)) for i in range(2)]
                r_wb = [Res(f"wb{i}") for i in range(4)]
                r_qw = [Res(f"qw{i}") for i in range(4)]
                r_sig = Res("sig")
                r_px1 = [Res("px10"), Res("px11")]
                r_px2 = [Res("px20"), Res("px21")]

                ts2 = e5(nc.sbuf_tensor("ts2", [128, KC * 128], BF16))
                r_ts2 = Res("ts2")

                def quantize(src_sb, r_src, width, dst, r_dst, ci):
                    p.op("act", lambda E: E.activation(
                        dst, src_sb[:, 0:width], AF.Sign,
                        bias=ncvec[:, ci:ci + 1], scale=1.0),
                        reads=[r_src, r_cvec], writes=[r_dst])
                    p.op("act", lambda E: E.activation(
                        ts2[:, 0:width], src_sb[:, 0:width], AF.Sign,
                        bias=cvec[:, ci:ci + 1], scale=1.0),
                        reads=[r_src, r_cvec], writes=[r_ts2])
                    p.op("dve", lambda E: E.tensor_tensor(
                        dst, dst, ts2[:, 0:width], ALU.add),
                        reads=[r_ts2, r_dst], writes=[r_dst])

                def w12_dma(m):
                    slw = m % 2
                    for wi, (wt, q_) in enumerate((("w1t", "sp"), ("w2t", "pool"))):
                        qi = slw * 2 + wi
                        p.op(q_, lambda E, wt=wt, m=m, qi=qi: E.dma_start(
                            out=wb[qi][:, :], in_=ap_in[wt][m].rearrange("p kc j -> p (kc j)")),
                            writes=[r_wb[qi]], dma=True)

                w12_dma(0)
                for m in range(MB):
                    slw = m % 2
                    if m + 1 < MB:
                        w12_dma(m + 1)
                    for wi, ci in ((0, 0), (1, 1)):
                        qi = slw * 2 + wi
                        quantize(wb[qi], r_wb[qi], KC * 128,
                                 qw[qi][:, :, :].rearrange("p kc j -> p (kc j)"), r_qw[qi], ci)
                        psx = (ps_x1, ps_x2)[wi]
                        rpx = (r_px1, r_px2)[wi]
                        for th in range(NTH):
                            px = th % 2
                            for kc in range(KC):
                                stf = kc == 0
                                spf = kc == KC - 1
                                p.op("pe", lambda E, qi=qi, kc=kc, th=th, px=px, psx=psx,
                                     st=stf, sp_=spf:
                                     E.matmul(psx[px][:, :], qw[qi][:, kc, :],
                                              ybf_sb[:, kc, th * TH:(th + 1) * TH],
                                              start=st, stop=sp_),
                                     reads=[r_qw[qi], r_ybf],
                                     writes=[rpx[px]] if (stf or spf) else [],
                                     inc=(1 if spf else 0), lag=not spf)
                    for th in range(NTH):
                        px = th % 2
                        p.op("act", lambda E, px=px: E.activation(
                            sig[:, :], ps_x1[px][:, :], AF.Sigmoid, scale=0.5),
                            reads=[r_px1[px]], writes=[r_sig])
                        p.op("dve", lambda E, m=m, th=th, px=px: E.tensor_tensor(
                            H_sb[:, m, th * TH:(th + 1) * TH], ps_x2[px][:, :],
                            sig[:, :], ALU.mult),
                            reads=[r_px2[px], r_sig], writes=[r_H])
                dump("H", H_sb[:, :, :], [r_H])
                p.barrier()
            sR.close()

            with ExitStack() as s3b:
                e6 = s3b.enter_context
                WH3 = (MB + 1) // 2
                w3b = [e6(nc.sbuf_tensor(f"w3b{i}", [128, WH3 * 128], F32)) for i in range(4)]
                q3 = [e6(nc.sbuf_tensor(f"q3{i}", [128, MB, 128], BF16)) for i in range(2)]
                yst = [e6(nc.sbuf_tensor("yst0", [128, TPC], F32))]
                ts23 = e6(nc.sbuf_tensor("ts23", [128, WH3 * 128], BF16))
                r_ts23 = Res("ts23")
                r2c = [e6(nc.sbuf_tensor(f"r2c{i}", [128, TPC], F32)) for i in range(2)]
                r2b = e6(nc.sbuf_tensor("r2b", [128, TPC], BF16))
                r2sq = r2b
                ps_r = [e6(nc.psum_tensor(f"ps_r{i}", [128, TH], F32)) for i in range(2)]
                ps_t1 = [e6(nc.psum_tensor(f"ps_t1{i}", [1, TH], F32)) for i in range(NTH)]
                ps_t2 = [e6(nc.psum_tensor(f"ps_t2{i}", [1, TH], F32)) for i in range(NTH)]
                r_w3b = [Res(f"w3b{i}") for i in range(4)]
                r_q3 = [Res("q30"), Res("q31")]
                r_yst = [Res("yst0")]
                r_r2c = [Res("r2c0"), Res("r2c1")]
                r_r2b, r_r2sq = Res("r2b"), Res("r2sq")
                r_pr = [Res("pr0"), Res("pr1")]
                r_pt1 = [Res(f"pt1{i}") for i in range(NTH)]
                r_pt2 = [Res(f"pt2{i}") for i in range(NTH)]
                r_r2dram = Res("r2dram")

                mh3 = MB // 2

                def w3_dma(dc):
                    slq = dc % 2
                    for hi, (m0, m1, q_) in enumerate(((0, mh3, "sp"), (mh3, MB, "pool"))):
                        slw = slq * 2 + hi
                        p.op(q_, lambda E, dc=dc, slw=slw, m0=m0, m1=m1: E.dma_start(
                            out=w3b[slw][:, 0:(m1 - m0) * 128],
                            in_=ap_in["w3t"][dc][:, m0:m1, :].rearrange("p m j -> p (m j)")),
                            writes=[r_w3b[slw]], dma=True)

                w3_dma(0)
                for dc in range(DC):
                    slq = dc % 2
                    if dc + 1 < DC:
                        w3_dma(dc + 1)
                    for hi, (m0, m1) in enumerate(((0, mh3), (mh3, MB))):
                        slw = slq * 2 + hi
                        wdt = (m1 - m0) * 128
                        qdst = q3[slq][:, m0:m1, :].rearrange("p m j -> p (m j)")
                        p.op("act", lambda E, slw=slw, wdt=wdt, qdst=qdst: E.activation(
                            qdst, w3b[slw][:, 0:wdt], AF.Sign,
                            bias=ncvec[:, 2:3], scale=1.0),
                            reads=[r_w3b[slw], r_cvec], writes=[r_q3[slq]])
                        p.op("act", lambda E, slw=slw, wdt=wdt: E.activation(
                            ts23[:, 0:wdt], w3b[slw][:, 0:wdt], AF.Sign,
                            bias=cvec[:, 2:3], scale=1.0),
                            reads=[r_w3b[slw], r_cvec], writes=[r_ts23])
                        p.op("dve", lambda E, wdt=wdt, qdst=qdst: E.tensor_tensor(
                            qdst, qdst, ts23[:, 0:wdt], ALU.add),
                            reads=[r_ts23, r_q3[slq]], writes=[r_q3[slq]])
                    p.op("sp", lambda E, dc=dc: E.dma_start(
                        out=yst[0][:, :], in_=y_dram.ap()[dc * 128:(dc + 1) * 128, :]),
                        reads=[], writes=[r_yst[0]], dma=True)
                    for th in range(NTH):
                        pr_ = (dc * NTH + th) % 2
                        for m in range(MB):
                            st3 = m == 0
                            sp3 = m == MB - 1
                            p.op("pe", lambda E, slq=slq, m=m, th=th, pr_=pr_,
                                 st=st3, sp_=sp3:
                                 E.matmul(ps_r[pr_][:, :], q3[slq][:, m, :],
                                          H_sb[:, m, th * TH:(th + 1) * TH],
                                          start=st, stop=sp_),
                                 reads=[r_q3[slq], r_H],
                                 writes=[r_pr[pr_]] if (st3 or sp3) else [],
                                 inc=(1 if sp3 else 0), lag=not sp3)
                        p.op("dve", lambda E, slq=slq, th=th, pr_=pr_: E.scalar_tensor_tensor(
                            r2c[slq][:, th * TH:(th + 1) * TH], ps_r[pr_][:, :],
                            0.25, yst[0][:, th * TH:(th + 1) * TH], ALU.mult, ALU.add),
                            reads=[r_pr[pr_], r_yst[0]], writes=[r_r2c[slq]])
                    p.op("dve", lambda E, slq=slq: E.tensor_copy(r2b[:, :], r2c[slq][:, :]),
                         reads=[r_r2c[slq]], writes=[r_r2b])
                    for th in range(NTH):
                        sp2 = dc == DC - 1
                        st2 = dc == 0
                        p.op("pe", lambda E, th=th, dc=dc, st=st2, sp_=sp2:
                             E.matmul(ps_t1[th][:, :], onesc_bf[:, :],
                                      r2b[:, th * TH:(th + 1) * TH], start=st, stop=sp_),
                             reads=[r_r2b, r_small], writes=[r_pt1[th]])
                    p.op("act", lambda E: E.activation(r2sq[:, :], r2b[:, :], AF.Square),
                         reads=[r_r2b], writes=[r_r2b])
                    for th in range(NTH):
                        sp2 = dc == DC - 1
                        st2 = dc == 0
                        p.op("pe", lambda E, th=th, dc=dc, st=st2, sp_=sp2:
                             E.matmul(ps_t2[th][:, :], onesc_bf[:, :],
                                      r2sq[:, th * TH:(th + 1) * TH], start=st, stop=sp_),
                             reads=[r_r2b, r_small], writes=[r_pt2[th]])
                    p.op("sp", lambda E, dc=dc, slq=slq: E.dma_start(
                        out=r2_dram.ap()[dc * 128:(dc + 1) * 128, :], in_=r2c[slq][:, :]),
                        reads=[r_r2c[slq]], writes=[r_r2dram], dma=True)

                ps_bc2 = e6(nc.psum_tensor("ps_bc2", [128, TH], F32))
                r_psbc2 = Res("psbc2")
                ln_finalize(ps_t1, ps_t2, r_pt1, r_pt2, D, ps_bc2, r_psbc2)

                for dc in range(DC):
                    slq = dc % 2
                    p.op("sp", lambda E, dc=dc, slq=slq: E.dma_start(
                        out=r2c[slq][:, :], in_=r2_dram.ap()[dc * 128:(dc + 1) * 128, :]),
                        reads=[r_r2dram], writes=[r_r2c[slq]], dma=True)
                    p.op("dve", lambda E, slq=slq: E.tensor_tensor(
                        r2c[slq][:, :], r2c[slq][:, :], RB[:, :], ALU.mult),
                        reads=[r_RB, r_r2c[slq]], writes=[r_r2c[slq]])
                    p.op("dve", lambda E, slq=slq: E.tensor_tensor(
                        r2c[slq][:, :], r2c[slq][:, :], MBc[:, :], ALU.subtract),
                        reads=[r_MB, r_r2c[slq]], writes=[r_r2c[slq]])
                    p.op("sp", lambda E, dc=dc, slq=slq: E.dma_start(
                        out=out_t[dc * 128:(dc + 1) * 128, :], in_=r2c[slq][:, :]),
                        reads=[r_r2c[slq]], writes=[], dma=True)

    p.final_wait("sp")

    with nc.Block() as block:
        p.emit_all(nc, block)
    ctx.close()
    n_ops = sum(len(v) for v in p.ops.values())
    return nc, n_ops



def prepare_inputs(c, L0_attn_x, L0_attn_Wqkv, L0_attn_b_qkv, L0_attn_Wout,
                   L0_attn_b_out, L0_ffn_w1, L0_ffn_w2, L0_ffn_w3):
    S, D = c["S"], c["D"]
    HPC, TPC, KC, MB, DC, QG = c["HPC"], c["TPC"], c["KC"], c["MB"], c["DC"], c["QG"]
    f32 = np.float32

    x2d = np.asarray(L0_attn_x, f32).reshape(S, D)
    x_fm = np.ascontiguousarray(x2d.T)
    Wqkv = np.asarray(L0_attn_Wqkv, f32)
    bqkv = np.asarray(L0_attn_b_qkv, f32)
    Wout = np.ascontiguousarray(np.asarray(L0_attn_Wout, f32))
    bout = np.asarray(L0_attn_b_out, f32)
    w1 = np.asarray(L0_ffn_w1, f32)
    w2 = np.asarray(L0_ffn_w2, f32)
    w3 = np.asarray(L0_ffn_w3, f32)

    inv = (1.0 / (10000.0 ** (np.arange(0, 32, 2) / 32.0))).astype(np.float64)
    t = np.arange(S, dtype=np.float64)
    fr = np.outer(t, inv)
    cos = np.cos(fr).astype(f32)
    sin = np.sin(fr).astype(f32)
    cossin = np.ascontiguousarray(
        np.concatenate([np.tile(cos, (1, HPC)), np.tile(sin, (1, HPC))], 1))

    nd = c["NDIAG"]
    masks = np.zeros((nd, 128, QG), f32)
    for i in range(nd):
        s_idx = np.arange(128)[:, None] + i * 128
        j_idx = np.arange(QG)[None, :]
        masks[i] = np.where(s_idx > j_idx, -10000.0, 0.0)

    ident = np.eye(128, dtype=ml_dtypes.bfloat16)
    ones_row_bf = np.ones((1, 128), ml_dtypes.bfloat16)
    ones_col_bf = np.ones((128, 1), ml_dtypes.bfloat16)
    ones_row_f32 = np.ones((1, 128), f32)
    consts = np.zeros((128, 8), f32)
    consts[:, 0] = -C_EXP
    consts[:, 1] = LN_EPS

    w1t = np.ascontiguousarray(w1.T.reshape(KC, 128, MB, 128).transpose(2, 1, 0, 3))
    w2t = np.ascontiguousarray(w2.T.reshape(KC, 128, MB, 128).transpose(2, 1, 0, 3))
    w3t = np.ascontiguousarray(w3.T.reshape(MB, 128, DC, 128).transpose(2, 1, 0, 3))
    bout_g = np.ascontiguousarray(bout.reshape(DC, 128).T)

    shared = dict(x_fm=x_fm, cossin=cossin, masks=masks, ident=ident,
                  ones_row_bf=ones_row_bf, ones_col_bf=ones_col_bf,
                  ones_row_f32=ones_row_f32, consts=consts, wout=Wout,
                  bout_g=bout_g, w1t=w1t, w2t=w2t, w3t=w3t)

    in_maps = []
    for cid in range(NCORES):
        HW = HPC * 64
        cols = slice(cid * HW, (cid + 1) * HW)
        wq = np.ascontiguousarray(np.concatenate(
            [Wqkv[:, cols], Wqkv[:, D:][:, cols], Wqkv[:, 2 * D:][:, cols]], 1))
        bq = np.concatenate([bqkv[cols], bqkv[D:][cols], bqkv[2 * D:][cols]])
        bq_b = np.ascontiguousarray(
            np.broadcast_to(bq[None, :], (128, 3 * HW)).astype(f32))
        x_own = np.ascontiguousarray(x_fm[:, cid * TPC:(cid + 1) * TPC])
        m = dict(shared)
        m.update(wqkv=wq, bqkv_b=bq_b, x_own=x_own)
        in_maps.append(m)
    return in_maps


def kernel(**inputs):
    c = derive(full_cfg())
    in_maps = prepare_inputs(c, **inputs)
    nc, n_ops = build_nc(c)
    res = run_bass_kernel_spmd(nc, in_maps, core_ids=list(range(NCORES)))
    out_fm = np.concatenate([res.results[i]["out_c"] for i in range(NCORES)], axis=1)
    return np.ascontiguousarray(out_fm.T).reshape(1, c["S"], c["D"]).astype(np.float32)


# version 13
# speedup vs baseline: 1.1655x; 1.1519x over previous
import math
from contextlib import ExitStack

import numpy as np
import ml_dtypes

import concourse.bass as bass
import concourse.mybir as mybir
from concourse.bass_utils import run_bass_kernel_spmd
from concourse.aot_env import allow_ici_all_to_all

F32 = mybir.dt.float32
BF16 = mybir.dt.bfloat16
AF = mybir.ActivationFunctionType
ALU = mybir.AluOpType
AXX = mybir.AxisListType.X

NCORES = 8
C_EXP = 8.0
LN_EPS = 1e-5
Q_EPS = 1e-8


def full_cfg():
    return dict(S=8192, D=2048, NH=32, FH=5632)


def derive(cfg):
    c = dict(cfg)
    S, D, NH, FH = c["S"], c["D"], c["NH"], c["FH"]
    assert D == NH * 64
    c["HPC"] = NH // NCORES
    assert c["HPC"] % 2 == 0 or c["HPC"] == 1
    c["PPC"] = max(1, c["HPC"] // 2)
    c["TPC"] = S // NCORES
    c["KC"] = D // 128
    c["NKT"] = S // 128
    c["QG"] = min(512, S)
    c["NQG"] = S // c["QG"]
    c["NDIAG"] = c["QG"] // 128
    c["MB"] = FH // 128
    c["DC"] = D // 128
    c["TH"] = min(512, c["TPC"])
    c["NTH"] = c["TPC"] // c["TH"]
    c["QKVW"] = 3 * c["HPC"] * 64
    return c



class Res:
    __slots__ = ("w", "r", "name")

    def __init__(self, name=""):
        self.w = {}
        self.r = {}
        self.name = name


class Prog:
    ENGS = ("pe", "act", "dve", "pool", "sp")
    DMA_RING = 6

    def __init__(self):
        self.ops = {e: [] for e in self.ENGS}
        self.cnt = {}
        self.guar = {e: {} for e in self.ENGS}
        self.pending = {e: None for e in self.ENGS}
        self.dma_i = {"sp": 0, "pool": 0}
        for e in self.ENGS:
            self.cnt[e] = 0
        for q in ("sp", "pool"):
            for i in range(self.DMA_RING):
                self.cnt[f"{q}d{i}"] = 0
        self.cnt["cc"] = 0
        self.cnt["cc2"] = 0

    def barrier(self):
        snap = dict(self.cnt)
        for e in self.ENGS:
            cur = self.pending[e]
            if cur is None:
                self.pending[e] = dict(snap)
            else:
                for s, v in snap.items():
                    cur[s] = max(cur.get(s, 0), v)

    def op(self, eng, emit, reads=(), writes=(), inc=1, dma=False, sem=None,
           lag=False):
        if dma:
            idx = self.dma_i[eng]
            self.dma_i[eng] += 1
            sem = f"{eng}d{idx % self.DMA_RING}"
            inc = 16
        elif sem is None:
            sem = eng
        waits = {}
        if dma:
            prev = 16 * (idx // self.DMA_RING)
            if prev > 0:
                waits[sem] = prev
        if self.pending[eng] is not None:
            for s, v in self.pending[eng].items():
                if v > 0:
                    waits[s] = max(waits.get(s, 0), v)
            self.pending[eng] = None
        for r in reads:
            for s, v in r.w.items():
                waits[s] = max(waits.get(s, 0), v)
        for wr in writes:
            for s, v in wr.w.items():
                waits[s] = max(waits.get(s, 0), v)
            for s, v in wr.r.items():
                waits[s] = max(waits.get(s, 0), v)
        g = self.guar[eng]

        def fwd_own(sname, v):
            return (sname == eng or sname.startswith(eng + "d")) and v > self.cnt[sname]

        ww = {s: v for s, v in waits.items()
              if v > g.get(s, 0) and not fwd_own(s, v)}
        for s, v in ww.items():
            g[s] = v
        self.ops[eng].append((ww, emit, inc, sem))
        self.cnt[sem] += inc
        me = (sem, self.cnt[sem] + (1 if lag else 0))
        for wr in writes:
            wr.w = {me[0]: me[1]}
            wr.r = {}
        for r in reads:
            r.r[me[0]] = me[1]

    def final_wait(self, eng):
        snap = dict(self.cnt)

        def emit(e, _snap=snap):
            for s, v in _snap.items():
                if v > 0:
                    e.wait_ge(self._sems[s], v)
        self.ops[eng].append(({}, ("finalwait", emit), 0, eng))

    def emit_all(self, nc, block):
        sems = {}
        ctx = self._ctx
        for s in self.cnt:
            sems[s] = ctx.enter_context(nc.semaphore(f"sem_{s}"))
        self._sems = sems

        def mk(section, eng_name):
            def body(engobj):
                for waits, emit, inc, sem in self.ops[eng_name]:
                    for s, v in sorted(waits.items()):
                        engobj.wait_ge(sems[s], v)
                    if isinstance(emit, tuple) and emit[0] == "finalwait":
                        emit[1](engobj)
                        continue
                    inst = emit(engobj)
                    if inc:
                        inst.then_inc(sems[sem], inc)
            section(body)

        mk(block.tensor, "pe")
        mk(block.scalar, "act")
        mk(block.vector, "dve")
        mk(block.gpsimd, "pool")
        mk(block.sync, "sp")



def build_nc(c, debug=()):
    S, D = c["S"], c["D"]
    HPC, PPC, TPC = c["HPC"], c["PPC"], c["TPC"]
    KC, NKT, QG, NQG = c["KC"], c["NKT"], c["QG"], c["NQG"]
    MB, DC, TH, NTH = c["MB"], c["DC"], c["TH"], c["NTH"]
    QKVW = c["QKVW"]
    HW = HPC * 64

    nc = bass.Bass("TRN2", target_bir_lowering=False)
    p = Prog()

    ap_in = {}
    for name, shape, dt in [
        ("x_fm", [D, S], F32),
        ("x_own", [D, TPC], F32),
        ("wqkv", [D, QKVW], F32),
        ("bqkv_b", [128, QKVW], F32),
        ("cossin", [S, 2 * HPC * 16], F32),
        ("masks", [c["NDIAG"], 128, QG], F32),
        ("ident", [128, 128], BF16),
        ("ones_row_bf", [1, 128], BF16),
        ("ones_col_bf", [128, 1], BF16),
        ("ones_row_f32", [1, 128], F32),
        ("consts", [128, 8], F32),
        ("wout", [D, D], F32),
        ("bout_g", [128, DC], F32),
        ("w1t", [MB, 128, KC, 128], F32),
        ("w2t", [MB, 128, KC, 128], F32),
        ("w3t", [DC, 128, MB, 128], F32),
    ]:
        ap_in[name] = nc.dram_tensor(name, shape, dt, kind="ExternalInput").ap()

    out_t = nc.dram_tensor("out_c", [D, TPC], F32, kind="ExternalOutput").ap()

    a2a_in0 = nc.dram_tensor("a2a_in0", [D // 2, TPC], BF16)
    a2a_out0 = nc.dram_tensor("a2a_out0", [D // 2, TPC], BF16)
    a2a_in1 = nc.dram_tensor("a2a_in1", [D // 2, TPC], BF16)
    a2a_out1 = nc.dram_tensor("a2a_out1", [D // 2, TPC], BF16)
    y_dram = nc.dram_tensor("y_dram", [D, TPC], F32)
    r2_dram = nc.dram_tensor("r2_dram", [D, TPC], F32)

    ctx = ExitStack()
    p._ctx = ctx
    e = ctx.enter_context

    def sb(name, shape, dt):
        return e(nc.sbuf_tensor(name, shape, dt))

    dbg = set(debug)

    def dump(name, src_ap, reads):
        if name not in dbg:
            return
        t = nc.dram_tensor("dbg_" + name, list(src_ap.shape), src_ap.dtype,
                           kind="ExternalOutput").ap()
        p.op("sp", lambda E, t=t, src_ap=src_ap: E.dma_start(out=t, in_=src_ap),
             reads=list(reads), writes=[], dma=True)

    consts_sb = sb("consts_sb", [128, 8], F32)
    ident_sb = sb("ident_sb", [128, 128], BF16)
    onesr_bf = sb("onesr_bf", [1, 128], BF16)
    onesc_bf = sb("onesc_bf", [128, 1], BF16)
    onesr_f = sb("onesr_f", [1, 128], F32)
    bout_sb = sb("bout_sb", [128, DC], F32)
    r_small = Res("small")
    for t, src in [(consts_sb, "consts"), (ident_sb, "ident"),
                   (onesr_bf, "ones_row_bf"), (onesc_bf, "ones_col_bf"),
                   (onesr_f, "ones_row_f32"), (bout_sb, "bout_g")]:
        p.op("sp", lambda E, t=t, src=src: E.dma_start(out=t[:, :], in_=ap_in[src]),
             writes=[r_small], dma=True)

    NG1 = (KC + 7) // 8 * MB
    NG3 = (MB + 7) // 8 * DC
    gacc1 = sb("gacc1", [128, NG1], F32)
    gacc2 = sb("gacc2", [128, NG1], F32)
    gacc3 = sb("gacc3", [128, NG3], F32)
    cvec = sb("cvec", [128, 4], F32)
    ncvec = sb("ncvec", [128, 4], F32)
    r_gacc = [Res("gacc1"), Res("gacc2"), Res("gacc3")]
    r_cvec = Res("cvec")

    r_A = Res("A")
    r_a2ain = [Res("a2ain0"), Res("a2ain1")]
    r_a2a = [Res("a2a0"), Res("a2a1")]

    def emit_cc(half):
        src = (a2a_in0, a2a_in1)[half]
        dst = (a2a_out0, a2a_out1)[half]

        def go(E, src=src, dst=dst):
            with allow_ici_all_to_all():
                return E.collective_compute(
                    "AllToAll", ALU.bypass, replica_groups=[list(range(NCORES))],
                    ins=[src.ap()], outs=[dst.ap()])
        p.op("pool", go, reads=[r_a2ain[half]], writes=[r_a2a[half]], inc=1,
             sem=("cc", "cc2")[half])

    GW = 1024
    GCH = GW // 128
    gamma_jobs = []
    gcols = [0, 0, 0]

    def add_gjobs(src3, acc):
        nt = src3.shape[1]
        for a in range(0, nt, GCH):
            b = min(a + GCH, nt)
            gamma_jobs.append((src3[:, a:b, :].rearrange("p m j -> p (m j)"),
                               (b - a) * 128, acc, gcols[acc]))
            gcols[acc] += 1
    for m in range(MB):
        add_gjobs(ap_in["w1t"][m], 0)
        add_gjobs(ap_in["w2t"][m], 1)
    for dc in range(DC):
        add_gjobs(ap_in["w3t"][dc], 2)

    def make_gamma_ops(gbuf, r_gbuf):
        state = {"i": 0}

        def step():
            i = state["i"]
            if i >= len(gamma_jobs):
                return
            state["i"] += 1
            src, width, acc, col = gamma_jobs[i]
            slot = i % 2
            gt = gbuf[slot]
            rs = r_gbuf[slot]
            accs = [gacc1, gacc2, gacc3][acc]
            p.op("sp", lambda E, gt=gt, src=src, width=width:
                 E.dma_start(out=gt[:, 0:width], in_=src),
                 reads=[], writes=[rs], dma=True)
            p.op("dve", lambda E, gt=gt, accs=accs, col=col, width=width:
                 E.tensor_reduce(accs[:, col:col + 1], gt[:, 0:width], AXX, ALU.add, True),
                 reads=[rs], writes=[r_gacc[acc]])
        return step

    with ExitStack() as s01:
        e1 = s01.enter_context
        A_st = [e1(nc.sbuf_tensor(f"A_st{i}", [64, QG], BF16)) for i in range(2)]
        q_sb = e1(nc.sbuf_tensor("q_sb", [128, HPC, S], BF16))
        k_sb = e1(nc.sbuf_tensor("k_sb", [128, PPC, S], BF16))
        v_sb = e1(nc.sbuf_tensor("v_sb", [128, NKT * HPC * 65 + 63], BF16))
        vv = v_sb[:, 0:NKT * HPC * 65].rearrange("p (t h c) -> p t h c", h=HPC, c=65)
        gbuf = [e1(nc.sbuf_tensor(f"gbuf{i}", [128, GW], F32)) for i in range(2)]
        r_gbuf = [Res("gbuf0"), Res("gbuf1")]
        gamma_step = make_gamma_ops(gbuf, r_gbuf)

        def drain_gamma():
            while True:
                before = len(p.ops["sp"]) + len(p.ops["pool"])
                gamma_step()
                if len(p.ops["sp"]) + len(p.ops["pool"]) == before:
                    break

        _gf_done = [False]
        gtot = e1(nc.sbuf_tensor("gtot", [128, 4], F32))
        grow = e1(nc.sbuf_tensor("grow", [1, 4], F32))
        onescf = e1(nc.sbuf_tensor("onescf", [128, 1], F32))

        def gamma_finalize():
            if _gf_done[0]:
                return
            _gf_done[0] = True
            ps_g = ps_s[0][0:1, 0:4]
            ps_gb = ps_s[0][0:128, 8:12]
            r_gtot, r_grow = Res("gtot"), Res("grow")
            nels = {0: c["FH"] * D, 1: c["FH"] * D, 2: D * c["FH"]}
            for i, (acc, w) in enumerate([(gacc1, gcols[0]), (gacc2, gcols[1]), (gacc3, gcols[2])]):
                p.op("dve", lambda E, i=i, acc=acc, w=w: E.tensor_reduce(
                    gtot[:, i:i + 1], acc[:, 0:w], AXX, ALU.add, False),
                    reads=[r_gacc[i]], writes=[r_gtot])
            p.op("dve", lambda E: E.memset(gtot[:, 3:4], 0.0), writes=[r_gtot])
            r_ocf = Res("ocf")
            p.op("dve", lambda E: E.memset(onescf[:, :], 1.0), writes=[r_ocf])
            p.op("pe", lambda E, ps_g=ps_g: E.matmul(ps_g, onescf[:, :], gtot[:, :],
                                                     start=True, stop=True),
                 reads=[r_gtot, r_ocf], writes=[r_ss[0]])
            p.op("dve", lambda E, ps_g=ps_g: E.tensor_copy(grow[:, :], ps_g),
                 reads=[r_ss[0]], writes=[r_grow])
            for i in range(3):
                scale = 0.5 / nels[i]
                p.op("dve", lambda E, i=i, scale=scale: E.tensor_scalar(
                    grow[:, i:i + 1], grow[:, i:i + 1], scale, 0.5 * Q_EPS,
                    ALU.mult, ALU.add),
                    reads=[r_grow], writes=[r_grow])
            p.op("pe", lambda E, ps_gb=ps_gb: E.matmul(ps_gb, onesr_f[:, :], grow[:, :],
                                                       start=True, stop=True),
                 reads=[r_grow, r_small], writes=[r_ss[0]])
            p.op("dve", lambda E, ps_gb=ps_gb: E.tensor_copy(cvec[:, :], ps_gb),
                 reads=[r_ss[0]], writes=[r_cvec])
            p.op("dve", lambda E: E.tensor_scalar(ncvec[:, :], cvec[:, :], -1.0, None, ALU.mult),
                 reads=[r_cvec], writes=[r_cvec])

        r_q, r_k, r_v = Res("q"), Res("k"), Res("v")
        r_Ast = [Res("Ast0"), Res("Ast1")]

        p.op("dve", lambda E: E.memset(v_sb[:, NKT * HPC * 65:], 0.0), writes=[r_v])
        p.op("dve", lambda E: E.memset(vv[:, :, :, 64:65], 1.0), writes=[r_v])
        for h in range(HPC):
            zlo, zhi = ((64, 128) if h % 2 == 0 else (0, 64)) if HPC > 1 else (64, 128)
            p.op("dve", lambda E, h=h, zlo=zlo, zhi=zhi:
                 E.memset(q_sb[zlo:zhi, h, :], 0.0), writes=[r_q])

        with ExitStack() as s0:
            e0 = s0.enter_context
            wqkv_sb = e0(nc.sbuf_tensor("wqkv_sb", [128, KC, QKVW], BF16))
            bqkv_sb = e0(nc.sbuf_tensor("bqkv_sb", [128, QKVW], F32))
            bksc_sb = e0(nc.sbuf_tensor("bksc_sb", [128, HW], F32))
            xg = [e0(nc.sbuf_tensor(f"xg{i}", [128, KC, 128], BF16)) for i in range(2)]
            cs = [e0(nc.sbuf_tensor(f"cs{i}", [128, 2 * HPC * 16], F32)) for i in range(2)]
            q_scr = [e0(nc.sbuf_tensor(f"q_scr{i}", [128, HPC, 64], F32)) for i in range(2)]
            k_scr = [e0(nc.sbuf_tensor(f"k_scr{i}", [128, HPC, 64], F32)) for i in range(2)]
            rt = [e0(nc.sbuf_tensor(f"rt{i}", [128, HPC, 16], F32)) for i in range(4)]
            qbf = [e0(nc.sbuf_tensor(f"qbf{i}", [128, HPC * 64], BF16)) for i in range(2)]
            kbf = [e0(nc.sbuf_tensor(f"kbf{i}", [128, HPC * 64], BF16)) for i in range(2)]

            ps_qk = [e0(nc.psum_tensor(f"ps_qk{i}", [128, 2 * HW], F32)) for i in range(2)]
            ps_v = [e0(nc.psum_tensor(f"ps_v{i}", [128, HW], F32)) for i in range(2)]
            ps_t = [e0(nc.psum_tensor(f"ps_t{i}", [128, 128], BF16)) for i in range(2)]

            r_wqkv, r_bq, r_bk = Res("wqkv"), Res("bqkv"), Res("bksc")
            r_xg = [Res("xg0"), Res("xg1")]
            r_cs = [Res("cs0"), Res("cs1")]
            r_psqk = [Res("psqk0"), Res("psqk1")]
            r_psv = [Res("psv0"), Res("psv1")]
            r_qscr = [Res("qscr0"), Res("qscr1")]
            r_kscr = [Res("kscr0"), Res("kscr1")]
            r_rt = [Res(f"rt{i}") for i in range(4)]
            r_qbf = [Res("qbf0"), Res("qbf1")]
            r_kbf = [Res("kbf0"), Res("kbf1")]
            r_pst = [Res("pst0"), Res("pst1")]

            p.op("pool", lambda E: E.dma_start(
                out=wqkv_sb[:, :, :],
                in_=ap_in["wqkv"].rearrange("(kc p) n -> p kc n", p=128)),
                writes=[r_wqkv], dma=True)
            p.op("sp", lambda E: E.dma_start(out=bqkv_sb[:, :], in_=ap_in["bqkv_b"]),
                 writes=[r_bq], dma=True)
            p.op("dve", lambda E: E.tensor_scalar(
                bksc_sb[:, :], bqkv_sb[:, HW:2 * HW], 0.125, None, ALU.mult),
                reads=[r_bq], writes=[r_bk])

            def emit_transposes(tt):
                pq = tt % 2
                for pr in range(PPC):
                    for pp, (bf, rbf) in enumerate(((qbf[pq], r_qbf[pq]), (kbf[pq], r_kbf[pq]))):
                        src_ap = bf[:, pr * 128:(pr + 1) * 128]
                        p.op("pe", lambda E, src_ap=src_ap, pp=pp:
                             E.transpose(ps_t[pp][:, :], src_ap, ident_sb[:, :]),
                             reads=[rbf, r_small], writes=[r_pst[pp]])
                        if pp == 1:
                            p.op("dve", lambda E, pp=pp, pr=pr, tt=tt:
                                 E.tensor_copy(k_sb[:, pr, tt * 128:(tt + 1) * 128],
                                               ps_t[pp][:, :]),
                                 reads=[r_pst[pp]], writes=[r_k])
                        else:
                            p.op("dve", lambda E, pp=pp, pr=pr, tt=tt:
                                 E.tensor_copy(q_sb[0:64, 2 * pr, tt * 128:(tt + 1) * 128],
                                               ps_t[pp][0:64, :]),
                                 reads=[r_pst[pp]], writes=[r_q])
                            if HPC > 1:
                                p.op("dve", lambda E, pp=pp, pr=pr, tt=tt:
                                     E.tensor_copy(q_sb[64:128, 2 * pr + 1, tt * 128:(tt + 1) * 128],
                                                   ps_t[pp][64:128, :]),
                                     reads=[r_pst[pp]], writes=[r_q])

            for tt in range(NKT):
                slot = tt % 2
                p.op("pool", lambda E, tt=tt, slot=slot: E.dma_start(
                    out=xg[slot][:, :, :],
                    in_=ap_in["x_fm"][:, tt * 128:(tt + 1) * 128]
                        .rearrange("(kc p) t -> p kc t", p=128)),
                    writes=[r_xg[slot]], dma=True)
                p.op("sp", lambda E, tt=tt, slot=slot: E.dma_start(
                    out=cs[slot][:, :],
                    in_=ap_in["cossin"][tt * 128:(tt + 1) * 128, :]),
                    writes=[r_cs[slot]], dma=True)
                pp0 = tt % 2
                for kc in range(KC):
                    st = kc == 0
                    sp_ = kc == KC - 1
                    for ps, rps, off, w_ in ((ps_qk[pp0], r_psqk[pp0], 0, 2 * HW),
                                             (ps_v[pp0], r_psv[pp0], 2 * HW, HW)):
                        p.op("pe", lambda E, slot=slot, kc=kc, ps=ps, off=off, w_=w_, st=st, sp_=sp_:
                             E.matmul(ps[:, :], xg[slot][:, kc, :],
                                      wqkv_sb[:, kc, off:off + w_], start=st, stop=sp_),
                             reads=[r_xg[slot], r_wqkv],
                             writes=[rps] if (st or sp_) else [],
                             inc=(1 if sp_ else 0), lag=not sp_)
                p.op("dve", lambda E, pp0=pp0: E.tensor_tensor(
                    q_scr[pp0][:, :, :], ps_qk[pp0][:, 0:HW].rearrange("p (h d) -> p h d", d=64),
                    bqkv_sb[:, 0:HW].rearrange("p (h d) -> p h d", d=64), ALU.add),
                    reads=[r_psqk[pp0], r_bq], writes=[r_qscr[pp0]])
                p.op("dve", lambda E, pp0=pp0: E.scalar_tensor_tensor(
                    k_scr[pp0][:, :, :], ps_qk[pp0][:, HW:2 * HW].rearrange("p (h d) -> p h d", d=64),
                    0.125, bksc_sb[:, :].rearrange("p (h d) -> p h d", d=64),
                    ALU.mult, ALU.add),
                    reads=[r_psqk[pp0], r_bk], writes=[r_kscr[pp0]])
                p.op("dve", lambda E, tt=tt, pp0=pp0: E.tensor_tensor(
                    vv[:, tt, :, 0:64], ps_v[pp0][:, :].rearrange("p (h d) -> p h d", d=64),
                    bqkv_sb[:, 2 * HW:].rearrange("p (h d) -> p h d", d=64), ALU.add),
                    reads=[r_psv[pp0], r_bq], writes=[r_v])
                for scr, rscr in ((q_scr[pp0], r_qscr[pp0]), (k_scr[pp0], r_kscr[pp0])):
                    u1 = scr[:, :, 0:16]
                    u2 = scr[:, :, 16:32]
                    co = cs[slot][:, 0:HPC * 16].rearrange("p (h d) -> p h d", d=16)
                    si = cs[slot][:, HPC * 16:].rearrange("p (h d) -> p h d", d=16)
                    p.op("dve", lambda E, u1=u1, co=co: E.tensor_tensor(rt[0][:, :, :], u1, co, ALU.mult),
                         reads=[rscr, r_cs[slot]], writes=[r_rt[0]])
                    p.op("dve", lambda E, u2=u2, si=si: E.tensor_tensor(rt[1][:, :, :], u2, si, ALU.mult),
                         reads=[rscr, r_cs[slot]], writes=[r_rt[1]])
                    p.op("dve", lambda E, u1=u1, si=si: E.tensor_tensor(rt[2][:, :, :], u1, si, ALU.mult),
                         reads=[rscr, r_cs[slot]], writes=[r_rt[2]])
                    p.op("dve", lambda E, u2=u2, co=co: E.tensor_tensor(rt[3][:, :, :], u2, co, ALU.mult),
                         reads=[rscr, r_cs[slot]], writes=[r_rt[3]])
                    p.op("dve", lambda E, u1=u1: E.tensor_tensor(u1, rt[0][:, :, :], rt[1][:, :, :], ALU.subtract),
                         reads=[r_rt[0], r_rt[1]], writes=[rscr])
                    p.op("dve", lambda E, u2=u2: E.tensor_tensor(u2, rt[2][:, :, :], rt[3][:, :, :], ALU.add),
                         reads=[r_rt[2], r_rt[3]], writes=[rscr])
                p.op("dve", lambda E, pp0=pp0: E.tensor_copy(
                    qbf[pp0][:, :], q_scr[pp0][:, :, :].rearrange("p h d -> p (h d)")),
                    reads=[r_qscr[pp0]], writes=[r_qbf[pp0]])
                p.op("dve", lambda E, pp0=pp0: E.tensor_copy(
                    kbf[pp0][:, :], k_scr[pp0][:, :, :].rearrange("p h d -> p (h d)")),
                    reads=[r_kscr[pp0]], writes=[r_kbf[pp0]])
                if tt >= 1:
                    emit_transposes(tt - 1)
                if tt == NKT - 1:
                    emit_transposes(tt)
                gamma_step()
                gamma_step()
            dump("q_sb", q_sb[:, :, :], [r_q])
            dump("k_sb", k_sb[:, :, :], [r_k])
            dump("v_sb", v_sb[:, :], [r_v])
            p.barrier()

        NS = 3
        with ExitStack() as s1:
            e2 = s1.enter_context
            masks_sb = e2(nc.sbuf_tensor("masks_sb", [128, c["NDIAG"], QG], F32))
            p_sb = [e2(nc.sbuf_tensor(f"p_sb{i}", [128, 2 * QG], BF16)) for i in range(NS)]
            rcp_f = e2(nc.sbuf_tensor("rcp_f", [1, QG], F32))
            rcp_bf = e2(nc.sbuf_tensor("rcp_bf", [1, QG], BF16))
            rb_sb = e2(nc.sbuf_tensor("rb_sb", [64, QG], F32))
            ps_s = [e2(nc.psum_tensor(f"ps_s{i}", [128, 2 * QG], F32)) for i in range(NS)]
            ps_o = [e2(nc.psum_tensor(f"ps_o{i}", [128, QG], F32)) for i in range(2)]

            r_ps = [Res(f"p{i}") for i in range(NS)]
            r_ss = [Res(f"ss{i}") for i in range(NS)]
            r_so = [Res("so0"), Res("so1")]
            r_rcp, r_rb = Res("rcp"), Res("rb")
            r_masks = Res("masks")
            p.op("sp", lambda E: E.dma_start(
                out=masks_sb[:, :, :],
                in_=ap_in["masks"].rearrange("n p q -> p n q")),
                writes=[r_masks], dma=True)

            hh = max(1, HPC // 2)
            a2a_v0 = a2a_in0.ap().rearrange("(d hl hd) t -> hl hd d t", hl=hh, hd=64)
            a2a_v1 = a2a_in1.ap().rearrange("(d hl hd) t -> hl hd d t", hl=hh, hd=64)

            LAG = 2
            unit = 0
            for h in range(HPC):
                pr, hp = h // 2, h % 2
                for qg in range(NQG):
                    op_ = unit % 2
                    nkc = (qg + 1) * (QG // 128)
                    np_ = nkc // 2

                    def emit_spair(j, h=h, pr=pr, qg=qg):
                        sl = j % NS
                        for half in range(2):
                            kc = 2 * j + half
                            p.op("pe", lambda E, sl=sl, half=half, kc=kc, h=h, pr=pr, qg=qg:
                                 E.matmul(ps_s[sl][:, half * QG:(half + 1) * QG],
                                          k_sb[:, pr, kc * 128:(kc + 1) * 128],
                                          q_sb[:, h, qg * QG:(qg + 1) * QG],
                                          start=True, stop=True),
                                 reads=[r_k, r_q],
                                 writes=[r_ss[sl]],
                                 inc=(1 if half == 1 else 0), lag=(half == 0))
                        for half in range(2):
                            kc = 2 * j + half
                            diag = kc - qg * (QG // 128)
                            if diag >= 0:
                                p.op("dve", lambda E, sl=sl, half=half, diag=diag:
                                     E.tensor_tensor(
                                         ps_s[sl][:, half * QG:(half + 1) * QG],
                                         ps_s[sl][:, half * QG:(half + 1) * QG],
                                         masks_sb[:, diag, :], ALU.add),
                                     reads=[r_masks, r_ss[sl]], writes=[r_ss[sl]])
                        p.op("act", lambda E, sl=sl: E.activation(
                            p_sb[sl][:, :], ps_s[sl][:, :], AF.Exp,
                            bias=consts_sb[:, 0:1], scale=1.0),
                            reads=[r_ss[sl], r_small], writes=[r_ps[sl]])

                    def emit_avpair(j, h=h, op_=op_, np_=np_):
                        sl = j % NS
                        for half in range(2):
                            kc = 2 * j + half
                            st = kc == 0
                            sp_ = kc == 2 * np_ - 1
                            vb = (kc * HPC + h) * 65
                            p.op("pe", lambda E, sl=sl, half=half, vb=vb, op_=op_, st=st, sp_=sp_:
                                 E.matmul(ps_o[op_][:, :], v_sb[:, vb:vb + 128],
                                          p_sb[sl][:, half * QG:(half + 1) * QG],
                                          start=st, stop=sp_),
                                 reads=[r_v, r_ps[sl]],
                                 writes=[r_so[op_]] if (st or sp_) else [],
                                 inc=(1 if sp_ else 0), lag=not sp_)

                    for j in range(np_ + LAG):
                        if j < np_:
                            emit_spair(j)
                        if j >= LAG:
                            emit_avpair(j - LAG)

                    bsl = (np_ - 1) % NS
                    ast = unit % 2
                    p.op("dve", lambda E, op_=op_: E.reciprocal(rcp_f[:, :], ps_o[op_][64:65, :]),
                         reads=[r_so[op_]], writes=[r_rcp])
                    p.op("dve", lambda E: E.tensor_copy(rcp_bf[:, :], rcp_f[:, :]),
                         reads=[r_rcp], writes=[r_rcp])
                    p.op("pe", lambda E, bsl=bsl: E.matmul(
                        ps_s[bsl][0:64, 0:QG], onesr_bf[:, 0:64], rcp_bf[:, :],
                        start=True, stop=True),
                        reads=[r_rcp, r_small], writes=[r_ss[bsl]])
                    p.op("dve", lambda E, bsl=bsl: E.tensor_copy(rb_sb[:, :], ps_s[bsl][0:64, 0:QG]),
                         reads=[r_ss[bsl]], writes=[r_rb])
                    p.op("dve", lambda E, op_=op_, ast=ast: E.tensor_tensor(
                        A_st[ast][:, :], ps_o[op_][0:64, :], rb_sb[:, :], ALU.mult),
                        reads=[r_so[op_], r_rb], writes=[r_Ast[ast]])
                    hl = 2 * pr + hp if HPC > 1 else 0
                    half = 0 if hl < hh else 1
                    a2a_v = (a2a_v0, a2a_v1)[half]
                    hll = hl - half * hh
                    tok0 = qg * QG
                    if QG <= TPC:
                        dst = a2a_v[hll][:, tok0 // TPC, tok0 % TPC:tok0 % TPC + QG]
                    else:
                        dst = a2a_v[hll][:, tok0 // TPC:(tok0 + QG) // TPC, :]
                    p.op("sp", lambda E, dst=dst, ast=ast: E.dma_start(
                        out=dst, in_=A_st[ast][:, :]),
                        reads=[r_Ast[ast]], writes=[r_a2ain[half]], dma=True)
                    unit += 1
                    if unit == hh * NQG:
                        emit_cc(0)
                    for _ in range(4):
                        gamma_step()
                    if unit == min(48, HPC * NQG - 2):
                        drain_gamma()
                        gamma_finalize()
            drain_gamma()
            gamma_finalize()
            p.barrier()

        dump("cvec", cvec[:, :], [r_cvec])
        dump("gacc1", gacc1[:, :], [r_gacc[0]])
        dump("gacc3", gacc3[:, :], [r_gacc[2]])
        p.barrier()

    emit_cc(1)

    with ExitStack() as s23:
        e23 = s23.enter_context
        T = TPC
        stat = e23(nc.sbuf_tensor("stat", [1, 4 * TPC], F32))
        stat_bf = e23(nc.sbuf_tensor("stat_bf", [1, 2 * TPC], BF16))
        RB = e23(nc.sbuf_tensor("RB", [128, TPC], F32))
        MBc = e23(nc.sbuf_tensor("MBc", [128, TPC], F32))
        ysc = e23(nc.sbuf_tensor("ysc", [128, TPC], F32))
        r_stat, r_statbf = Res("stat"), Res("statbf")
        r_RB, r_MB, r_ysc = Res("RB"), Res("MB"), Res("ysc")

        def ln_finalize(ps1, ps2, rp1, rp2, nfeat, ps_bc, r_psbc):
            for th in range(NTH):
                p.op("dve", lambda E, th=th, ps1=ps1: E.tensor_scalar(
                    stat[:, th * TH:(th + 1) * TH], ps1[th][:, :],
                    1.0 / nfeat, None, ALU.mult),
                    reads=[rp1[th]], writes=[r_stat])
                p.op("dve", lambda E, th=th, ps2=ps2: E.tensor_scalar(
                    stat[:, T + th * TH:T + (th + 1) * TH], ps2[th][:, :],
                    1.0 / nfeat, None, ALU.mult),
                    reads=[rp2[th]], writes=[r_stat])
            p.op("dve", lambda E: E.tensor_tensor(
                stat[:, 3 * T:4 * T], stat[:, 0:T], stat[:, 0:T], ALU.mult),
                reads=[r_stat], writes=[r_stat])
            p.op("dve", lambda E: E.tensor_tensor(
                stat[:, T:2 * T], stat[:, T:2 * T], stat[:, 3 * T:4 * T], ALU.subtract),
                reads=[r_stat], writes=[r_stat])
            p.op("act", lambda E: E.activation(
                stat[:, 2 * T:3 * T], stat[:, T:2 * T], AF.Ln,
                bias=consts_sb[0:1, 1:2], scale=1.0),
                reads=[r_stat, r_small], writes=[r_stat])
            p.op("act", lambda E: E.activation(
                stat[:, 2 * T:3 * T], stat[:, 2 * T:3 * T], AF.Exp,
                bias=0.0, scale=-0.5),
                reads=[r_stat], writes=[r_stat])
            p.op("dve", lambda E: E.tensor_tensor(
                stat[:, 3 * T:4 * T], stat[:, 0:T], stat[:, 2 * T:3 * T], ALU.mult),
                reads=[r_stat], writes=[r_stat])
            p.op("dve", lambda E: E.tensor_copy(stat_bf[:, 0:T], stat[:, 2 * T:3 * T]),
                 reads=[r_stat], writes=[r_statbf])
            p.op("dve", lambda E: E.tensor_copy(stat_bf[:, T:2 * T], stat[:, 3 * T:4 * T]),
                 reads=[r_stat], writes=[r_statbf])
            for dst, rdst, off in ((RB, r_RB, 0), (MBc, r_MB, TPC)):
                for th in range(NTH):
                    p.op("pe", lambda E, off=off, th=th, ps_bc=ps_bc: E.matmul(
                        ps_bc[:, :], onesr_bf[:, :],
                        stat_bf[:, off + th * TH:off + (th + 1) * TH],
                        start=True, stop=True),
                        reads=[r_statbf, r_small], writes=[r_psbc])
                    p.op("dve", lambda E, dst=dst, th=th, ps_bc=ps_bc: E.tensor_copy(
                        dst[:, th * TH:(th + 1) * TH], ps_bc[:, :]),
                        reads=[r_psbc], writes=[rdst])

        sR = ExitStack()
        ybf_sb = sR.enter_context(nc.sbuf_tensor("ybf_sb", [128, KC, TPC], BF16, side="right"))
        r_ybf = Res("ybf")
        sR2 = ExitStack()
        xo_sb = [sR2.enter_context(nc.sbuf_tensor(f"xo{i}", [128, TPC], F32, side="right")) for i in range(2)]
        wo_sb = [sR2.enter_context(nc.sbuf_tensor(f"wo{i}", [128, KC, 128], BF16, side="right")) for i in range(2)]
        r_xo = [Res("xo0"), Res("xo1")]
        r_wo = [Res("wo0"), Res("wo1")]

        def wout_dma(dc, slw):
            p.op("pool", lambda E, dc=dc, slw=slw: E.dma_start(
                out=wo_sb[slw][:, :, :],
                in_=ap_in["wout"][:, dc * 128:(dc + 1) * 128]
                    .rearrange("(kc p) j -> p kc j", p=128)),
                writes=[r_wo[slw]], dma=True)
            p.op("sp", lambda E, dc=dc, slw=slw: E.dma_start(
                out=xo_sb[slw][:, :], in_=ap_in["x_own"][dc * 128:(dc + 1) * 128, :]),
                writes=[r_xo[slw]], dma=True)

        wout_dma(0, 0)

        with ExitStack() as s2:
            e3 = s2.enter_context
            af_sb = e3(nc.sbuf_tensor("af_sb", [128, KC, TPC], BF16))
            r1_sb = e3(nc.sbuf_tensor("r1_sb", [128, KC, TPC], F32))
            r1b = e3(nc.sbuf_tensor("r1b", [128, TPC], BF16))
            r1sq = e3(nc.sbuf_tensor("r1sq", [128, TPC], BF16))

            ps_w = [e3(nc.psum_tensor(f"ps_w{i}", [128, TH], F32)) for i in range(2)]
            ps_s1 = [e3(nc.psum_tensor(f"ps_s1{i}", [1, TH], F32)) for i in range(NTH)]
            ps_s2 = [e3(nc.psum_tensor(f"ps_s2{i}", [1, TH], F32)) for i in range(NTH)]
            ps_bc = e3(nc.psum_tensor("ps_bc", [128, TH], F32))

            r_af = Res("af")
            r_r1 = Res("r1")
            r_psw = [Res("psw0"), Res("psw1")]
            r_ps1 = [Res(f"ps1{i}") for i in range(NTH)]
            r_ps2 = [Res(f"ps2{i}") for i in range(NTH)]
            r_r1b, r_r1sq = Res("r1b"), Res("r1sq")
            r_psbc = Res("psbc")

            r_afh = [Res("af0"), Res("af1")]
            if hh * 64 == 128:
                p.op("pool", lambda E: E.dma_start(
                    out=af_sb[:, 0:DC:2, :],
                    in_=a2a_out0.ap().rearrange("(r p) t -> p r t", p=128)),
                    reads=[r_a2a[0]], writes=[r_afh[0]], dma=True)
                p.op("pool", lambda E: E.dma_start(
                    out=af_sb[:, 1:DC:2, :],
                    in_=a2a_out1.ap().rearrange("(r p) t -> p r t", p=128)),
                    reads=[r_a2a[1]], writes=[r_afh[1]], dma=True)
            else:
                p.op("pool", lambda E: E.dma_start(
                    out=af_sb[0:64, :, :],
                    in_=a2a_out0.ap().rearrange("(r p) t -> p r t", p=64)),
                    reads=[r_a2a[0]], writes=[r_afh[0]], dma=True)
                p.op("pool", lambda E: E.dma_start(
                    out=af_sb[64:128, :, :],
                    in_=a2a_out1.ap().rearrange("(r p) t -> p r t", p=64)),
                    reads=[r_a2a[1]], writes=[r_afh[1]], dma=True)

            dc_order = list(range(0, DC, 2)) + list(range(1, DC, 2))
            for dci, dc in enumerate(dc_order):
                slw = dci % 2
                if dci + 1 < DC:
                    wout_dma(dc_order[dci + 1], (dci + 1) % 2)
                kc_order = list(range(0, KC, 2)) + list(range(1, KC, 2))
                for th in range(NTH):
                    pw = (dci * NTH + th) % 2
                    for kci, kc in enumerate(kc_order):
                        stw = kci == 0
                        spw = kci == KC - 1
                        p.op("pe", lambda E, slw=slw, kc=kc, th=th, pw=pw,
                             st=stw, sp_=spw:
                             E.matmul(ps_w[pw][:, :], wo_sb[slw][:, kc, :],
                                      af_sb[:, kc, th * TH:(th + 1) * TH], start=st, stop=sp_),
                             reads=[r_wo[slw], r_afh[kc % 2]] if hh * 64 == 128
                                   else [r_wo[slw], r_afh[0], r_afh[1]],
                             writes=[r_psw[pw]] if (stw or spw) else [],
                             inc=(1 if spw else 0), lag=not spw)
                    p.op("dve", lambda E, dc=dc, th=th, pw=pw, slw=slw:
                         E.scalar_tensor_tensor(
                             r1_sb[:, dc, th * TH:(th + 1) * TH], ps_w[pw][:, :],
                             bout_sb[:, dc:dc + 1], xo_sb[slw][:, th * TH:(th + 1) * TH],
                             ALU.add, ALU.add),
                         reads=[r_psw[pw], r_xo[slw], r_small], writes=[r_r1])
                p.op("dve", lambda E, dc=dc: E.tensor_copy(r1b[:, :], r1_sb[:, dc, :]),
                     reads=[r_r1], writes=[r_r1b])
                p.op("act", lambda E: E.activation(r1sq[:, :], r1b[:, :], AF.Square),
                     reads=[r_r1b], writes=[r_r1sq])
                for th in range(NTH):
                    sp2 = dci == DC - 1
                    st2 = dci == 0
                    p.op("pe", lambda E, th=th, st=st2, sp_=sp2:
                         E.matmul(ps_s1[th][:, :], onesc_bf[:, :],
                                  r1b[:, th * TH:(th + 1) * TH], start=st, stop=sp_),
                         reads=[r_r1b, r_small], writes=[r_ps1[th]])
                    p.op("pe", lambda E, th=th, st=st2, sp_=sp2:
                         E.matmul(ps_s2[th][:, :], onesc_bf[:, :],
                                  r1sq[:, th * TH:(th + 1) * TH], start=st, stop=sp_),
                         reads=[r_r1sq, r_small], writes=[r_ps2[th]])

            ln_finalize(ps_s1, ps_s2, r_ps1, r_ps2, D, ps_bc, r_psbc)

            dump("r1", r1_sb[:, :, :], [r_r1])
            dump("af", af_sb[:, :, :], [r_af])
            for dc in range(DC):
                p.op("dve", lambda E, dc=dc: E.tensor_tensor(
                    ysc[:, :], r1_sb[:, dc, :], RB[:, :], ALU.mult),
                    reads=[r_r1, r_RB], writes=[r_ysc])
                p.op("dve", lambda E, dc=dc: E.tensor_tensor(
                    ysc[:, :], ysc[:, :], MBc[:, :], ALU.subtract),
                    reads=[r_MB, r_ysc], writes=[r_ysc])
                p.op("dve", lambda E, dc=dc: E.tensor_copy(ybf_sb[:, dc, :], ysc[:, :]),
                     reads=[r_ysc], writes=[r_ybf])
                p.op("sp", lambda E, dc=dc: E.dma_start(
                    out=y_dram.ap()[dc * 128:(dc + 1) * 128, :], in_=ysc[:, :]),
                    reads=[r_ysc], writes=[], dma=True)
            dump("ybf", ybf_sb[:, :, :], [r_ybf])
            p.barrier()

        sR2.close()

        with ExitStack() as s3:
            e4 = s3.enter_context
            H_sb = e4(nc.sbuf_tensor("H_sb", [128, MB, TPC], BF16))
            r_H = Res("H")
            with ExitStack() as s3a:
                e5 = s3a.enter_context
                wb = [e5(nc.sbuf_tensor(f"wb{i}", [128, KC * 128], F32)) for i in range(4)]
                qw = [e5(nc.sbuf_tensor(f"qw{i}", [128, KC, 128], BF16)) for i in range(4)]
                sig = e5(nc.sbuf_tensor("sig", [128, TH], BF16))
                ps_x1 = [e5(nc.psum_tensor(f"ps_x1{i}", [128, TH], F32)) for i in range(2)]
                ps_x2 = [e5(nc.psum_tensor(f"ps_x2{i}", [128, TH], F32)) for i in range(2)]
                r_wb = [Res(f"wb{i}") for i in range(4)]
                r_qw = [Res(f"qw{i}") for i in range(4)]
                r_sig = Res("sig")
                r_px1 = [Res("px10"), Res("px11")]
                r_px2 = [Res("px20"), Res("px21")]

                ts2 = e5(nc.sbuf_tensor("ts2", [128, KC * 128], BF16))
                r_ts2 = Res("ts2")

                def quantize(src_sb, r_src, width, dst, r_dst, ci):
                    p.op("act", lambda E: E.activation(
                        dst, src_sb[:, 0:width], AF.Sign,
                        bias=ncvec[:, ci:ci + 1], scale=1.0),
                        reads=[r_src, r_cvec], writes=[r_dst])
                    p.op("act", lambda E: E.activation(
                        ts2[:, 0:width], src_sb[:, 0:width], AF.Sign,
                        bias=cvec[:, ci:ci + 1], scale=1.0),
                        reads=[r_src, r_cvec], writes=[r_ts2])
                    p.op("dve", lambda E: E.tensor_tensor(
                        dst, dst, ts2[:, 0:width], ALU.add),
                        reads=[r_ts2, r_dst], writes=[r_dst])

                def w12_dma(m):
                    slw = m % 2
                    for wi, (wt, q_) in enumerate((("w1t", "sp"), ("w2t", "pool"))):
                        qi = slw * 2 + wi
                        p.op(q_, lambda E, wt=wt, m=m, qi=qi: E.dma_start(
                            out=wb[qi][:, :], in_=ap_in[wt][m].rearrange("p kc j -> p (kc j)")),
                            writes=[r_wb[qi]], dma=True)

                w12_dma(0)
                for m in range(MB):
                    slw = m % 2
                    if m + 1 < MB:
                        w12_dma(m + 1)
                    for wi, ci in ((0, 0), (1, 1)):
                        qi = slw * 2 + wi
                        quantize(wb[qi], r_wb[qi], KC * 128,
                                 qw[qi][:, :, :].rearrange("p kc j -> p (kc j)"), r_qw[qi], ci)
                        psx = (ps_x1, ps_x2)[wi]
                        rpx = (r_px1, r_px2)[wi]
                        for th in range(NTH):
                            px = th % 2
                            for kc in range(KC):
                                stf = kc == 0
                                spf = kc == KC - 1
                                p.op("pe", lambda E, qi=qi, kc=kc, th=th, px=px, psx=psx,
                                     st=stf, sp_=spf:
                                     E.matmul(psx[px][:, :], qw[qi][:, kc, :],
                                              ybf_sb[:, kc, th * TH:(th + 1) * TH],
                                              start=st, stop=sp_),
                                     reads=[r_qw[qi], r_ybf],
                                     writes=[rpx[px]] if (stf or spf) else [],
                                     inc=(1 if spf else 0), lag=not spf)
                    for th in range(NTH):
                        px = th % 2
                        p.op("act", lambda E, px=px: E.activation(
                            sig[:, :], ps_x1[px][:, :], AF.Sigmoid, scale=0.5),
                            reads=[r_px1[px]], writes=[r_sig])
                        p.op("dve", lambda E, m=m, th=th, px=px: E.tensor_tensor(
                            H_sb[:, m, th * TH:(th + 1) * TH], ps_x2[px][:, :],
                            sig[:, :], ALU.mult),
                            reads=[r_px2[px], r_sig], writes=[r_H])
                dump("H", H_sb[:, :, :], [r_H])
                p.barrier()
            sR.close()

            with ExitStack() as s3b:
                e6 = s3b.enter_context
                WH3 = (MB + 1) // 2
                w3b = [e6(nc.sbuf_tensor(f"w3b{i}", [128, WH3 * 128], F32)) for i in range(4)]
                q3 = [e6(nc.sbuf_tensor(f"q3{i}", [128, MB, 128], BF16)) for i in range(2)]
                yst = [e6(nc.sbuf_tensor("yst0", [128, TPC], F32))]
                ts23 = e6(nc.sbuf_tensor("ts23", [128, WH3 * 128], BF16))
                r_ts23 = Res("ts23")
                r2c = [e6(nc.sbuf_tensor(f"r2c{i}", [128, TPC], F32)) for i in range(2)]
                r2b = e6(nc.sbuf_tensor("r2b", [128, TPC], BF16))
                r2sq = r2b
                ps_r = [e6(nc.psum_tensor(f"ps_r{i}", [128, TH], F32)) for i in range(2)]
                ps_t1 = [e6(nc.psum_tensor(f"ps_t1{i}", [1, TH], F32)) for i in range(NTH)]
                ps_t2 = [e6(nc.psum_tensor(f"ps_t2{i}", [1, TH], F32)) for i in range(NTH)]
                r_w3b = [Res(f"w3b{i}") for i in range(4)]
                r_q3 = [Res("q30"), Res("q31")]
                r_yst = [Res("yst0")]
                r_r2c = [Res("r2c0"), Res("r2c1")]
                r_r2b, r_r2sq = Res("r2b"), Res("r2sq")
                r_pr = [Res("pr0"), Res("pr1")]
                r_pt1 = [Res(f"pt1{i}") for i in range(NTH)]
                r_pt2 = [Res(f"pt2{i}") for i in range(NTH)]
                r_r2dram = Res("r2dram")

                mh3 = MB // 2

                def w3_dma(dc):
                    slq = dc % 2
                    for hi, (m0, m1, q_) in enumerate(((0, mh3, "sp"), (mh3, MB, "pool"))):
                        slw = slq * 2 + hi
                        p.op(q_, lambda E, dc=dc, slw=slw, m0=m0, m1=m1: E.dma_start(
                            out=w3b[slw][:, 0:(m1 - m0) * 128],
                            in_=ap_in["w3t"][dc][:, m0:m1, :].rearrange("p m j -> p (m j)")),
                            writes=[r_w3b[slw]], dma=True)

                w3_dma(0)
                for dc in range(DC):
                    slq = dc % 2
                    if dc + 1 < DC:
                        w3_dma(dc + 1)
                    for hi, (m0, m1) in enumerate(((0, mh3), (mh3, MB))):
                        slw = slq * 2 + hi
                        wdt = (m1 - m0) * 128
                        qdst = q3[slq][:, m0:m1, :].rearrange("p m j -> p (m j)")
                        p.op("act", lambda E, slw=slw, wdt=wdt, qdst=qdst: E.activation(
                            qdst, w3b[slw][:, 0:wdt], AF.Sign,
                            bias=ncvec[:, 2:3], scale=1.0),
                            reads=[r_w3b[slw], r_cvec], writes=[r_q3[slq]])
                        p.op("act", lambda E, slw=slw, wdt=wdt: E.activation(
                            ts23[:, 0:wdt], w3b[slw][:, 0:wdt], AF.Sign,
                            bias=cvec[:, 2:3], scale=1.0),
                            reads=[r_w3b[slw], r_cvec], writes=[r_ts23])
                        p.op("dve", lambda E, wdt=wdt, qdst=qdst: E.tensor_tensor(
                            qdst, qdst, ts23[:, 0:wdt], ALU.add),
                            reads=[r_ts23, r_q3[slq]], writes=[r_q3[slq]])
                    p.op("sp", lambda E, dc=dc: E.dma_start(
                        out=yst[0][:, :], in_=y_dram.ap()[dc * 128:(dc + 1) * 128, :]),
                        reads=[], writes=[r_yst[0]], dma=True)
                    for th in range(NTH):
                        pr_ = (dc * NTH + th) % 2
                        for m in range(MB):
                            st3 = m == 0
                            sp3 = m == MB - 1
                            p.op("pe", lambda E, slq=slq, m=m, th=th, pr_=pr_,
                                 st=st3, sp_=sp3:
                                 E.matmul(ps_r[pr_][:, :], q3[slq][:, m, :],
                                          H_sb[:, m, th * TH:(th + 1) * TH],
                                          start=st, stop=sp_),
                                 reads=[r_q3[slq], r_H],
                                 writes=[r_pr[pr_]] if (st3 or sp3) else [],
                                 inc=(1 if sp3 else 0), lag=not sp3)
                        p.op("dve", lambda E, slq=slq, th=th, pr_=pr_: E.scalar_tensor_tensor(
                            r2c[slq][:, th * TH:(th + 1) * TH], ps_r[pr_][:, :],
                            0.25, yst[0][:, th * TH:(th + 1) * TH], ALU.mult, ALU.add),
                            reads=[r_pr[pr_], r_yst[0]], writes=[r_r2c[slq]])
                    p.op("dve", lambda E, slq=slq: E.tensor_copy(r2b[:, :], r2c[slq][:, :]),
                         reads=[r_r2c[slq]], writes=[r_r2b])
                    for th in range(NTH):
                        sp2 = dc == DC - 1
                        st2 = dc == 0
                        p.op("pe", lambda E, th=th, dc=dc, st=st2, sp_=sp2:
                             E.matmul(ps_t1[th][:, :], onesc_bf[:, :],
                                      r2b[:, th * TH:(th + 1) * TH], start=st, stop=sp_),
                             reads=[r_r2b, r_small], writes=[r_pt1[th]])
                    p.op("act", lambda E: E.activation(r2sq[:, :], r2b[:, :], AF.Square),
                         reads=[r_r2b], writes=[r_r2b])
                    for th in range(NTH):
                        sp2 = dc == DC - 1
                        st2 = dc == 0
                        p.op("pe", lambda E, th=th, dc=dc, st=st2, sp_=sp2:
                             E.matmul(ps_t2[th][:, :], onesc_bf[:, :],
                                      r2sq[:, th * TH:(th + 1) * TH], start=st, stop=sp_),
                             reads=[r_r2b, r_small], writes=[r_pt2[th]])
                    p.op("sp", lambda E, dc=dc, slq=slq: E.dma_start(
                        out=r2_dram.ap()[dc * 128:(dc + 1) * 128, :], in_=r2c[slq][:, :]),
                        reads=[r_r2c[slq]], writes=[r_r2dram], dma=True)

                ps_bc2 = e6(nc.psum_tensor("ps_bc2", [128, TH], F32))
                r_psbc2 = Res("psbc2")
                ln_finalize(ps_t1, ps_t2, r_pt1, r_pt2, D, ps_bc2, r_psbc2)

                for dc in range(DC):
                    slq = dc % 2
                    p.op("sp", lambda E, dc=dc, slq=slq: E.dma_start(
                        out=r2c[slq][:, :], in_=r2_dram.ap()[dc * 128:(dc + 1) * 128, :]),
                        reads=[r_r2dram], writes=[r_r2c[slq]], dma=True)
                    p.op("dve", lambda E, slq=slq: E.tensor_tensor(
                        r2c[slq][:, :], r2c[slq][:, :], RB[:, :], ALU.mult),
                        reads=[r_RB, r_r2c[slq]], writes=[r_r2c[slq]])
                    p.op("dve", lambda E, slq=slq: E.tensor_tensor(
                        r2c[slq][:, :], r2c[slq][:, :], MBc[:, :], ALU.subtract),
                        reads=[r_MB, r_r2c[slq]], writes=[r_r2c[slq]])
                    p.op("sp", lambda E, dc=dc, slq=slq: E.dma_start(
                        out=out_t[dc * 128:(dc + 1) * 128, :], in_=r2c[slq][:, :]),
                        reads=[r_r2c[slq]], writes=[], dma=True)

    p.final_wait("sp")

    with nc.Block() as block:
        p.emit_all(nc, block)
    ctx.close()
    n_ops = sum(len(v) for v in p.ops.values())
    return nc, n_ops



def prepare_inputs(c, L0_attn_x, L0_attn_Wqkv, L0_attn_b_qkv, L0_attn_Wout,
                   L0_attn_b_out, L0_ffn_w1, L0_ffn_w2, L0_ffn_w3):
    S, D = c["S"], c["D"]
    HPC, TPC, KC, MB, DC, QG = c["HPC"], c["TPC"], c["KC"], c["MB"], c["DC"], c["QG"]
    f32 = np.float32

    x2d = np.asarray(L0_attn_x, f32).reshape(S, D)
    x_fm = np.ascontiguousarray(x2d.T)
    Wqkv = np.asarray(L0_attn_Wqkv, f32)
    bqkv = np.asarray(L0_attn_b_qkv, f32)
    Wout = np.ascontiguousarray(np.asarray(L0_attn_Wout, f32))
    bout = np.asarray(L0_attn_b_out, f32)
    w1 = np.asarray(L0_ffn_w1, f32)
    w2 = np.asarray(L0_ffn_w2, f32)
    w3 = np.asarray(L0_ffn_w3, f32)

    inv = (1.0 / (10000.0 ** (np.arange(0, 32, 2) / 32.0))).astype(np.float64)
    t = np.arange(S, dtype=np.float64)
    fr = np.outer(t, inv)
    cos = np.cos(fr).astype(f32)
    sin = np.sin(fr).astype(f32)
    cossin = np.ascontiguousarray(
        np.concatenate([np.tile(cos, (1, HPC)), np.tile(sin, (1, HPC))], 1))

    nd = c["NDIAG"]
    masks = np.zeros((nd, 128, QG), f32)
    for i in range(nd):
        s_idx = np.arange(128)[:, None] + i * 128
        j_idx = np.arange(QG)[None, :]
        masks[i] = np.where(s_idx > j_idx, -10000.0, 0.0)

    ident = np.eye(128, dtype=ml_dtypes.bfloat16)
    ones_row_bf = np.ones((1, 128), ml_dtypes.bfloat16)
    ones_col_bf = np.ones((128, 1), ml_dtypes.bfloat16)
    ones_row_f32 = np.ones((1, 128), f32)
    consts = np.zeros((128, 8), f32)
    consts[:, 0] = -C_EXP
    consts[:, 1] = LN_EPS

    w1t = np.ascontiguousarray(w1.T.reshape(KC, 128, MB, 128).transpose(2, 1, 0, 3))
    w2t = np.ascontiguousarray(w2.T.reshape(KC, 128, MB, 128).transpose(2, 1, 0, 3))
    w3t = np.ascontiguousarray(w3.T.reshape(MB, 128, DC, 128).transpose(2, 1, 0, 3))
    bout_g = np.ascontiguousarray(bout.reshape(DC, 128).T)

    shared = dict(x_fm=x_fm, cossin=cossin, masks=masks, ident=ident,
                  ones_row_bf=ones_row_bf, ones_col_bf=ones_col_bf,
                  ones_row_f32=ones_row_f32, consts=consts, wout=Wout,
                  bout_g=bout_g, w1t=w1t, w2t=w2t, w3t=w3t)

    in_maps = []
    for cid in range(NCORES):
        HW = HPC * 64
        cols = slice(cid * HW, (cid + 1) * HW)
        wq = np.ascontiguousarray(np.concatenate(
            [Wqkv[:, cols], Wqkv[:, D:][:, cols], Wqkv[:, 2 * D:][:, cols]], 1))
        bq = np.concatenate([bqkv[cols], bqkv[D:][cols], bqkv[2 * D:][cols]])
        bq_b = np.ascontiguousarray(
            np.broadcast_to(bq[None, :], (128, 3 * HW)).astype(f32))
        x_own = np.ascontiguousarray(x_fm[:, cid * TPC:(cid + 1) * TPC])
        m = dict(shared)
        m.update(wqkv=wq, bqkv_b=bq_b, x_own=x_own)
        in_maps.append(m)
    return in_maps


def kernel(**inputs):
    c = derive(full_cfg())
    in_maps = prepare_inputs(c, **inputs)
    nc, n_ops = build_nc(c)
    res = run_bass_kernel_spmd(nc, in_maps, core_ids=list(range(NCORES)))
    out_fm = np.concatenate([res.results[i]["out_c"] for i in range(NCORES)], axis=1)
    return np.ascontiguousarray(out_fm.T).reshape(1, c["S"], c["D"]).astype(np.float32)


# version 14
# speedup vs baseline: 1.1678x; 1.0019x over previous
import math
from contextlib import ExitStack

import numpy as np
import ml_dtypes

import concourse.bass as bass
import concourse.mybir as mybir
from concourse.bass_utils import run_bass_kernel_spmd
from concourse.aot_env import allow_ici_all_to_all

F32 = mybir.dt.float32
BF16 = mybir.dt.bfloat16
AF = mybir.ActivationFunctionType
ALU = mybir.AluOpType
AXX = mybir.AxisListType.X

NCORES = 8
C_EXP = 8.0
LN_EPS = 1e-5
Q_EPS = 1e-8


def full_cfg():
    return dict(S=8192, D=2048, NH=32, FH=5632)


def derive(cfg):
    c = dict(cfg)
    S, D, NH, FH = c["S"], c["D"], c["NH"], c["FH"]
    assert D == NH * 64
    c["HPC"] = NH // NCORES
    assert c["HPC"] % 2 == 0 or c["HPC"] == 1
    c["PPC"] = max(1, c["HPC"] // 2)
    c["TPC"] = S // NCORES
    c["KC"] = D // 128
    c["NKT"] = S // 128
    c["QG"] = min(512, S)
    c["NQG"] = S // c["QG"]
    c["NDIAG"] = c["QG"] // 128
    c["MB"] = FH // 128
    c["DC"] = D // 128
    c["TH"] = min(512, c["TPC"])
    c["NTH"] = c["TPC"] // c["TH"]
    c["QKVW"] = 3 * c["HPC"] * 64
    return c



class Res:
    __slots__ = ("w", "r", "name")

    def __init__(self, name=""):
        self.w = {}
        self.r = {}
        self.name = name


class Prog:
    ENGS = ("pe", "act", "dve", "pool", "sp")
    DMA_RING = 6

    def __init__(self):
        self.ops = {e: [] for e in self.ENGS}
        self.cnt = {}
        self.guar = {e: {} for e in self.ENGS}
        self.pending = {e: None for e in self.ENGS}
        self.dma_i = {"sp": 0, "pool": 0, "act": 0}
        for e in self.ENGS:
            self.cnt[e] = 0
        for q in ("sp", "pool", "act"):
            for i in range(self.DMA_RING):
                self.cnt[f"{q}d{i}"] = 0
        self.cnt["cc"] = 0

    def barrier(self):
        snap = dict(self.cnt)
        for e in self.ENGS:
            cur = self.pending[e]
            if cur is None:
                self.pending[e] = dict(snap)
            else:
                for s, v in snap.items():
                    cur[s] = max(cur.get(s, 0), v)

    def op(self, eng, emit, reads=(), writes=(), inc=1, dma=False, sem=None,
           lag=False):
        if dma:
            idx = self.dma_i[eng]
            self.dma_i[eng] += 1
            sem = f"{eng}d{idx % self.DMA_RING}"
            inc = 16
        elif sem is None:
            sem = eng
        waits = {}
        if dma:
            prev = 16 * (idx // self.DMA_RING)
            if prev > 0:
                waits[sem] = prev
        if self.pending[eng] is not None:
            for s, v in self.pending[eng].items():
                if v > 0:
                    waits[s] = max(waits.get(s, 0), v)
            self.pending[eng] = None
        for r in reads:
            for s, v in r.w.items():
                waits[s] = max(waits.get(s, 0), v)
        for wr in writes:
            for s, v in wr.w.items():
                waits[s] = max(waits.get(s, 0), v)
            for s, v in wr.r.items():
                waits[s] = max(waits.get(s, 0), v)
        g = self.guar[eng]

        def fwd_own(sname, v):
            return (sname == eng or sname.startswith(eng + "d")) and v > self.cnt[sname]

        ww = {s: v for s, v in waits.items()
              if v > g.get(s, 0) and not fwd_own(s, v)}
        for s, v in ww.items():
            g[s] = v
        self.ops[eng].append((ww, emit, inc, sem))
        self.cnt[sem] += inc
        me = (sem, self.cnt[sem] + (1 if lag else 0))
        for wr in writes:
            wr.w = {me[0]: me[1]}
            wr.r = {}
        for r in reads:
            r.r[me[0]] = me[1]

    def final_wait(self, eng):
        snap = dict(self.cnt)

        def emit(e, _snap=snap):
            for s, v in _snap.items():
                if v > 0:
                    e.wait_ge(self._sems[s], v)
        self.ops[eng].append(({}, ("finalwait", emit), 0, eng))

    def emit_all(self, nc, block):
        sems = {}
        ctx = self._ctx
        for s in self.cnt:
            sems[s] = ctx.enter_context(nc.semaphore(f"sem_{s}"))
        self._sems = sems

        def mk(section, eng_name):
            def body(engobj):
                for waits, emit, inc, sem in self.ops[eng_name]:
                    for s, v in sorted(waits.items()):
                        engobj.wait_ge(sems[s], v)
                    if isinstance(emit, tuple) and emit[0] == "finalwait":
                        emit[1](engobj)
                        continue
                    inst = emit(engobj)
                    if inc:
                        inst.then_inc(sems[sem], inc)
            section(body)

        mk(block.tensor, "pe")
        mk(block.scalar, "act")
        mk(block.vector, "dve")
        mk(block.gpsimd, "pool")
        mk(block.sync, "sp")



def build_nc(c, debug=()):
    S, D = c["S"], c["D"]
    HPC, PPC, TPC = c["HPC"], c["PPC"], c["TPC"]
    KC, NKT, QG, NQG = c["KC"], c["NKT"], c["QG"], c["NQG"]
    MB, DC, TH, NTH = c["MB"], c["DC"], c["TH"], c["NTH"]
    QKVW = c["QKVW"]
    HW = HPC * 64

    nc = bass.Bass("TRN2", target_bir_lowering=False)
    p = Prog()

    ap_in = {}
    for name, shape, dt in [
        ("x_fm", [D, S], F32),
        ("x_own", [D, TPC], F32),
        ("wqkv", [D, QKVW], F32),
        ("bqkv_b", [128, QKVW], F32),
        ("cossin", [S, 2 * HPC * 16], F32),
        ("masks", [c["NDIAG"], 128, QG], F32),
        ("ident", [128, 128], BF16),
        ("ones_row_bf", [1, 128], BF16),
        ("ones_col_bf", [128, 1], BF16),
        ("ones_row_f32", [1, 128], F32),
        ("consts", [128, 8], F32),
        ("wout", [D, D], F32),
        ("bout_g", [128, DC], F32),
        ("w1t", [MB, 128, KC, 128], F32),
        ("w2t", [MB, 128, KC, 128], F32),
        ("w3t", [DC, 128, MB, 128], F32),
    ]:
        ap_in[name] = nc.dram_tensor(name, shape, dt, kind="ExternalInput").ap()

    out_t = nc.dram_tensor("out_c", [D, TPC], F32, kind="ExternalOutput").ap()

    a2a_in = nc.dram_tensor("a2a_in", [D, TPC], BF16)
    a2a_out = nc.dram_tensor("a2a_out", [D, TPC], BF16)
    y_dram = nc.dram_tensor("y_dram", [D, TPC], F32)
    r2_dram = nc.dram_tensor("r2_dram", [D, TPC], F32)

    ctx = ExitStack()
    p._ctx = ctx
    e = ctx.enter_context

    def sb(name, shape, dt):
        return e(nc.sbuf_tensor(name, shape, dt))

    dbg = set(debug)

    def dump(name, src_ap, reads):
        if name not in dbg:
            return
        t = nc.dram_tensor("dbg_" + name, list(src_ap.shape), src_ap.dtype,
                           kind="ExternalOutput").ap()
        p.op("sp", lambda E, t=t, src_ap=src_ap: E.dma_start(out=t, in_=src_ap),
             reads=list(reads), writes=[], dma=True)

    consts_sb = sb("consts_sb", [128, 8], F32)
    ident_sb = sb("ident_sb", [128, 128], BF16)
    onesr_bf = sb("onesr_bf", [1, 128], BF16)
    onesc_bf = sb("onesc_bf", [128, 1], BF16)
    onesr_f = sb("onesr_f", [1, 128], F32)
    bout_sb = sb("bout_sb", [128, DC], F32)
    r_small = Res("small")
    for t, src in [(consts_sb, "consts"), (ident_sb, "ident"),
                   (onesr_bf, "ones_row_bf"), (onesc_bf, "ones_col_bf"),
                   (onesr_f, "ones_row_f32"), (bout_sb, "bout_g")]:
        p.op("sp", lambda E, t=t, src=src: E.dma_start(out=t[:, :], in_=ap_in[src]),
             writes=[r_small], dma=True)

    NG1 = (KC + 7) // 8 * MB
    NG3 = (MB + 7) // 8 * DC
    gacc1 = sb("gacc1", [128, NG1], F32)
    gacc2 = sb("gacc2", [128, NG1], F32)
    gacc3 = sb("gacc3", [128, NG3], F32)
    cvec = sb("cvec", [128, 4], F32)
    ncvec = sb("ncvec", [128, 4], F32)
    r_gacc = [Res("gacc1"), Res("gacc2"), Res("gacc3")]
    r_cvec = Res("cvec")

    r_A = Res("A")
    r_a2ain = Res("a2ain")

    GW = 1024
    GCH = GW // 128
    gamma_jobs = []
    gcols = [0, 0, 0]

    def add_gjobs(src3, acc):
        nt = src3.shape[1]
        for a in range(0, nt, GCH):
            b = min(a + GCH, nt)
            gamma_jobs.append((src3[:, a:b, :].rearrange("p m j -> p (m j)"),
                               (b - a) * 128, acc, gcols[acc]))
            gcols[acc] += 1
    for m in range(MB):
        add_gjobs(ap_in["w1t"][m], 0)
        add_gjobs(ap_in["w2t"][m], 1)
    for dc in range(DC):
        add_gjobs(ap_in["w3t"][dc], 2)

    def make_gamma_ops(gbuf, r_gbuf):
        state = {"i": 0}

        def step():
            i = state["i"]
            if i >= len(gamma_jobs):
                return
            state["i"] += 1
            src, width, acc, col = gamma_jobs[i]
            slot = i % 2
            gt = gbuf[slot]
            rs = r_gbuf[slot]
            accs = [gacc1, gacc2, gacc3][acc]
            p.op("sp", lambda E, gt=gt, src=src, width=width:
                 E.dma_start(out=gt[:, 0:width], in_=src),
                 reads=[], writes=[rs], dma=True)
            p.op("dve", lambda E, gt=gt, accs=accs, col=col, width=width:
                 E.tensor_reduce(accs[:, col:col + 1], gt[:, 0:width], AXX, ALU.add, True),
                 reads=[rs], writes=[r_gacc[acc]])
        return step

    with ExitStack() as s01:
        e1 = s01.enter_context
        A_st = [e1(nc.sbuf_tensor(f"A_st{i}", [64, QG], BF16)) for i in range(2)]
        q_sb = e1(nc.sbuf_tensor("q_sb", [128, HPC, S], BF16))
        k_sb = e1(nc.sbuf_tensor("k_sb", [128, PPC, S], BF16))
        v_sb = e1(nc.sbuf_tensor("v_sb", [128, NKT * HPC * 65 + 63], BF16))
        vv = v_sb[:, 0:NKT * HPC * 65].rearrange("p (t h c) -> p t h c", h=HPC, c=65)
        gbuf = [e1(nc.sbuf_tensor(f"gbuf{i}", [128, GW], F32)) for i in range(2)]
        r_gbuf = [Res("gbuf0"), Res("gbuf1")]
        gamma_step = make_gamma_ops(gbuf, r_gbuf)

        def drain_gamma():
            while True:
                before = len(p.ops["sp"]) + len(p.ops["pool"])
                gamma_step()
                if len(p.ops["sp"]) + len(p.ops["pool"]) == before:
                    break

        _gf_done = [False]
        gtot = e1(nc.sbuf_tensor("gtot", [128, 4], F32))
        grow = e1(nc.sbuf_tensor("grow", [1, 4], F32))
        onescf = e1(nc.sbuf_tensor("onescf", [128, 1], F32))

        def gamma_finalize():
            if _gf_done[0]:
                return
            _gf_done[0] = True
            ps_g = ps_s[0][0:1, 0:4]
            ps_gb = ps_s[0][0:128, 8:12]
            r_gtot, r_grow = Res("gtot"), Res("grow")
            nels = {0: c["FH"] * D, 1: c["FH"] * D, 2: D * c["FH"]}
            for i, (acc, w) in enumerate([(gacc1, gcols[0]), (gacc2, gcols[1]), (gacc3, gcols[2])]):
                p.op("dve", lambda E, i=i, acc=acc, w=w: E.tensor_reduce(
                    gtot[:, i:i + 1], acc[:, 0:w], AXX, ALU.add, False),
                    reads=[r_gacc[i]], writes=[r_gtot])
            p.op("dve", lambda E: E.memset(gtot[:, 3:4], 0.0), writes=[r_gtot])
            r_ocf = Res("ocf")
            p.op("dve", lambda E: E.memset(onescf[:, :], 1.0), writes=[r_ocf])
            p.op("pe", lambda E, ps_g=ps_g: E.matmul(ps_g, onescf[:, :], gtot[:, :],
                                                     start=True, stop=True),
                 reads=[r_gtot, r_ocf], writes=[r_ss[0]])
            p.op("dve", lambda E, ps_g=ps_g: E.tensor_copy(grow[:, :], ps_g),
                 reads=[r_ss[0]], writes=[r_grow])
            for i in range(3):
                scale = 0.5 / nels[i]
                p.op("dve", lambda E, i=i, scale=scale: E.tensor_scalar(
                    grow[:, i:i + 1], grow[:, i:i + 1], scale, 0.5 * Q_EPS,
                    ALU.mult, ALU.add),
                    reads=[r_grow], writes=[r_grow])
            p.op("pe", lambda E, ps_gb=ps_gb: E.matmul(ps_gb, onesr_f[:, :], grow[:, :],
                                                       start=True, stop=True),
                 reads=[r_grow, r_small], writes=[r_ss[0]])
            p.op("dve", lambda E, ps_gb=ps_gb: E.tensor_copy(cvec[:, :], ps_gb),
                 reads=[r_ss[0]], writes=[r_cvec])
            p.op("dve", lambda E: E.tensor_scalar(ncvec[:, :], cvec[:, :], -1.0, None, ALU.mult),
                 reads=[r_cvec], writes=[r_cvec])

        r_q, r_k, r_v = Res("q"), Res("k"), Res("v")
        r_Ast = [Res("Ast0"), Res("Ast1")]

        p.op("dve", lambda E: E.memset(v_sb[:, NKT * HPC * 65:], 0.0), writes=[r_v])
        p.op("dve", lambda E: E.memset(vv[:, :, :, 64:65], 1.0), writes=[r_v])
        for h in range(HPC):
            zlo, zhi = ((64, 128) if h % 2 == 0 else (0, 64)) if HPC > 1 else (64, 128)
            p.op("dve", lambda E, h=h, zlo=zlo, zhi=zhi:
                 E.memset(q_sb[zlo:zhi, h, :], 0.0), writes=[r_q])

        with ExitStack() as s0:
            e0 = s0.enter_context
            wqkv_sb = e0(nc.sbuf_tensor("wqkv_sb", [128, KC, QKVW], BF16))
            bqkv_sb = e0(nc.sbuf_tensor("bqkv_sb", [128, QKVW], F32))
            bksc_sb = e0(nc.sbuf_tensor("bksc_sb", [128, HW], F32))
            xg = [e0(nc.sbuf_tensor(f"xg{i}", [128, KC, 128], BF16)) for i in range(2)]
            cs = [e0(nc.sbuf_tensor(f"cs{i}", [128, 2 * HPC * 16], F32)) for i in range(2)]
            q_scr = [e0(nc.sbuf_tensor(f"q_scr{i}", [128, HPC, 64], F32)) for i in range(2)]
            k_scr = [e0(nc.sbuf_tensor(f"k_scr{i}", [128, HPC, 64], F32)) for i in range(2)]
            rt = [e0(nc.sbuf_tensor(f"rt{i}", [128, HPC, 16], F32)) for i in range(4)]
            qbf = [e0(nc.sbuf_tensor(f"qbf{i}", [128, HPC * 64], BF16)) for i in range(2)]
            kbf = [e0(nc.sbuf_tensor(f"kbf{i}", [128, HPC * 64], BF16)) for i in range(2)]

            ps_qk = [e0(nc.psum_tensor(f"ps_qk{i}", [128, 2 * HW], F32)) for i in range(2)]
            ps_v = [e0(nc.psum_tensor(f"ps_v{i}", [128, HW], F32)) for i in range(2)]
            ps_t = [e0(nc.psum_tensor(f"ps_t{i}", [128, 128], BF16)) for i in range(2)]

            r_wqkv, r_bq, r_bk = Res("wqkv"), Res("bqkv"), Res("bksc")
            r_xg = [Res("xg0"), Res("xg1")]
            r_cs = [Res("cs0"), Res("cs1")]
            r_psqk = [Res("psqk0"), Res("psqk1")]
            r_psv = [Res("psv0"), Res("psv1")]
            r_qscr = [Res("qscr0"), Res("qscr1")]
            r_kscr = [Res("kscr0"), Res("kscr1")]
            r_rt = [Res(f"rt{i}") for i in range(4)]
            r_qbf = [Res("qbf0"), Res("qbf1")]
            r_kbf = [Res("kbf0"), Res("kbf1")]
            r_pst = [Res("pst0"), Res("pst1")]

            p.op("pool", lambda E: E.dma_start(
                out=wqkv_sb[:, :, :],
                in_=ap_in["wqkv"].rearrange("(kc p) n -> p kc n", p=128)),
                writes=[r_wqkv], dma=True)
            p.op("sp", lambda E: E.dma_start(out=bqkv_sb[:, :], in_=ap_in["bqkv_b"]),
                 writes=[r_bq], dma=True)
            p.op("dve", lambda E: E.tensor_scalar(
                bksc_sb[:, :], bqkv_sb[:, HW:2 * HW], 0.125, None, ALU.mult),
                reads=[r_bq], writes=[r_bk])

            def emit_transposes(tt):
                pq = tt % 2
                for pr in range(PPC):
                    for pp, (bf, rbf) in enumerate(((qbf[pq], r_qbf[pq]), (kbf[pq], r_kbf[pq]))):
                        src_ap = bf[:, pr * 128:(pr + 1) * 128]
                        p.op("pe", lambda E, src_ap=src_ap, pp=pp:
                             E.transpose(ps_t[pp][:, :], src_ap, ident_sb[:, :]),
                             reads=[rbf, r_small], writes=[r_pst[pp]])
                        if pp == 1:
                            p.op("dve", lambda E, pp=pp, pr=pr, tt=tt:
                                 E.tensor_copy(k_sb[:, pr, tt * 128:(tt + 1) * 128],
                                               ps_t[pp][:, :]),
                                 reads=[r_pst[pp]], writes=[r_k])
                        else:
                            p.op("dve", lambda E, pp=pp, pr=pr, tt=tt:
                                 E.tensor_copy(q_sb[0:64, 2 * pr, tt * 128:(tt + 1) * 128],
                                               ps_t[pp][0:64, :]),
                                 reads=[r_pst[pp]], writes=[r_q])
                            if HPC > 1:
                                p.op("dve", lambda E, pp=pp, pr=pr, tt=tt:
                                     E.tensor_copy(q_sb[64:128, 2 * pr + 1, tt * 128:(tt + 1) * 128],
                                                   ps_t[pp][64:128, :]),
                                     reads=[r_pst[pp]], writes=[r_q])

            for tt in range(NKT):
                slot = tt % 2
                p.op("pool", lambda E, tt=tt, slot=slot: E.dma_start(
                    out=xg[slot][:, :, :],
                    in_=ap_in["x_fm"][:, tt * 128:(tt + 1) * 128]
                        .rearrange("(kc p) t -> p kc t", p=128)),
                    writes=[r_xg[slot]], dma=True)
                p.op("sp", lambda E, tt=tt, slot=slot: E.dma_start(
                    out=cs[slot][:, :],
                    in_=ap_in["cossin"][tt * 128:(tt + 1) * 128, :]),
                    writes=[r_cs[slot]], dma=True)
                pp0 = tt % 2
                for kc in range(KC):
                    st = kc == 0
                    sp_ = kc == KC - 1
                    for ps, rps, off, w_ in ((ps_qk[pp0], r_psqk[pp0], 0, 2 * HW),
                                             (ps_v[pp0], r_psv[pp0], 2 * HW, HW)):
                        p.op("pe", lambda E, slot=slot, kc=kc, ps=ps, off=off, w_=w_, st=st, sp_=sp_:
                             E.matmul(ps[:, :], xg[slot][:, kc, :],
                                      wqkv_sb[:, kc, off:off + w_], start=st, stop=sp_),
                             reads=[r_xg[slot], r_wqkv],
                             writes=[rps] if (st or sp_) else [],
                             inc=(1 if sp_ else 0), lag=not sp_)
                p.op("dve", lambda E, pp0=pp0: E.tensor_tensor(
                    q_scr[pp0][:, :, :], ps_qk[pp0][:, 0:HW].rearrange("p (h d) -> p h d", d=64),
                    bqkv_sb[:, 0:HW].rearrange("p (h d) -> p h d", d=64), ALU.add),
                    reads=[r_psqk[pp0], r_bq], writes=[r_qscr[pp0]])
                p.op("dve", lambda E, pp0=pp0: E.scalar_tensor_tensor(
                    k_scr[pp0][:, :, :], ps_qk[pp0][:, HW:2 * HW].rearrange("p (h d) -> p h d", d=64),
                    0.125, bksc_sb[:, :].rearrange("p (h d) -> p h d", d=64),
                    ALU.mult, ALU.add),
                    reads=[r_psqk[pp0], r_bk], writes=[r_kscr[pp0]])
                p.op("dve", lambda E, tt=tt, pp0=pp0: E.tensor_tensor(
                    vv[:, tt, :, 0:64], ps_v[pp0][:, :].rearrange("p (h d) -> p h d", d=64),
                    bqkv_sb[:, 2 * HW:].rearrange("p (h d) -> p h d", d=64), ALU.add),
                    reads=[r_psv[pp0], r_bq], writes=[r_v])
                for scr, rscr in ((q_scr[pp0], r_qscr[pp0]), (k_scr[pp0], r_kscr[pp0])):
                    u1 = scr[:, :, 0:16]
                    u2 = scr[:, :, 16:32]
                    co = cs[slot][:, 0:HPC * 16].rearrange("p (h d) -> p h d", d=16)
                    si = cs[slot][:, HPC * 16:].rearrange("p (h d) -> p h d", d=16)
                    p.op("dve", lambda E, u1=u1, co=co: E.tensor_tensor(rt[0][:, :, :], u1, co, ALU.mult),
                         reads=[rscr, r_cs[slot]], writes=[r_rt[0]])
                    p.op("dve", lambda E, u2=u2, si=si: E.tensor_tensor(rt[1][:, :, :], u2, si, ALU.mult),
                         reads=[rscr, r_cs[slot]], writes=[r_rt[1]])
                    p.op("dve", lambda E, u1=u1, si=si: E.tensor_tensor(rt[2][:, :, :], u1, si, ALU.mult),
                         reads=[rscr, r_cs[slot]], writes=[r_rt[2]])
                    p.op("dve", lambda E, u2=u2, co=co: E.tensor_tensor(rt[3][:, :, :], u2, co, ALU.mult),
                         reads=[rscr, r_cs[slot]], writes=[r_rt[3]])
                    p.op("dve", lambda E, u1=u1: E.tensor_tensor(u1, rt[0][:, :, :], rt[1][:, :, :], ALU.subtract),
                         reads=[r_rt[0], r_rt[1]], writes=[rscr])
                    p.op("dve", lambda E, u2=u2: E.tensor_tensor(u2, rt[2][:, :, :], rt[3][:, :, :], ALU.add),
                         reads=[r_rt[2], r_rt[3]], writes=[rscr])
                p.op("dve", lambda E, pp0=pp0: E.tensor_copy(
                    qbf[pp0][:, :], q_scr[pp0][:, :, :].rearrange("p h d -> p (h d)")),
                    reads=[r_qscr[pp0]], writes=[r_qbf[pp0]])
                p.op("dve", lambda E, pp0=pp0: E.tensor_copy(
                    kbf[pp0][:, :], k_scr[pp0][:, :, :].rearrange("p h d -> p (h d)")),
                    reads=[r_kscr[pp0]], writes=[r_kbf[pp0]])
                if tt >= 1:
                    emit_transposes(tt - 1)
                if tt == NKT - 1:
                    emit_transposes(tt)
                gamma_step()
                gamma_step()
            dump("q_sb", q_sb[:, :, :], [r_q])
            dump("k_sb", k_sb[:, :, :], [r_k])
            dump("v_sb", v_sb[:, :], [r_v])
            p.barrier()

        NS = 3
        with ExitStack() as s1:
            e2 = s1.enter_context
            masks_sb = e2(nc.sbuf_tensor("masks_sb", [128, c["NDIAG"], QG], F32))
            p_sb = [e2(nc.sbuf_tensor(f"p_sb{i}", [128, 2 * QG], BF16)) for i in range(NS)]
            rcp_f = e2(nc.sbuf_tensor("rcp_f", [1, QG], F32))
            rcp_bf = e2(nc.sbuf_tensor("rcp_bf", [1, QG], BF16))
            rb_sb = e2(nc.sbuf_tensor("rb_sb", [64, QG], F32))
            ps_s = [e2(nc.psum_tensor(f"ps_s{i}", [128, 2 * QG], F32)) for i in range(NS)]
            ps_o = [e2(nc.psum_tensor(f"ps_o{i}", [128, QG], F32)) for i in range(2)]

            r_ps = [Res(f"p{i}") for i in range(NS)]
            r_ss = [Res(f"ss{i}") for i in range(NS)]
            r_so = [Res("so0"), Res("so1")]
            r_rcp, r_rb = Res("rcp"), Res("rb")
            r_masks = Res("masks")
            p.op("sp", lambda E: E.dma_start(
                out=masks_sb[:, :, :],
                in_=ap_in["masks"].rearrange("n p q -> p n q")),
                writes=[r_masks], dma=True)

            a2a_v = a2a_in.ap().rearrange("(d hl hd) t -> hl hd d t", hl=HPC, hd=64)

            LAG = 2
            unit = 0
            for h in range(HPC):
                pr, hp = h // 2, h % 2
                for qg in range(NQG):
                    op_ = unit % 2
                    nkc = (qg + 1) * (QG // 128)
                    np_ = nkc // 2

                    def emit_spair(j, h=h, pr=pr, qg=qg):
                        sl = j % NS
                        for half in range(2):
                            kc = 2 * j + half
                            p.op("pe", lambda E, sl=sl, half=half, kc=kc, h=h, pr=pr, qg=qg:
                                 E.matmul(ps_s[sl][:, half * QG:(half + 1) * QG],
                                          k_sb[:, pr, kc * 128:(kc + 1) * 128],
                                          q_sb[:, h, qg * QG:(qg + 1) * QG],
                                          start=True, stop=True),
                                 reads=[r_k, r_q],
                                 writes=[r_ss[sl]],
                                 inc=(1 if half == 1 else 0), lag=(half == 0))
                        for half in range(2):
                            kc = 2 * j + half
                            diag = kc - qg * (QG // 128)
                            if diag >= 0:
                                p.op("dve", lambda E, sl=sl, half=half, diag=diag:
                                     E.tensor_tensor(
                                         ps_s[sl][:, half * QG:(half + 1) * QG],
                                         ps_s[sl][:, half * QG:(half + 1) * QG],
                                         masks_sb[:, diag, :], ALU.add),
                                     reads=[r_masks, r_ss[sl]], writes=[r_ss[sl]])
                        p.op("act", lambda E, sl=sl: E.activation(
                            p_sb[sl][:, :], ps_s[sl][:, :], AF.Exp,
                            bias=consts_sb[:, 0:1], scale=1.0),
                            reads=[r_ss[sl], r_small], writes=[r_ps[sl]])

                    def emit_avpair(j, h=h, op_=op_, np_=np_):
                        sl = j % NS
                        for half in range(2):
                            kc = 2 * j + half
                            st = kc == 0
                            sp_ = kc == 2 * np_ - 1
                            vb = (kc * HPC + h) * 65
                            p.op("pe", lambda E, sl=sl, half=half, vb=vb, op_=op_, st=st, sp_=sp_:
                                 E.matmul(ps_o[op_][:, :], v_sb[:, vb:vb + 128],
                                          p_sb[sl][:, half * QG:(half + 1) * QG],
                                          start=st, stop=sp_),
                                 reads=[r_v, r_ps[sl]],
                                 writes=[r_so[op_]] if (st or sp_) else [],
                                 inc=(1 if sp_ else 0), lag=not sp_)

                    for j in range(np_ + LAG):
                        if j < np_:
                            emit_spair(j)
                        if j >= LAG:
                            emit_avpair(j - LAG)

                    bsl = (np_ - 1) % NS
                    ast = unit % 2
                    p.op("dve", lambda E, op_=op_: E.reciprocal(rcp_f[:, :], ps_o[op_][64:65, :]),
                         reads=[r_so[op_]], writes=[r_rcp])
                    p.op("dve", lambda E: E.tensor_copy(rcp_bf[:, :], rcp_f[:, :]),
                         reads=[r_rcp], writes=[r_rcp])
                    p.op("pe", lambda E, bsl=bsl: E.matmul(
                        ps_s[bsl][0:64, 0:QG], onesr_bf[:, 0:64], rcp_bf[:, :],
                        start=True, stop=True),
                        reads=[r_rcp, r_small], writes=[r_ss[bsl]])
                    p.op("dve", lambda E, bsl=bsl: E.tensor_copy(rb_sb[:, :], ps_s[bsl][0:64, 0:QG]),
                         reads=[r_ss[bsl]], writes=[r_rb])
                    p.op("dve", lambda E, op_=op_, ast=ast: E.tensor_tensor(
                        A_st[ast][:, :], ps_o[op_][0:64, :], rb_sb[:, :], ALU.mult),
                        reads=[r_so[op_], r_rb], writes=[r_Ast[ast]])
                    hl = 2 * pr + hp if HPC > 1 else 0
                    tok0 = qg * QG
                    if QG <= TPC:
                        dst = a2a_v[hl][:, tok0 // TPC, tok0 % TPC:tok0 % TPC + QG]
                    else:
                        dst = a2a_v[hl][:, tok0 // TPC:(tok0 + QG) // TPC, :]
                    p.op("sp", lambda E, dst=dst, ast=ast: E.dma_start(
                        out=dst, in_=A_st[ast][:, :]),
                        reads=[r_Ast[ast]], writes=[r_a2ain], dma=True)
                    unit += 1
                    for _ in range(4):
                        gamma_step()
                    if unit == min(48, HPC * NQG - 2):
                        drain_gamma()
                        gamma_finalize()
            drain_gamma()
            gamma_finalize()
            p.barrier()

        dump("cvec", cvec[:, :], [r_cvec])
        dump("gacc1", gacc1[:, :], [r_gacc[0]])
        dump("gacc3", gacc3[:, :], [r_gacc[2]])
        p.barrier()

    r_a2a = Res("a2a")

    def emit_cc(E):
        with allow_ici_all_to_all():
            return E.collective_compute(
                "AllToAll", ALU.bypass, replica_groups=[list(range(NCORES))],
                ins=[a2a_in.ap()], outs=[a2a_out.ap()])
    p.op("pool", emit_cc, reads=[r_a2ain], writes=[r_a2a], inc=1, sem="cc")

    with ExitStack() as s23:
        e23 = s23.enter_context
        T = TPC
        stat = e23(nc.sbuf_tensor("stat", [1, 4 * TPC], F32))
        stat_bf = e23(nc.sbuf_tensor("stat_bf", [1, 2 * TPC], BF16))
        RB = e23(nc.sbuf_tensor("RB", [128, TPC], F32))
        MBc = e23(nc.sbuf_tensor("MBc", [128, TPC], F32))
        ysc = e23(nc.sbuf_tensor("ysc", [128, TPC], F32))
        r_stat, r_statbf = Res("stat"), Res("statbf")
        r_RB, r_MB, r_ysc = Res("RB"), Res("MB"), Res("ysc")

        def ln_finalize(ps1, ps2, rp1, rp2, nfeat, ps_bc, r_psbc):
            for th in range(NTH):
                p.op("dve", lambda E, th=th, ps1=ps1: E.tensor_scalar(
                    stat[:, th * TH:(th + 1) * TH], ps1[th][:, :],
                    1.0 / nfeat, None, ALU.mult),
                    reads=[rp1[th]], writes=[r_stat])
                p.op("dve", lambda E, th=th, ps2=ps2: E.tensor_scalar(
                    stat[:, T + th * TH:T + (th + 1) * TH], ps2[th][:, :],
                    1.0 / nfeat, None, ALU.mult),
                    reads=[rp2[th]], writes=[r_stat])
            p.op("dve", lambda E: E.tensor_tensor(
                stat[:, 3 * T:4 * T], stat[:, 0:T], stat[:, 0:T], ALU.mult),
                reads=[r_stat], writes=[r_stat])
            p.op("dve", lambda E: E.tensor_tensor(
                stat[:, T:2 * T], stat[:, T:2 * T], stat[:, 3 * T:4 * T], ALU.subtract),
                reads=[r_stat], writes=[r_stat])
            p.op("act", lambda E: E.activation(
                stat[:, 2 * T:3 * T], stat[:, T:2 * T], AF.Ln,
                bias=consts_sb[0:1, 1:2], scale=1.0),
                reads=[r_stat, r_small], writes=[r_stat])
            p.op("act", lambda E: E.activation(
                stat[:, 2 * T:3 * T], stat[:, 2 * T:3 * T], AF.Exp,
                bias=0.0, scale=-0.5),
                reads=[r_stat], writes=[r_stat])
            p.op("dve", lambda E: E.tensor_tensor(
                stat[:, 3 * T:4 * T], stat[:, 0:T], stat[:, 2 * T:3 * T], ALU.mult),
                reads=[r_stat], writes=[r_stat])
            p.op("dve", lambda E: E.tensor_copy(stat_bf[:, 0:T], stat[:, 2 * T:3 * T]),
                 reads=[r_stat], writes=[r_statbf])
            p.op("dve", lambda E: E.tensor_copy(stat_bf[:, T:2 * T], stat[:, 3 * T:4 * T]),
                 reads=[r_stat], writes=[r_statbf])
            for dst, rdst, off in ((RB, r_RB, 0), (MBc, r_MB, TPC)):
                for th in range(NTH):
                    p.op("pe", lambda E, off=off, th=th, ps_bc=ps_bc: E.matmul(
                        ps_bc[:, :], onesr_bf[:, :],
                        stat_bf[:, off + th * TH:off + (th + 1) * TH],
                        start=True, stop=True),
                        reads=[r_statbf, r_small], writes=[r_psbc])
                    p.op("dve", lambda E, dst=dst, th=th, ps_bc=ps_bc: E.tensor_copy(
                        dst[:, th * TH:(th + 1) * TH], ps_bc[:, :]),
                        reads=[r_psbc], writes=[rdst])

        sR = ExitStack()
        ybf_sb = sR.enter_context(nc.sbuf_tensor("ybf_sb", [128, KC, TPC], BF16, side="right"))
        r_ybf = Res("ybf")
        sR2 = ExitStack()
        xo_sb = [sR2.enter_context(nc.sbuf_tensor(f"xo{i}", [128, TPC], F32, side="right")) for i in range(2)]
        wo_sb = [sR2.enter_context(nc.sbuf_tensor(f"wo{i}", [128, KC, 128], BF16, side="right")) for i in range(2)]
        r_xo = [Res("xo0"), Res("xo1")]
        r_wo = [Res("wo0"), Res("wo1")]

        def wout_dma(dc):
            slw = dc % 2
            p.op("pool", lambda E, dc=dc, slw=slw: E.dma_start(
                out=wo_sb[slw][:, :, :],
                in_=ap_in["wout"][:, dc * 128:(dc + 1) * 128]
                    .rearrange("(kc p) j -> p kc j", p=128)),
                writes=[r_wo[slw]], dma=True)
            p.op("sp", lambda E, dc=dc, slw=slw: E.dma_start(
                out=xo_sb[slw][:, :], in_=ap_in["x_own"][dc * 128:(dc + 1) * 128, :]),
                writes=[r_xo[slw]], dma=True)

        wout_dma(0)

        with ExitStack() as s2:
            e3 = s2.enter_context
            af_sb = e3(nc.sbuf_tensor("af_sb", [128, KC, TPC], BF16))
            r1_sb = e3(nc.sbuf_tensor("r1_sb", [128, KC, TPC], F32))
            r1b = e3(nc.sbuf_tensor("r1b", [128, TPC], BF16))
            r1sq = e3(nc.sbuf_tensor("r1sq", [128, TPC], BF16))

            ps_w = [e3(nc.psum_tensor(f"ps_w{i}", [128, TH], F32)) for i in range(2)]
            ps_s1 = [e3(nc.psum_tensor(f"ps_s1{i}", [1, TH], F32)) for i in range(NTH)]
            ps_s2 = [e3(nc.psum_tensor(f"ps_s2{i}", [1, TH], F32)) for i in range(NTH)]
            ps_bc = e3(nc.psum_tensor("ps_bc", [128, TH], F32))

            r_af = Res("af")
            r_r1 = Res("r1")
            r_psw = [Res("psw0"), Res("psw1")]
            r_ps1 = [Res(f"ps1{i}") for i in range(NTH)]
            r_ps2 = [Res(f"ps2{i}") for i in range(NTH)]
            r_r1b, r_r1sq = Res("r1b"), Res("r1sq")
            r_psbc = Res("psbc")

            p.op("pool", lambda E: E.dma_start(
                out=af_sb[:, :, :],
                in_=a2a_out.ap().rearrange("(kc p) t -> p kc t", p=128)),
                reads=[r_a2a], writes=[r_af], dma=True)

            for dc in range(DC):
                slw = dc % 2
                if dc + 1 < DC:
                    wout_dma(dc + 1)
                for th in range(NTH):
                    pw = (dc * NTH + th) % 2
                    for kc in range(KC):
                        stw = kc == 0
                        spw = kc == KC - 1
                        p.op("pe", lambda E, slw=slw, kc=kc, th=th, pw=pw,
                             st=stw, sp_=spw:
                             E.matmul(ps_w[pw][:, :], wo_sb[slw][:, kc, :],
                                      af_sb[:, kc, th * TH:(th + 1) * TH], start=st, stop=sp_),
                             reads=[r_wo[slw], r_af],
                             writes=[r_psw[pw]] if (stw or spw) else [],
                             inc=(1 if spw else 0), lag=not spw)
                    p.op("dve", lambda E, dc=dc, th=th, pw=pw, slw=slw:
                         E.scalar_tensor_tensor(
                             r1_sb[:, dc, th * TH:(th + 1) * TH], ps_w[pw][:, :],
                             bout_sb[:, dc:dc + 1], xo_sb[slw][:, th * TH:(th + 1) * TH],
                             ALU.add, ALU.add),
                         reads=[r_psw[pw], r_xo[slw], r_small], writes=[r_r1])
                p.op("dve", lambda E, dc=dc: E.tensor_copy(r1b[:, :], r1_sb[:, dc, :]),
                     reads=[r_r1], writes=[r_r1b])
                p.op("act", lambda E: E.activation(r1sq[:, :], r1b[:, :], AF.Square),
                     reads=[r_r1b], writes=[r_r1sq])
                for th in range(NTH):
                    sp2 = dc == DC - 1
                    st2 = dc == 0
                    p.op("pe", lambda E, th=th, dc=dc, st=st2, sp_=sp2:
                         E.matmul(ps_s1[th][:, :], onesc_bf[:, :],
                                  r1b[:, th * TH:(th + 1) * TH], start=st, stop=sp_),
                         reads=[r_r1b, r_small], writes=[r_ps1[th]])
                    p.op("pe", lambda E, th=th, dc=dc, st=st2, sp_=sp2:
                         E.matmul(ps_s2[th][:, :], onesc_bf[:, :],
                                  r1sq[:, th * TH:(th + 1) * TH], start=st, stop=sp_),
                         reads=[r_r1sq, r_small], writes=[r_ps2[th]])

            ln_finalize(ps_s1, ps_s2, r_ps1, r_ps2, D, ps_bc, r_psbc)

            dump("r1", r1_sb[:, :, :], [r_r1])
            dump("af", af_sb[:, :, :], [r_af])
            for dc in range(DC):
                p.op("dve", lambda E, dc=dc: E.tensor_tensor(
                    ysc[:, :], r1_sb[:, dc, :], RB[:, :], ALU.mult),
                    reads=[r_r1, r_RB], writes=[r_ysc])
                p.op("dve", lambda E, dc=dc: E.tensor_tensor(
                    ysc[:, :], ysc[:, :], MBc[:, :], ALU.subtract),
                    reads=[r_MB, r_ysc], writes=[r_ysc])
                p.op("dve", lambda E, dc=dc: E.tensor_copy(ybf_sb[:, dc, :], ysc[:, :]),
                     reads=[r_ysc], writes=[r_ybf])
                p.op("act", lambda E, dc=dc: E.dma_start(
                    out=y_dram.ap()[dc * 128:(dc + 1) * 128, :], in_=ysc[:, :]),
                    reads=[r_ysc], writes=[], dma=True)
            dump("ybf", ybf_sb[:, :, :], [r_ybf])
            p.barrier()

        sR2.close()

        with ExitStack() as s3:
            e4 = s3.enter_context
            H_sb = e4(nc.sbuf_tensor("H_sb", [128, MB, TPC], BF16))
            r_H = Res("H")
            with ExitStack() as s3a:
                e5 = s3a.enter_context
                wb = [e5(nc.sbuf_tensor(f"wb{i}", [128, KC * 128], F32)) for i in range(4)]
                qw = [e5(nc.sbuf_tensor(f"qw{i}", [128, KC, 128], BF16)) for i in range(4)]
                sig = e5(nc.sbuf_tensor("sig", [128, TH], BF16))
                ps_x1 = [e5(nc.psum_tensor(f"ps_x1{i}", [128, TH], F32)) for i in range(2)]
                ps_x2 = [e5(nc.psum_tensor(f"ps_x2{i}", [128, TH], F32)) for i in range(2)]
                r_wb = [Res(f"wb{i}") for i in range(4)]
                r_qw = [Res(f"qw{i}") for i in range(4)]
                r_sig = Res("sig")
                r_px1 = [Res("px10"), Res("px11")]
                r_px2 = [Res("px20"), Res("px21")]

                ts2 = e5(nc.sbuf_tensor("ts2", [128, KC * 128], BF16))
                r_ts2 = Res("ts2")

                def quantize(src_sb, r_src, width, dst, r_dst, ci):
                    p.op("act", lambda E: E.activation(
                        dst, src_sb[:, 0:width], AF.Sign,
                        bias=ncvec[:, ci:ci + 1], scale=1.0),
                        reads=[r_src, r_cvec], writes=[r_dst])
                    p.op("act", lambda E: E.activation(
                        ts2[:, 0:width], src_sb[:, 0:width], AF.Sign,
                        bias=cvec[:, ci:ci + 1], scale=1.0),
                        reads=[r_src, r_cvec], writes=[r_ts2])
                    p.op("dve", lambda E: E.tensor_tensor(
                        dst, dst, ts2[:, 0:width], ALU.add),
                        reads=[r_ts2, r_dst], writes=[r_dst])

                def w12_dma(m):
                    slw = m % 2
                    for wi, (wt, q_) in enumerate((("w1t", "sp"), ("w2t", "pool"))):
                        qi = slw * 2 + wi
                        p.op(q_, lambda E, wt=wt, m=m, qi=qi: E.dma_start(
                            out=wb[qi][:, :], in_=ap_in[wt][m].rearrange("p kc j -> p (kc j)")),
                            writes=[r_wb[qi]], dma=True)

                w12_dma(0)
                for m in range(MB):
                    slw = m % 2
                    if m + 1 < MB:
                        w12_dma(m + 1)
                    for wi, ci in ((0, 0), (1, 1)):
                        qi = slw * 2 + wi
                        quantize(wb[qi], r_wb[qi], KC * 128,
                                 qw[qi][:, :, :].rearrange("p kc j -> p (kc j)"), r_qw[qi], ci)
                        psx = (ps_x1, ps_x2)[wi]
                        rpx = (r_px1, r_px2)[wi]
                        for th in range(NTH):
                            px = th % 2
                            for kc in range(KC):
                                stf = kc == 0
                                spf = kc == KC - 1
                                p.op("pe", lambda E, qi=qi, kc=kc, th=th, px=px, psx=psx,
                                     st=stf, sp_=spf:
                                     E.matmul(psx[px][:, :], qw[qi][:, kc, :],
                                              ybf_sb[:, kc, th * TH:(th + 1) * TH],
                                              start=st, stop=sp_),
                                     reads=[r_qw[qi], r_ybf],
                                     writes=[rpx[px]] if (stf or spf) else [],
                                     inc=(1 if spf else 0), lag=not spf)
                    for th in range(NTH):
                        px = th % 2
                        p.op("act", lambda E, px=px: E.activation(
                            sig[:, :], ps_x1[px][:, :], AF.Sigmoid, scale=0.5),
                            reads=[r_px1[px]], writes=[r_sig])
                        p.op("dve", lambda E, m=m, th=th, px=px: E.tensor_tensor(
                            H_sb[:, m, th * TH:(th + 1) * TH], ps_x2[px][:, :],
                            sig[:, :], ALU.mult),
                            reads=[r_px2[px], r_sig], writes=[r_H])
                dump("H", H_sb[:, :, :], [r_H])
                p.barrier()
            sR.close()

            with ExitStack() as s3b:
                e6 = s3b.enter_context
                WH3 = (MB + 1) // 2
                w3b = [e6(nc.sbuf_tensor(f"w3b{i}", [128, WH3 * 128], F32)) for i in range(4)]
                q3 = [e6(nc.sbuf_tensor(f"q3{i}", [128, MB, 128], BF16)) for i in range(2)]
                yst = [e6(nc.sbuf_tensor("yst0", [128, TPC], F32))]
                ts23 = e6(nc.sbuf_tensor("ts23", [128, WH3 * 128], BF16))
                r_ts23 = Res("ts23")
                r2c = [e6(nc.sbuf_tensor(f"r2c{i}", [128, TPC], F32)) for i in range(2)]
                r2b = e6(nc.sbuf_tensor("r2b", [128, TPC], BF16))
                r2sq = r2b
                ps_r = [e6(nc.psum_tensor(f"ps_r{i}", [128, TH], F32)) for i in range(2)]
                ps_t1 = [e6(nc.psum_tensor(f"ps_t1{i}", [1, TH], F32)) for i in range(NTH)]
                ps_t2 = [e6(nc.psum_tensor(f"ps_t2{i}", [1, TH], F32)) for i in range(NTH)]
                r_w3b = [Res(f"w3b{i}") for i in range(4)]
                r_q3 = [Res("q30"), Res("q31")]
                r_yst = [Res("yst0")]
                r_r2c = [Res("r2c0"), Res("r2c1")]
                r_r2b, r_r2sq = Res("r2b"), Res("r2sq")
                r_pr = [Res("pr0"), Res("pr1")]
                r_pt1 = [Res(f"pt1{i}") for i in range(NTH)]
                r_pt2 = [Res(f"pt2{i}") for i in range(NTH)]
                r_r2dram = Res("r2dram")

                mh3 = MB // 2

                def w3_dma(dc):
                    slq = dc % 2
                    for hi, (m0, m1, q_) in enumerate(((0, mh3, "sp"), (mh3, MB, "pool"))):
                        slw = slq * 2 + hi
                        p.op(q_, lambda E, dc=dc, slw=slw, m0=m0, m1=m1: E.dma_start(
                            out=w3b[slw][:, 0:(m1 - m0) * 128],
                            in_=ap_in["w3t"][dc][:, m0:m1, :].rearrange("p m j -> p (m j)")),
                            writes=[r_w3b[slw]], dma=True)

                w3_dma(0)
                for dc in range(DC):
                    slq = dc % 2
                    if dc + 1 < DC:
                        w3_dma(dc + 1)
                    for hi, (m0, m1) in enumerate(((0, mh3), (mh3, MB))):
                        slw = slq * 2 + hi
                        wdt = (m1 - m0) * 128
                        qdst = q3[slq][:, m0:m1, :].rearrange("p m j -> p (m j)")
                        p.op("act", lambda E, slw=slw, wdt=wdt, qdst=qdst: E.activation(
                            qdst, w3b[slw][:, 0:wdt], AF.Sign,
                            bias=ncvec[:, 2:3], scale=1.0),
                            reads=[r_w3b[slw], r_cvec], writes=[r_q3[slq]])
                        p.op("act", lambda E, slw=slw, wdt=wdt: E.activation(
                            ts23[:, 0:wdt], w3b[slw][:, 0:wdt], AF.Sign,
                            bias=cvec[:, 2:3], scale=1.0),
                            reads=[r_w3b[slw], r_cvec], writes=[r_ts23])
                        p.op("dve", lambda E, wdt=wdt, qdst=qdst: E.tensor_tensor(
                            qdst, qdst, ts23[:, 0:wdt], ALU.add),
                            reads=[r_ts23, r_q3[slq]], writes=[r_q3[slq]])
                    p.op("act", lambda E, dc=dc: E.dma_start(
                        out=yst[0][:, :], in_=y_dram.ap()[dc * 128:(dc + 1) * 128, :]),
                        reads=[], writes=[r_yst[0]], dma=True)
                    for th in range(NTH):
                        pr_ = (dc * NTH + th) % 2
                        for m in range(MB):
                            st3 = m == 0
                            sp3 = m == MB - 1
                            p.op("pe", lambda E, slq=slq, m=m, th=th, pr_=pr_,
                                 st=st3, sp_=sp3:
                                 E.matmul(ps_r[pr_][:, :], q3[slq][:, m, :],
                                          H_sb[:, m, th * TH:(th + 1) * TH],
                                          start=st, stop=sp_),
                                 reads=[r_q3[slq], r_H],
                                 writes=[r_pr[pr_]] if (st3 or sp3) else [],
                                 inc=(1 if sp3 else 0), lag=not sp3)
                        p.op("dve", lambda E, slq=slq, th=th, pr_=pr_: E.scalar_tensor_tensor(
                            r2c[slq][:, th * TH:(th + 1) * TH], ps_r[pr_][:, :],
                            0.25, yst[0][:, th * TH:(th + 1) * TH], ALU.mult, ALU.add),
                            reads=[r_pr[pr_], r_yst[0]], writes=[r_r2c[slq]])
                    p.op("dve", lambda E, slq=slq: E.tensor_copy(r2b[:, :], r2c[slq][:, :]),
                         reads=[r_r2c[slq]], writes=[r_r2b])
                    for th in range(NTH):
                        sp2 = dc == DC - 1
                        st2 = dc == 0
                        p.op("pe", lambda E, th=th, dc=dc, st=st2, sp_=sp2:
                             E.matmul(ps_t1[th][:, :], onesc_bf[:, :],
                                      r2b[:, th * TH:(th + 1) * TH], start=st, stop=sp_),
                             reads=[r_r2b, r_small], writes=[r_pt1[th]])
                    p.op("act", lambda E: E.activation(r2sq[:, :], r2b[:, :], AF.Square),
                         reads=[r_r2b], writes=[r_r2b])
                    for th in range(NTH):
                        sp2 = dc == DC - 1
                        st2 = dc == 0
                        p.op("pe", lambda E, th=th, dc=dc, st=st2, sp_=sp2:
                             E.matmul(ps_t2[th][:, :], onesc_bf[:, :],
                                      r2sq[:, th * TH:(th + 1) * TH], start=st, stop=sp_),
                             reads=[r_r2b, r_small], writes=[r_pt2[th]])
                    p.op("act", lambda E, dc=dc, slq=slq: E.dma_start(
                        out=r2_dram.ap()[dc * 128:(dc + 1) * 128, :], in_=r2c[slq][:, :]),
                        reads=[r_r2c[slq]], writes=[r_r2dram], dma=True)

                ps_bc2 = e6(nc.psum_tensor("ps_bc2", [128, TH], F32))
                r_psbc2 = Res("psbc2")
                ln_finalize(ps_t1, ps_t2, r_pt1, r_pt2, D, ps_bc2, r_psbc2)

                for dc in range(DC):
                    slq = dc % 2
                    p.op("act", lambda E, dc=dc, slq=slq: E.dma_start(
                        out=r2c[slq][:, :], in_=r2_dram.ap()[dc * 128:(dc + 1) * 128, :]),
                        reads=[r_r2dram], writes=[r_r2c[slq]], dma=True)
                    p.op("dve", lambda E, slq=slq: E.tensor_tensor(
                        r2c[slq][:, :], r2c[slq][:, :], RB[:, :], ALU.mult),
                        reads=[r_RB, r_r2c[slq]], writes=[r_r2c[slq]])
                    p.op("dve", lambda E, slq=slq: E.tensor_tensor(
                        r2c[slq][:, :], r2c[slq][:, :], MBc[:, :], ALU.subtract),
                        reads=[r_MB, r_r2c[slq]], writes=[r_r2c[slq]])
                    p.op("sp", lambda E, dc=dc, slq=slq: E.dma_start(
                        out=out_t[dc * 128:(dc + 1) * 128, :], in_=r2c[slq][:, :]),
                        reads=[r_r2c[slq]], writes=[], dma=True)

    p.final_wait("sp")

    with nc.Block() as block:
        p.emit_all(nc, block)
    ctx.close()
    n_ops = sum(len(v) for v in p.ops.values())
    return nc, n_ops



def prepare_inputs(c, L0_attn_x, L0_attn_Wqkv, L0_attn_b_qkv, L0_attn_Wout,
                   L0_attn_b_out, L0_ffn_w1, L0_ffn_w2, L0_ffn_w3):
    S, D = c["S"], c["D"]
    HPC, TPC, KC, MB, DC, QG = c["HPC"], c["TPC"], c["KC"], c["MB"], c["DC"], c["QG"]
    f32 = np.float32

    x2d = np.asarray(L0_attn_x, f32).reshape(S, D)
    x_fm = np.ascontiguousarray(x2d.T)
    Wqkv = np.asarray(L0_attn_Wqkv, f32)
    bqkv = np.asarray(L0_attn_b_qkv, f32)
    Wout = np.ascontiguousarray(np.asarray(L0_attn_Wout, f32))
    bout = np.asarray(L0_attn_b_out, f32)
    w1 = np.asarray(L0_ffn_w1, f32)
    w2 = np.asarray(L0_ffn_w2, f32)
    w3 = np.asarray(L0_ffn_w3, f32)

    inv = (1.0 / (10000.0 ** (np.arange(0, 32, 2) / 32.0))).astype(np.float64)
    t = np.arange(S, dtype=np.float64)
    fr = np.outer(t, inv)
    cos = np.cos(fr).astype(f32)
    sin = np.sin(fr).astype(f32)
    cossin = np.ascontiguousarray(
        np.concatenate([np.tile(cos, (1, HPC)), np.tile(sin, (1, HPC))], 1))

    nd = c["NDIAG"]
    masks = np.zeros((nd, 128, QG), f32)
    for i in range(nd):
        s_idx = np.arange(128)[:, None] + i * 128
        j_idx = np.arange(QG)[None, :]
        masks[i] = np.where(s_idx > j_idx, -10000.0, 0.0)

    ident = np.eye(128, dtype=ml_dtypes.bfloat16)
    ones_row_bf = np.ones((1, 128), ml_dtypes.bfloat16)
    ones_col_bf = np.ones((128, 1), ml_dtypes.bfloat16)
    ones_row_f32 = np.ones((1, 128), f32)
    consts = np.zeros((128, 8), f32)
    consts[:, 0] = -C_EXP
    consts[:, 1] = LN_EPS

    w1t = np.ascontiguousarray(w1.T.reshape(KC, 128, MB, 128).transpose(2, 1, 0, 3))
    w2t = np.ascontiguousarray(w2.T.reshape(KC, 128, MB, 128).transpose(2, 1, 0, 3))
    w3t = np.ascontiguousarray(w3.T.reshape(MB, 128, DC, 128).transpose(2, 1, 0, 3))
    bout_g = np.ascontiguousarray(bout.reshape(DC, 128).T)

    shared = dict(x_fm=x_fm, cossin=cossin, masks=masks, ident=ident,
                  ones_row_bf=ones_row_bf, ones_col_bf=ones_col_bf,
                  ones_row_f32=ones_row_f32, consts=consts, wout=Wout,
                  bout_g=bout_g, w1t=w1t, w2t=w2t, w3t=w3t)

    in_maps = []
    for cid in range(NCORES):
        HW = HPC * 64
        cols = slice(cid * HW, (cid + 1) * HW)
        wq = np.ascontiguousarray(np.concatenate(
            [Wqkv[:, cols], Wqkv[:, D:][:, cols], Wqkv[:, 2 * D:][:, cols]], 1))
        bq = np.concatenate([bqkv[cols], bqkv[D:][cols], bqkv[2 * D:][cols]])
        bq_b = np.ascontiguousarray(
            np.broadcast_to(bq[None, :], (128, 3 * HW)).astype(f32))
        x_own = np.ascontiguousarray(x_fm[:, cid * TPC:(cid + 1) * TPC])
        m = dict(shared)
        m.update(wqkv=wq, bqkv_b=bq_b, x_own=x_own)
        in_maps.append(m)
    return in_maps


def kernel(**inputs):
    c = derive(full_cfg())
    in_maps = prepare_inputs(c, **inputs)
    nc, n_ops = build_nc(c)
    res = run_bass_kernel_spmd(nc, in_maps, core_ids=list(range(NCORES)))
    out_fm = np.concatenate([res.results[i]["out_c"] for i in range(NCORES)], axis=1)
    return np.ascontiguousarray(out_fm.T).reshape(1, c["S"], c["D"]).astype(np.float32)
